# version 06
# baseline (speedup 1.0000x reference)
import numpy as np
import ml_dtypes
from contextlib import ExitStack

import concourse.bass as bass
import concourse.mybir as mybir
from concourse.bass_utils import run_bass_kernel_spmd

F32 = mybir.dt.float32
BF16 = mybir.dt.bfloat16
I8 = mybir.dt.int8
AF = mybir.ActivationFunctionType
ALU = mybir.AluOpType
AX = mybir.AxisListType

NCORES = 8
SEQ = 2
S = 4096
D = 1024
INNER = 4096
TC = S // 128
GC = SEQ * TC
DC = D // 128
IC = INNER // 128
MASKV = -2500.0
M32 = 12582912.0
M64 = 6755399441055744.0

ENGINES = ("sync", "gpsimd", "scalar", "vector", "tensor")
ESEM = {"sync": "LX", "gpsimd": "LG", "scalar": "SC", "vector": "V", "tensor": "T"}
def _sem_step(name):
    return 16 if name.startswith("L") else 1


class Sched:

    def __init__(self, counts):
        self.ops = {e: [] for e in ENGINES}
        self.n = counts
        self._last_selfwait = {}

    def op(self, eng, builder, inc=None):
        step = _sem_step(inc) if inc else 0
        if inc:
            self.n[inc] = self.n.get(inc, 0) + step
        self.ops[eng].append([builder, inc, step])
        return self.n.get(inc, 0)

    def wait(self, eng, sem, val):
        if val and val > 0:
            self.ops[eng].append([("W", sem, val), None, 0])

    def wait_all(self, eng, snapshot):
        for s_, v in snapshot.items():
            self.wait(eng, s_, v)

    def count(self, sem):
        return self.n.get(sem, 0)

    def close(self):
        for e in ENGINES:
            for item in reversed(self.ops[e]):
                if not isinstance(item[0], tuple):
                    if item[1] is None:
                        sem = ESEM[e]
                        item[1] = sem
                        item[2] = _sem_step(sem)
                        self.n[sem] = self.n.get(sem, 0) + _sem_step(sem)
                    break


def run_block(nc, sched, sems):
    sched.close()
    with nc.Block() as blk:
        for ename in ENGINES:
            def body(e, _ename=ename):
                for b, inc, step in sched.ops[_ename]:
                    if isinstance(b, tuple):
                        e.wait_ge(sems[b[1]], b[2])
                    else:
                        ins = b(e)
                        if inc is not None:
                            ins.then_inc(sems[inc], step)
            getattr(blk, ename)(body)


def snap(counts):
    return dict(counts)



def phase_quant(nc, sc, t, c):
    mats = [(t["wgT"], t["wgq_s"], 0, 128), (t["wuT"], t["wuq_s"], 1, 128),
            (t["wdT"], t["wdq_s"], 2, 512)]
    stg = c["q_stg"]
    qbf = c["q_bf"]
    macc = c["q_macc"]
    tmp = c["q_tmp"]
    s11 = c["q_s11"]
    onec = c["ones_col"]
    oner = c["ones_rowf"]
    p11 = c["q_p11"]
    pbc = c["q_pbc"]
    pbc2 = c["q_pbc2"]
    wsc, wmean = t["wsc"], t["wmean"]

    vload = {}
    gstore = {}
    for dram, scratch, m, nr in mats:
        isp = nr // 128
        for k in range(8):
            par = k % 2
            if par in vload:
                sc.wait("sync", "V", vload[par])
            lv = sc.op("sync", (lambda dram=dram, k=k, par=par, nr=nr, isp=isp:
                                lambda e: e.dma_start(
                                    out=stg[par][:].rearrange("p (i o) -> p i o", i=isp),
                                    in_=dram[k * nr:(k + 1) * nr, :]
                                    .rearrange("(i p) o -> p i o", p=128)))(),
                       inc="Lstg%d" % par)
            sc.wait("vector", "Lstg%d" % par, lv)
            if k == 0:
                vload[par] = sc.op("vector",
                                   (lambda par=par, m=m:
                                    lambda e: e.tensor_reduce(out=macc[:, m:m + 1],
                                                              in_=stg[par][:], axis=AX.X,
                                                              op=ALU.add,
                                                              apply_absolute_value=True))(),
                                   inc="V")
            else:
                rv = sc.op("vector", (lambda par=par, m=m:
                                      lambda e: e.tensor_reduce(out=tmp[:, m:m + 1],
                                                                in_=stg[par][:],
                                                                axis=AX.X, op=ALU.add,
                                                                apply_absolute_value=True))(),
                           inc="V")
                sc.wait("vector", "V", rv)
                vload[par] = sc.op("vector", (lambda m=m:
                                              lambda e: e.tensor_add(out=macc[:, m:m + 1],
                                                                     in0=macc[:, m:m + 1],
                                                                     in1=tmp[:, m:m + 1]))(),
                                   inc="V")
        sc.wait("tensor", "V", sc.count("V"))
        tt = sc.op("tensor", (lambda m=m: lambda e: e.matmul(p11[:], onec[:],
                                                             macc[:, m:m + 1],
                                                             start=True, stop=True))(), inc="T")
        sc.wait("vector", "T", tt)
        nel = float(8 * 128 * 4096)
        fv = sc.op("vector", (lambda m=m:
                              lambda e: e.tensor_scalar(out=s11[:, m:m + 1], in0=p11[:],
                                                        scalar1=1.0 / nel, scalar2=1e-5,
                                                        op0=ALU.mult, op1=ALU.max))(),
                   inc="V")
        sc.wait("vector", "V", fv)
        v = sc.op("vector", (lambda m=m:
                             lambda e: e.reciprocal(out=s11[:, 3:4], in_=s11[:, m:m + 1]))(),
                  inc="V")
        sc.wait("tensor", "V", v)
        sc.op("tensor", (lambda m=m: lambda e: e.matmul(pbc[:], oner[:], s11[:, m:m + 1],
                                                        start=True, stop=True))())
        tt = sc.op("tensor", lambda e: e.matmul(pbc2[:], oner[:], s11[:, 3:4],
                                                start=True, stop=True), inc="T")
        sc.wait("vector", "T", tt)
        sc.op("vector", (lambda m=m: lambda e: e.tensor_copy(out=wmean[:, m:m + 1],
                                                             in_=pbc[:]))())
        sc.op("vector", (lambda m=m: lambda e: e.tensor_copy(out=wsc[:, m:m + 1],
                                                             in_=pbc2[:]))(), inc="V")
        for k in range(8):
            par = k % 2
            sc.wait("sync", "V", vload[par])
            lv = sc.op("sync", (lambda dram=dram, k=k, par=par, nr=nr, isp=isp:
                                lambda e: e.dma_start(
                                    out=stg[par][:].rearrange("p (i o) -> p i o", i=isp),
                                    in_=dram[k * nr:(k + 1) * nr, :]
                                    .rearrange("(i p) o -> p i o", p=128)))(),
                       inc="Lstg%d" % par)
            sc.wait("vector", "Lstg%d" % par, lv)
            if par in gstore:
                sc.wait("vector", "Lqbf%d" % par, gstore[par])
            sc.op("vector", (lambda par=par, m=m:
                             lambda e: e.tensor_scalar(out=stg[par][:], in0=stg[par][:],
                                                       scalar1=wsc[:, m:m + 1],
                                                       scalar2=None, op0=ALU.mult))())
            sc.op("vector", (lambda par=par:
                             lambda e: e.tensor_scalar(out=stg[par][:], in0=stg[par][:],
                                                       scalar1=t["mgk"][:, 0:1],
                                                       scalar2=None, op0=ALU.add))())
            sc.op("vector", (lambda par=par:
                             lambda e: e.tensor_scalar(out=stg[par][:], in0=stg[par][:],
                                                       scalar1=t["mgk"][:, 1:2], scalar2=1.0,
                                                       op0=ALU.add, op1=ALU.min))())
            vload[par] = sc.op("vector", (lambda par=par:
                                          lambda e: e.tensor_scalar(out=qbf[par][:],
                                                                    in0=stg[par][:],
                                                                    scalar1=-1.0,
                                                                    scalar2=None,
                                                                    op0=ALU.max))(),
                               inc="V")
            sc.wait("gpsimd", "V", vload[par])
            gstore[par] = sc.op("gpsimd",
                                (lambda scratch=scratch, k=k, par=par, nr=nr, isp=isp:
                                 lambda e: e.dma_start(
                                     out=scratch[k * nr:(k + 1) * nr, :]
                                     .rearrange("(i p) o -> p i o", p=128),
                                     in_=qbf[par][:].rearrange("p (i o) -> p i o",
                                                               i=isp)))(), inc="Lqbf%d" % par)



def phase_pass1(nc, sc, t, c, a, s):
    KT, Vr, W = a["KT"], a["V"], a["W"]
    xb, xT = c["xb"], c["xT"]
    qstage = c["qstage"]
    ptx, psA, psV = c["ptx"], c["psA"], c["psV"]
    brows, ones5 = a["brows"], a["ones512"]
    NG = 16

    xb_free, evac_done, mm_done = {}, {}, {}
    qev_done, qst_done, vev_done = {}, {}, {}

    def emit_transposes(g):
        for ts in range(2):
            tc_ = 2 * g + ts
            r0 = (s * TC + tc_) * 128
            par = tc_ % 2
            if tc_ >= 2:
                sc.wait("gpsimd", "T", xb_free[tc_ - 2])
            lv = sc.op("gpsimd", (lambda r0=r0, par=par:
                                  lambda e: e.dma_start(out=xb[par][:],
                                                        in_=t["x_in"][r0:r0 + 128, :]))(),
                       inc="Lxb%d" % par)
            sc.wait("tensor", "Lxb%d" % par, lv)
            if tc_ >= 2:
                sc.wait("tensor", "V", evac_done[tc_ - 2])
            for dcs in range(DC):
                sc.op("tensor", (lambda par=par, dcs=dcs:
                                 lambda e: e.transpose(ptx[par][:, dcs * 128:(dcs + 1) * 128],
                                                       xb[par][:, dcs * 128:(dcs + 1) * 128],
                                                       a["ident"]))(),
                      inc="T" if dcs == DC - 1 else None)
            xb_free[tc_] = sc.count("T")
            sc.wait("vector", "T", sc.count("T"))
            if g >= 2:
                sc.wait("vector", "T", mm_done[g - 2])
            dst = xT[g % 2].rearrange("p (dc two u) -> p dc two u", dc=DC, two=2)
            src = ptx[par].ap().rearrange("p (dc u) -> p dc u", dc=DC)
            evac_done[tc_] = sc.op("vector", (lambda dst=dst, src=src, ts=ts:
                                              lambda e: e.tensor_copy(out=dst[:, :, ts, :],
                                                                      in_=src))(), inc="V")

    def emit_group_mm(g):
        sc.wait("tensor", "V", evac_done[2 * g + 1])
        for proj in (0, 1):
            for oc in range(DC):
                idx = g * 2 * DC + proj * DC + oc
                rot = idx % 2
                sc.wait("tensor", "V", qev_done.get(idx - 2, 0))
                for dcs in range(DC):
                    sc.op("tensor", (lambda rot=rot, proj=proj, oc=oc, dcs=dcs, g=g:
                                     lambda e: e.matmul(psA[rot].ap()[:, 0:256],
                                                        W[:, proj * 8192 + dcs * 1024 + oc * 128:
                                                          proj * 8192 + dcs * 1024 + oc * 128 + 128],
                                                        xT[g % 2][:, dcs * 256:(dcs + 1) * 256],
                                                        start=(dcs == 0), stop=False))())
                tt = sc.op("tensor", (lambda rot=rot, proj=proj, oc=oc:
                                      lambda e: e.matmul(psA[rot].ap()[:, 0:256],
                                                         brows[0:1, proj * 1024 + oc * 128:
                                                               proj * 1024 + oc * 128 + 128],
                                                         ones5[0:1, 0:256],
                                                         start=False, stop=True))(), inc="T")
                sc.wait("vector", "T", tt)
                if proj == 1:
                    qev_done[idx] = sc.op("vector",
                                          (lambda rot=rot, oc=oc, g=g:
                                           lambda e: e.tensor_copy(
                                               out=KT[:, oc * 4096 + g * 256:
                                                      oc * 4096 + g * 256 + 256],
                                               in_=psA[rot].ap()[:, 0:256]))(), inc="V")
                else:
                    if oc == 0 and g >= 1:
                        sc.wait("vector", "Lqst", qst_done[g - 1])
                    dstq = qstage.rearrange("p (two oc u) -> p two oc u", two=2, oc=DC)
                    qev_done[idx] = sc.op("vector",
                                          (lambda rot=rot, oc=oc, dstq=dstq:
                                           lambda e: e.tensor_copy(
                                               out=dstq[:, :, oc, :],
                                               in_=psA[rot].ap()[:, 0:256].rearrange(
                                                   "p (two u) -> p two u", two=2)))(),
                                          inc="V")
        sc.wait("gpsimd", "V", qev_done[g * 2 * DC + DC - 1])
        r0 = (s * TC + 2 * g) * 128
        qst_done[g] = sc.op("gpsimd", (lambda r0=r0:
                                       lambda e: e.dma_start(
                                           out=t["qt_s"][r0:r0 + 256, :]
                                           .rearrange("(i p) o -> p i o", p=128),
                                           in_=qstage.rearrange("p (i o) -> p i o",
                                                               i=2)))(), inc="Lqst")
        for ts in range(2):
            vidx = g * 2 + ts
            rot = vidx % 2
            tc_ = 2 * g + ts
            sc.wait("tensor", "V", vev_done.get(vidx - 2, 0))
            for j in range(2):
                for dcs in range(DC):
                    sc.op("tensor", (lambda rot=rot, ts=ts, j=j, dcs=dcs, g=g:
                                     lambda e: e.matmul(psV[rot].ap()[:, j * 512:(j + 1) * 512],
                                                        xT[g % 2][:, dcs * 256 + ts * 128:
                                                                   dcs * 256 + ts * 128 + 128],
                                                        W[:, 2 * 8192 + dcs * 1024 + j * 512:
                                                          2 * 8192 + dcs * 1024 + j * 512 + 512],
                                                        start=(dcs == 0), stop=False))())
                tt = sc.op("tensor", (lambda rot=rot, j=j:
                                      lambda e: e.matmul(psV[rot].ap()[:, j * 512:(j + 1) * 512],
                                                         ones5[0:1, 0:128],
                                                         brows[0:1, 2048 + j * 512:
                                                               2048 + (j + 1) * 512],
                                                         start=False, stop=True))(),
                           inc="T" if j == 1 else None)
            sc.wait("vector", "T", tt)
            vev_done[vidx] = sc.op("vector",
                                   (lambda rot=rot, tc_=tc_:
                                    lambda e: e.tensor_copy(
                                        out=Vr[:, tc_ * 1024:(tc_ + 1) * 1024],
                                        in_=psV[rot].ap()))(), inc="V")
        mm_done[g] = sc.count("T")

    for g in range(NG):
        emit_transposes(g)
        if g >= 1:
            emit_group_mm(g - 1)
    emit_group_mm(NG - 1)



def phase_qloop(nc, sc, t, c, a, s):
    KT, Vr = a["KT"], a["V"]
    qtile, xq = c["qtile"], c["xq"]
    Ssb, PTs = c["Ssb"], c["PTs"]
    Psb = c["Psb"]
    psS, psT, psO = c["psS"], c["psT"], c["psO"]
    st = c["stats"]
    hst, hqst = c["hstage"], c["hqstage"]
    hqb = c["hqb"]
    sqd, rbuf = c["sqdump"], c["rbuf"]
    mask = a["maskf"]

    score_done = {}
    epi_done = {}
    hstore, hqstore = {}, {}
    last_psS = {}
    last_psT = {}
    last_PTs = {}
    ex_prev = {}
    ssb_rd_T = {0: 0}

    for qt in range(TC):
        gc = s * TC + qt
        r0 = gc * 128
        nkc = qt + 1
        ngrp = (nkc + 3) // 4
        W_ = nkc * 128
        par = qt % 2
        if qt >= 2:
            sc.wait("sync", "T", score_done[qt - 2])
            sc.wait("sync", "V", epi_done.get(qt - 2, 0))
        qld = sc.op("sync", (lambda r0=r0, par=par:
                             lambda e: e.dma_start(out=qtile[par][:],
                                                   in_=t["qt_s"][r0:r0 + 128, :]))(), inc="Lqt%d" % par)
        xld = sc.op("sync", (lambda r0=r0, par=par:
                             lambda e: e.dma_start(out=xq[par][:],
                                                   in_=t["x_in"][r0:r0 + 128, :]))(), inc="Lxq%d" % par)
        sc.wait("tensor", "Lqt%d" % par, qld)
        ev_grp = {}
        for j in range(ngrp):
            w = min(512, (nkc - 4 * j) * 128)
            rot = j % 2
            sc.wait("tensor", "SC", last_psS.get(rot, 0))
            for dcs in range(DC):
                tt = sc.op("tensor", (lambda rot=rot, j=j, w=w, dcs=dcs, par=par:
                                      lambda e: e.matmul(psS[rot].ap()[:, 0:w],
                                                         qtile[par][:, dcs * 128:(dcs + 1) * 128],
                                                         KT[:, dcs * 4096 + j * 512:
                                                            dcs * 4096 + j * 512 + w],
                                                         start=(dcs == 0),
                                                         stop=(dcs == DC - 1)))(),
                           inc="T" if dcs == DC - 1 else None)
            sc.wait("vector", "T", tt)
            if j == ngrp - 1:
                sc.op("vector", (lambda rot=rot, w=w:
                                 lambda e: e.tensor_add(out=psS[rot].ap()[:, w - 128:w],
                                                        in0=psS[rot].ap()[:, w - 128:w],
                                                        in1=mask))())
            rv = sc.op("vector", (lambda rot=rot, w=w:
                                  lambda e: e.tensor_reduce(out=st[:, 0:1],
                                                            in_=psS[rot].ap()[:, 0:w],
                                                            axis=AX.X, op=ALU.max))(),
                       inc="V")
            sc.wait("vector", "V", rv)
            if j == 0:
                mv = sc.op("vector", lambda e: e.tensor_copy(out=st[:, 1:2], in_=st[:, 0:1]),
                           inc="V")
            else:
                mv = sc.op("vector", lambda e: e.tensor_max(out=st[:, 1:2], in0=st[:, 1:2],
                                                            in1=st[:, 0:1]), inc="V")
            sc.wait("scalar", "V", mv)
            if j <= 1:
                sc.wait("scalar", "T", ssb_rd_T.get(qt - 1, 0))
            ev_grp[j] = sc.op("scalar", (lambda rot=rot, j=j, w=w:
                                         lambda e: e.activation(out=Ssb[:, j * 512:j * 512 + w],
                                                                in_=psS[rot].ap()[:, 0:w],
                                                                func=AF.Copy,
                                                                scale=1.0 / 32.0))(), inc="SC")
            last_psS[rot] = ev_grp[j]
        score_done[qt] = sc.count("T")
        sc.wait("vector", "SC", ex_prev.get(qt - 1, 0))
        sc.wait("vector", "V", mv)
        bv = sc.op("vector", lambda e: e.tensor_scalar(out=st[:, 2:3], in0=st[:, 1:2],
                                                       scalar1=-1.0 / 32.0, scalar2=None,
                                                       op0=ALU.mult), inc="V")
        sc.wait("scalar", "V", bv)
        ex = sc.op("scalar", (lambda W_=W_:
                              lambda e: e.activation(out=Psb[:, 0:W_], in_=Ssb[:, 0:W_],
                                                     func=AF.Exp, bias=st[:, 2:3], scale=1.0,
                                                     accum_out=st[:, 3:4]))(), inc="SC")
        ex_prev[qt] = ex
        sc.wait("vector", "SC", ex)
        lv = sc.op("vector", lambda e: e.reciprocal(out=st[:, 4:5], in_=st[:, 3:4]), inc="V")
        sc.wait("vector", "V", lv)
        if t.get("dbg_p") is not None and s == 0 and qt == 0:
            sc.op("vector", lambda e: e.tensor_copy(out=c["dbgbuf"][:, 0:128],
                                                    in_=Ssb[:, 0:128]))
            sc.op("vector", lambda e: e.tensor_copy(out=c["dbgbuf"][:, 128:256],
                                                    in_=Psb[:, 0:128]))
            dv = sc.op("vector", lambda e: e.tensor_copy(out=c["dbgbuf"][:, 256:272],
                                                         in_=st[:, 0:16]), inc="V")
            sc.wait("gpsimd", "V", dv)
            sc.op("gpsimd", lambda e: e.dma_start(out=t["dbg_p"][0:128, 0:272],
                                                  in_=c["dbgbuf"][:, 0:272]), inc="LG")
            dbg_qt0 = True
        else:
            dbg_qt0 = False
        nb = (nkc + 3) // 4
        pt_ev = {}
        sc.wait("tensor", "SC", ex)
        sc.wait("tensor", "V", epi_done.get(qt - 1, 0))

        def emit_pt(b, qt=qt):
            rot = b % 2
            sc.wait("tensor", "V", last_psT.get(rot, 0))
            nch = min(4, nkc - 4 * b)
            for u in range(nch):
                cidx = 4 * b + u
                tt = sc.op("tensor", (lambda rot=rot, u=u, cidx=cidx:
                                      lambda e: e.transpose(psT[rot].ap()[:, u * 128:(u + 1) * 128],
                                                            Psb[:, cidx * 128:(cidx + 1) * 128],
                                                            a["ident"]))(),
                           inc="T" if u == nch - 1 else None)
            ssb_rd_T[qt] = tt
            sc.wait("vector", "T", tt)
            sc.wait("vector", "T", last_PTs.get(rot, 0))
            pt_ev[b] = sc.op("vector", (lambda rot=rot, nch=nch:
                                        lambda e: e.tensor_copy(
                                            out=PTs[:, rot * 512:rot * 512 + nch * 128],
                                            in_=psT[rot].ap()[:, 0:nch * 128]))(), inc="V")
            last_psT[rot] = pt_ev[b]

        def emit_pv(b, final):
            rot = b % 2
            sc.wait("tensor", "V", pt_ev[b])
            nch = min(4, nkc - 4 * b)
            tt = None
            for u in range(nch):
                cidx = 4 * b + u
                for j in range(2):
                    batch_last = u == nch - 1 and j == 1
                    tt = sc.op("tensor",
                               (lambda rot=rot, u=u, cidx=cidx, j=j, final=final:
                                lambda e: e.matmul(psO.ap()[:, j * 512:(j + 1) * 512],
                                                   PTs[:, rot * 512 + u * 128:
                                                       rot * 512 + (u + 1) * 128],
                                                   Vr[:, cidx * 1024 + j * 512:
                                                      cidx * 1024 + j * 512 + 512],
                                                   start=(cidx == 0),
                                                   stop=(final and u == nch - 1)))(),
                               inc="T" if batch_last else None)
            last_PTs[rot] = tt
            return tt

        for b in range(nb):
            emit_pt(b)
            if b >= 1:
                emit_pv(b - 1, final=False)
        pv_done = emit_pv(nb - 1, final=True)
        sc.wait("vector", "T", pv_done)
        sc.wait("vector", "Lxq%d" % par, xld)
        ep = sc.op("vector", (lambda par=par:
                              lambda e: e.scalar_tensor_tensor(out=rbuf, in0=psO.ap(),
                                                               scalar=st[:, 4:5], in1=xq[par][:],
                                                               op0=ALU.mult, op1=ALU.add,
                                                               accum_out=st[:, 5:6]))(), inc="V")
        epi_done[qt] = ep
        sc.wait("vector", "V", ep)
        if t.get("dbg_p") is not None and s == 0 and qt == 0:
            sc.op("vector", lambda e: e.tensor_copy(out=c["dbgbuf2"][:, 0:1024],
                                                    in_=rbuf))
        fv = sc.op("vector", lambda e: e.tensor_scalar(out=st[:, 6:7], in0=st[:, 5:6],
                                                       scalar1=1.0 / 1024.0, scalar2=None,
                                                       op0=ALU.mult), inc="V")
        sc.wait("vector", "V", fv)
        mv = sc.op("vector", lambda e: e.tensor_mul(out=st[:, 7:8], in0=st[:, 6:7],
                                                    in1=st[:, 6:7]), inc="V")
        sc.wait("scalar", "V", mv)
        sq = sc.op("scalar", lambda e: e.activation(out=sqd, in_=rbuf, func=AF.Square,
                                                    accum_out=st[:, 8:9]), inc="SC")
        sc.wait("vector", "SC", sq)
        fv = sc.op("vector", lambda e: e.tensor_scalar(out=st[:, 9:10], in0=st[:, 8:9],
                                                       scalar1=1.0 / 1024.0, scalar2=1e-5,
                                                       op0=ALU.mult, op1=ALU.add), inc="V")
        sc.wait("vector", "V", fv)
        vv = sc.op("vector", lambda e: e.tensor_sub(out=st[:, 10:11], in0=st[:, 9:10],
                                                    in1=st[:, 7:8]), inc="V")
        sc.wait("scalar", "V", vv)
        sc.op("scalar", lambda e: e.activation(out=st[:, 11:12], in_=st[:, 10:11],
                                               func=AF.Ln, scale=1.0))
        rs = sc.op("scalar", lambda e: e.activation(out=st[:, 12:13], in_=st[:, 11:12],
                                                    func=AF.Exp, scale=-0.5), inc="SC")
        sc.wait("vector", "SC", rs)
        nm = sc.op("vector", lambda e: e.scalar_tensor_tensor(out=st[:, 13:14], in0=st[:, 6:7],
                                                              scalar=-1.0, in1=st[:, 12:13],
                                                              op0=ALU.mult, op1=ALU.mult),
                   inc="V")
        sc.wait("scalar", "V", nm)
        if t.get("dbg_p") is not None and s == 0 and qt == 0:
            dv2 = sc.op("vector", lambda e: e.tensor_copy(out=c["dbgbuf2"][:, 1024:1040],
                                                          in_=st[:, 0:16]), inc="V")
            sc.wait("gpsimd", "V", dv2)
            sc.op("gpsimd", lambda e: e.dma_start(out=t["dbg_s"][0:128, 0:1040].cast(F32)
                                                  if False else t["dbg_s2"][0:128, :],
                                                  in_=c["dbgbuf2"][:, 0:1040]), inc="LG")
        if qt >= 2:
            sc.wait("scalar", "Lh%d" % par, hstore[qt - 2])
        hv = sc.op("scalar", (lambda par=par:
                              lambda e: e.activation(out=hst[par][:], in_=rbuf, func=AF.Identity,
                                                     scale=st[:, 12:13], bias=st[:, 13:14]))(),
                   inc="SC")
        sc.wait("gpsimd", "SC", hv)
        hstore[qt] = sc.op("gpsimd", (lambda r0=r0, par=par:
                                      lambda e: e.dma_start(out=t["h_s"][r0:r0 + 128, :],
                                                            in_=hst[par][:]))(), inc="Lh%d" % par)
        sc.wait("vector", "SC", hv)
        rv = sc.op("vector", (lambda par=par:
                              lambda e: e.tensor_reduce(out=st[:, 14:15], in_=hst[par][:],
                                                        axis=AX.X, op=ALU.max,
                                                        apply_absolute_value=True))(), inc="V")
        sc.wait("vector", "V", rv)
        fv = sc.op("vector", lambda e: e.tensor_scalar(out=st[:, 14:15], in0=st[:, 14:15],
                                                       scalar1=1e-5, scalar2=None,
                                                       op0=ALU.max), inc="V")
        sc.wait("vector", "V", fv)
        sc.op("vector", (lambda gc=gc:
                         lambda e: e.tensor_scalar(out=t["rs_h"][:, gc:gc + 1],
                                                   in0=st[:, 14:15], scalar1=1.0 / 127.0,
                                                   scalar2=None, op0=ALU.mult))())
        rv = sc.op("vector", lambda e: e.reciprocal(out=st[:, 15:16], in_=st[:, 14:15]),
                   inc="V")
        sc.wait("vector", "V", rv)
        fv = sc.op("vector", lambda e: e.tensor_scalar(out=st[:, 15:16], in0=st[:, 15:16],
                                                       scalar1=127.0, scalar2=None,
                                                       op0=ALU.mult), inc="V")
        sc.wait("vector", "V", fv)
        sc.op("vector", (lambda par=par:
                         lambda e: e.tensor_scalar(out=rbuf, in0=hst[par][:],
                                                   scalar1=st[:, 15:16], scalar2=None,
                                                   op0=ALU.mult))())
        sc.op("vector", lambda e: e.tensor_scalar(out=rbuf, in0=rbuf,
                                                  scalar1=t["mgk"][:, 0:1], scalar2=None,
                                                  op0=ALU.add))
        sc.op("vector", lambda e: e.tensor_scalar(out=rbuf, in0=rbuf,
                                                  scalar1=t["mgk"][:, 1:2],
                                                  scalar2=127.0, op0=ALU.add, op1=ALU.min))
        hb = sc.op("vector", lambda e: e.tensor_scalar(out=hqb, in0=rbuf, scalar1=-128.0,
                                                       scalar2=None, op0=ALU.max), inc="V")
        sc.wait("tensor", "V", hb)
        for b in range(2):
            rot = b % 2
            sc.wait("tensor", "V", last_psT.get(rot, 0))
            for u in range(4):
                dcs = 4 * b + u
                tt = sc.op("tensor", (lambda rot=rot, dcs=dcs, u=u:
                                      lambda e: e.transpose(psT[rot].ap()[:, u * 128:(u + 1) * 128],
                                                            hqb[:, dcs * 128:(dcs + 1) * 128],
                                                            a["ident"]))(),
                           inc="T" if u == 3 else None)
            sc.wait("vector", "T", tt)
            if b == 0 and qt >= 2:
                sc.wait("vector", "Lhq%d" % par, hqstore[qt - 2])
            hev = sc.op("vector", (lambda rot=rot, b=b, par=par:
                                   lambda e: e.tensor_copy(
                                       out=hqst[par][:, b * 512:(b + 1) * 512],
                                       in_=psT[rot].ap()))(), inc="V")
            last_psT[rot] = hev
        sc.wait("gpsimd", "V", hev)
        hqstore[qt] = sc.op("gpsimd", (lambda r0=r0, par=par:
                                       lambda e: e.dma_start(out=t["hq_s"][r0:r0 + 128, :],
                                                             in_=hqst[par][:]))(), inc="Lhq%d" % par)
        if t.get("dbg_st") is not None:
            sc.op("gpsimd", (lambda r0=r0:
                             lambda e: e.dma_start(out=t["dbg_st"][r0:r0 + 128, :],
                                                   in_=st[:, 0:16]))(), inc="LG")
            if s == 0 and qt == 20:
                sc.op("gpsimd", lambda e: e.dma_start(out=t["dbg_p"], in_=Psb), inc="LG")
                sc.op("gpsimd", lambda e: e.dma_start(out=t["dbg_s"], in_=Ssb), inc="LG")



def phase_b1(nc, sc, t, c):
    wgq, wuq = c["wgq"], c["wuq"]
    hqT, sg = c["hqT"], c["sg"]
    ybf, yqb = c["ybf"], c["yqstage"]
    yqbf = c["yqbf"]
    pg, pu, psT = c["pg"], c["pu"], c["psTb"]
    stb = c["statsb"]

    for dcs in range(DC):
        sc.op("sync", (lambda dcs=dcs:
                       lambda e: e.dma_start(out=wgq[:, dcs * 4096:(dcs + 1) * 4096],
                                             in_=t["wgq_s"][dcs * 128:(dcs + 1) * 128, :]))(),
              inc="LWB1")
        sc.op("sync", (lambda dcs=dcs:
                       lambda e: e.dma_start(out=wuq[:, dcs * 4096:(dcs + 1) * 4096],
                                             in_=t["wuq_s"][dcs * 128:(dcs + 1) * 128, :]))(),
              inc="LWB1")
    sc.wait("tensor", "LWB1", sc.count("LWB1"))

    b1mm = {}
    ymul = {}
    yb_done = {}
    yst = {}
    last_psT = {}

    def emit_windows(gc):
        r0 = gc * 128
        par = gc % 2
        if gc >= 2:
            sc.wait("sync", "T", b1mm[gc - 2])
        hld = sc.op("sync", (lambda r0=r0, par=par:
                             lambda e: e.dma_start(out=hqT[par][:],
                                                   in_=t["hq_s"][r0:r0 + 128, :]))(),
                    inc="Lhqt%d" % par)
        sc.wait("tensor", "Lhqt%d" % par, hld)
        sc.op("vector", (lambda gc=gc:
                         lambda e: e.tensor_mul(out=stb[:, 0:1], in0=t["rs_h"][:, gc:gc + 1],
                                                in1=t["wmean"][:, 0:1]))())
        cgv = sc.op("vector", (lambda gc=gc:
                               lambda e: e.tensor_mul(out=stb[:, 1:2],
                                                      in0=t["rs_h"][:, gc:gc + 1],
                                                      in1=t["wmean"][:, 1:2]))(), inc="V")
        sc.wait("vector", "V", cgv)
        for win in range(8):
            wi = gc * 8 + win
            rot = win % 2
            sc.wait("tensor", "V", ymul.get(wi - 2, 0))
            tt = None
            for wq_, ps_ in ((wgq, pg), (wuq, pu)):
                for dcs in range(DC):
                    last = dcs == DC - 1 and ps_ is pu
                    tt = sc.op("tensor",
                               (lambda wq_=wq_, ps_=ps_, rot=rot, win=win, dcs=dcs, par=par:
                                lambda e: e.matmul(ps_[rot].ap(),
                                                   hqT[par][:, dcs * 128:(dcs + 1) * 128],
                                                   wq_[:, dcs * 4096 + win * 512:
                                                       dcs * 4096 + win * 512 + 512],
                                                   start=(dcs == 0), stop=(dcs == DC - 1)))(),
                               inc="T" if last else None)
            sc.wait("scalar", "T", tt)
            if win == 0:
                sc.wait("scalar", "V", cgv)
            sl = sc.op("scalar", (lambda rot=rot:
                                  lambda e: e.activation(out=sg[rot], in_=pg[rot].ap(),
                                                         func=AF.Silu, scale=stb[:, 0:1]))(),
                       inc="SC")
            sc.wait("vector", "SC", sl)
            ymul[wi] = sc.op("vector",
                             (lambda rot=rot, win=win:
                              lambda e: e.scalar_tensor_tensor(
                                  out=ybf[:, win * 512:(win + 1) * 512],
                                  in0=pu[rot].ap(), scalar=stb[:, 1:2], in1=sg[rot],
                                  op0=ALU.mult, op1=ALU.mult))(), inc="V")
        b1mm[gc] = sc.count("T")
        rv = sc.op("vector", lambda e: e.tensor_reduce(out=stb[:, 2:3], in_=ybf, axis=AX.X,
                                                       op=ALU.max, apply_absolute_value=True),
                   inc="V")
        sc.wait("vector", "V", rv)
        fv = sc.op("vector", lambda e: e.tensor_scalar(out=stb[:, 2:3], in0=stb[:, 2:3],
                                                       scalar1=1e-5, scalar2=None,
                                                       op0=ALU.max), inc="V")
        sc.wait("vector", "V", fv)
        sc.op("vector", (lambda gc=gc:
                         lambda e: e.tensor_scalar(out=t["rs_y"][:, gc:gc + 1],
                                                   in0=stb[:, 2:3], scalar1=1.0 / 127.0,
                                                   scalar2=None, op0=ALU.mult))())
        rv = sc.op("vector", lambda e: e.reciprocal(out=stb[:, 3:4], in_=stb[:, 2:3]),
                   inc="V")
        sc.wait("vector", "V", rv)
        fv = sc.op("vector", lambda e: e.tensor_scalar(out=stb[:, 3:4], in0=stb[:, 3:4],
                                                       scalar1=127.0, scalar2=None,
                                                       op0=ALU.mult), inc="V")
        sc.wait("vector", "V", fv)
        if gc >= 2:
            sc.wait("vector", "T", tq_done.get(gc - 2, 0))
        sc.op("vector", lambda e: e.tensor_scalar(out=c["yf32"], in0=ybf,
                                                  scalar1=stb[:, 3:4], scalar2=None,
                                                  op0=ALU.mult))
        sc.op("vector", lambda e: e.tensor_scalar(out=c["yf32"], in0=c["yf32"],
                                                  scalar1=t["mgk"][:, 0:1], scalar2=None,
                                                  op0=ALU.add))
        sc.op("vector", lambda e: e.tensor_scalar(out=c["yf32"], in0=c["yf32"],
                                                  scalar1=t["mgk"][:, 1:2],
                                                  scalar2=127.0, op0=ALU.add, op1=ALU.min))
        yb_done[gc] = sc.op("vector", (lambda gc=gc:
                                       lambda e: e.tensor_scalar(out=yqbf[gc % 2],
                                                                 in0=c["yf32"],
                                                                 scalar1=-128.0, scalar2=None,
                                                                 op0=ALU.max))(), inc="V")

    tq_done = {}

    def emit_yq_T(gc):
        r0 = gc * 128
        sc.wait("tensor", "V", yb_done[gc])
        for b in range(8):
            rot = b % 2
            sc.wait("tensor", "V", last_psT.get(rot, 0))
            for u in range(4):
                icx = 4 * b + u
                tt = sc.op("tensor", (lambda rot=rot, icx=icx, u=u, gc=gc:
                                      lambda e: e.transpose(psT[rot].ap()[:, u * 128:(u + 1) * 128],
                                                            yqbf[gc % 2][:, icx * 128:(icx + 1) * 128],
                                                            t["identb"]))(),
                           inc="T" if u == 3 else None)
            if b == 7:
                tq_done[gc] = tt
            sc.wait("vector", "T", tt)
            if b == 0 and gc >= 1:
                sc.wait("vector", "Lyq", yst[gc - 1])
            pe = sc.op("vector", (lambda rot=rot, b=b:
                                  lambda e: e.tensor_copy(out=yqb[:, b * 512:(b + 1) * 512],
                                                          in_=psT[rot].ap()))(), inc="V")
            last_psT[rot] = pe
        sc.wait("gpsimd", "V", pe)
        yst[gc] = sc.op("gpsimd", (lambda r0=r0:
                                   lambda e: e.dma_start(out=t["yq_s"][r0:r0 + 128, :],
                                                         in_=yqb))(), inc="Lyq")

    import os as _os
    _gcl = int(_os.environ.get("B1GC", GC))
    for gc in range(_gcl):
        emit_windows(gc)
        if gc >= 1:
            emit_yq_T(gc - 1)
    emit_yq_T(_gcl - 1)



def phase_b2(nc, sc, t, c):
    wdq = c["wdq"]
    yqT, hres, ost = c["yqT"], c["hres"], c["ostage"]
    r2, sqd2 = c["r2"], c["sqd2"]
    pd = c["pd"]
    stb = c["statsd"]

    for icx in range(IC):
        sc.op("sync", (lambda icx=icx:
                       lambda e: e.dma_start(out=wdq[:, icx * 1024:(icx + 1) * 1024],
                                             in_=t["wdq_s"][icx * 128:(icx + 1) * 128, :]))(),
              inc="LWB2")
    sc.wait("tensor", "LWB2", sc.count("LWB2"))

    dmm, ep2, odone, ostore = {}, {}, {}, {}
    for gc in range(GC):
        r0 = gc * 128
        par = gc % 3
        rot = gc % 3
        if gc >= 3:
            sc.wait("sync", "T", dmm[gc - 3])
            sc.wait("sync", "V", ep2[gc - 3])
        yl = sc.op("sync", (lambda r0=r0, par=par:
                            lambda e: e.dma_start(out=yqT[par][:],
                                                  in_=t["yq_s"][r0:r0 + 128, :]))(), inc="Lyqt%d" % par)
        hl = sc.op("sync", (lambda r0=r0, par=par:
                            lambda e: e.dma_start(out=hres[par][:],
                                                  in_=t["h_s"][r0:r0 + 128, :]))(), inc="Lhr%d" % par)
        sc.wait("tensor", "Lyqt%d" % par, yl)
        sc.wait("tensor", "V", ep2.get(gc - 3, 0))
        tt = None
        for j in range(2):
            for icx in range(IC):
                last = icx == IC - 1 and j == 1
                tt = sc.op("tensor", (lambda rot=rot, j=j, icx=icx, par=par:
                                      lambda e: e.matmul(pd[rot].ap()[:, j * 512:(j + 1) * 512],
                                                         yqT[par][:, icx * 128:(icx + 1) * 128],
                                                         wdq[:, icx * 1024 + j * 512:
                                                             icx * 1024 + j * 512 + 512],
                                                         start=(icx == 0),
                                                         stop=(icx == IC - 1)))(),
                           inc="T" if last else None)
        dmm[gc] = tt
        cdv = sc.op("vector", (lambda gc=gc:
                               lambda e: e.tensor_mul(out=stb[:, 0:1],
                                                      in0=t["rs_y"][:, gc:gc + 1],
                                                      in1=t["wmean"][:, 2:3]))(), inc="V")
        sc.wait("vector", "T", dmm[gc])
        sc.wait("vector", "Lhr%d" % par, hl)
        sc.wait("vector", "V", cdv)
        sc.wait("vector", "SC", odone.get(gc - 1, 0))
        r2c = r2[gc % 2]
        ep = sc.op("vector", (lambda rot=rot, par=par, r2c=r2c:
                              lambda e: e.scalar_tensor_tensor(out=r2c, in0=pd[rot].ap(),
                                                               scalar=stb[:, 0:1],
                                                               in1=hres[par][:],
                                                               op0=ALU.mult, op1=ALU.add,
                                                               accum_out=stb[:, 1:2]))(),
                   inc="V")
        ep2[gc] = ep
        sc.wait("vector", "V", ep)
        fv = sc.op("vector", lambda e: e.tensor_scalar(out=stb[:, 2:3], in0=stb[:, 1:2],
                                                       scalar1=1.0 / 1024.0, scalar2=None,
                                                       op0=ALU.mult), inc="V")
        sc.wait("vector", "V", fv)
        mv = sc.op("vector", lambda e: e.tensor_mul(out=stb[:, 3:4], in0=stb[:, 2:3],
                                                    in1=stb[:, 2:3]), inc="V")
        sc.wait("scalar", "V", mv)
        sq = sc.op("scalar", (lambda r2c=r2c:
                              lambda e: e.activation(out=sqd2, in_=r2c, func=AF.Square,
                                                     accum_out=stb[:, 4:5]))(), inc="SC")
        sc.wait("vector", "SC", sq)
        fv = sc.op("vector", lambda e: e.tensor_scalar(out=stb[:, 5:6], in0=stb[:, 4:5],
                                                       scalar1=1.0 / 1024.0, scalar2=1e-5,
                                                       op0=ALU.mult, op1=ALU.add), inc="V")
        sc.wait("vector", "V", fv)
        vv = sc.op("vector", lambda e: e.tensor_sub(out=stb[:, 5:6], in0=stb[:, 5:6],
                                                    in1=stb[:, 3:4]), inc="V")
        sc.wait("scalar", "V", vv)
        sc.op("scalar", lambda e: e.activation(out=stb[:, 6:7], in_=stb[:, 5:6],
                                               func=AF.Ln, scale=1.0))
        rs = sc.op("scalar", lambda e: e.activation(out=stb[:, 7:8], in_=stb[:, 6:7],
                                                    func=AF.Exp, scale=-0.5), inc="SC")
        sc.wait("vector", "SC", rs)
        nm = sc.op("vector", lambda e: e.scalar_tensor_tensor(out=stb[:, 2:3], in0=stb[:, 2:3],
                                                              scalar=-1.0, in1=stb[:, 7:8],
                                                              op0=ALU.mult, op1=ALU.mult),
                   inc="V")
        sc.wait("scalar", "V", nm)
        if gc >= 3:
            sc.wait("scalar", "Lout%d" % par, ostore[gc - 3])
        ov = sc.op("scalar", (lambda par=par, r2c=r2c:
                              lambda e: e.activation(out=ost[par][:], in_=r2c,
                                                     func=AF.Identity,
                                                     scale=stb[:, 7:8], bias=stb[:, 2:3]))(),
                   inc="SC")
        odone[gc] = ov
        sc.wait("sync", "SC", ov)
        ostore[gc] = sc.op("sync", (lambda r0=r0, par=par:
                                    lambda e: e.dma_start(out=t["out"][r0:r0 + 128, :],
                                                          in_=ost[par][:]))(), inc="Lout%d" % par)
    sc.wait("sync", "Lout0", sc.count("Lout0"))
    sc.wait("sync", "Lout1", sc.count("Lout1"))
    sc.wait("sync", "Lout2", sc.count("Lout2"))



def build_nc(dbg=False, phases=("Q", "A", "B1", "B2"), nseq=SEQ):
    nc = bass.Bass("TRN2", target_bir_lowering=False, detect_race_conditions=False)
    skind = "ExternalOutput" if dbg else "Internal"
    t = {}
    t["x_in"] = nc.dram_tensor("x_in", [SEQ * S, D], F32, kind="ExternalInput").ap()
    for nm, shp in (("wqT", [D, D]), ("wkT", [D, D]), ("wvT", [D, D]),
                    ("wgT", [D, INNER]), ("wuT", [D, INNER]), ("wdT", [INNER, D])):
        t[nm] = nc.dram_tensor(nm, shp, F32, kind="ExternalInput").ap()
    t["brows"] = nc.dram_tensor("brows", [1, 3 * D], F32, kind="ExternalInput").ap()
    t["ident_in"] = nc.dram_tensor("ident_in", [128, 128], BF16, kind="ExternalInput").ap()
    t["onesf_in"] = nc.dram_tensor("onesf_in", [128, 128], F32, kind="ExternalInput").ap()
    t["onesb_in"] = nc.dram_tensor("onesb_in", [1, 512], BF16, kind="ExternalInput").ap()
    t["mask_in"] = nc.dram_tensor("mask_in", [128, 128], F32, kind="ExternalInput").ap()
    t["out"] = nc.dram_tensor("out", [SEQ * S, D], F32, kind="ExternalOutput").ap()
    t["qt_s"] = nc.dram_tensor("qt_s", [SEQ * S, D], BF16, kind=skind).ap()
    if dbg:
        t["dbg_st"] = nc.dram_tensor("dbg_st", [SEQ * S, 16], F32, kind=skind).ap()
        t["dbg_p"] = nc.dram_tensor("dbg_p", [128, 4096], BF16, kind=skind).ap()
        t["dbg_s"] = nc.dram_tensor("dbg_s", [128, 4096], BF16, kind=skind).ap()
        t["dbg_s2"] = nc.dram_tensor("dbg_s2", [128, 1040], F32, kind=skind).ap()
    t["h_s"] = nc.dram_tensor("h_s", [SEQ * S, D], F32, kind=skind).ap()
    t["hq_s"] = nc.dram_tensor("hq_s", [SEQ * S, D], BF16, kind=skind).ap()
    t["yq_s"] = nc.dram_tensor("yq_s", [SEQ * S, INNER], BF16, kind=skind).ap()
    t["wgq_s"] = nc.dram_tensor("wgq_s", [D, INNER], BF16, kind=skind).ap()
    t["wuq_s"] = nc.dram_tensor("wuq_s", [D, INNER], BF16, kind=skind).ap()
    t["wdq_s"] = nc.dram_tensor("wdq_s", [INNER, D], BF16, kind=skind).ap()

    counts = {}
    with ExitStack() as top:
        E = top.enter_context
        sem_names = ["LX", "LG", "SC", "V", "T", "Lconst", "LW", "Lqst", "Lyq",
                     "LWB1", "LWB2"]
        for pfx in ("Lstg", "Lqbf", "Lxb", "Lqt", "Lxq", "Lh", "Lhq", "Lhqt",
                    "Lyqt", "Lhr", "Lout"):
            sem_names += [pfx + "0", pfx + "1"]
        sem_names += ["Lyqt2", "Lhr2", "Lout2"]
        sems = {s_: E(nc.semaphore(s_)) for s_ in sem_names}
        ident = E(nc.sbuf_tensor("ident", [128, 128], BF16))
        maskf = E(nc.sbuf_tensor("maskf", [128, 128], F32))
        onesf = E(nc.sbuf_tensor("onesf", [128, 128], F32))
        onesb = E(nc.sbuf_tensor("onesb", [1, 512], BF16))
        t["wsc"] = E(nc.sbuf_tensor("wsc", [128, 4], F32)).ap()
        t["wmean"] = E(nc.sbuf_tensor("wmean", [128, 4], F32)).ap()
        t["rs_h"] = E(nc.sbuf_tensor("rs_h", [128, GC], F32)).ap()
        t["rs_y"] = E(nc.sbuf_tensor("rs_y", [128, GC], F32)).ap()
        t["mgk"] = E(nc.sbuf_tensor("mgk", [128, 4], F32)).ap()
        t["identb"] = ident.ap()

        if "Q" not in phases:
            with ExitStack() as ph:
                P = ph.enter_context
                sc = Sched(counts)
                sc.op("sync", lambda e: e.dma_start(out=ident.ap(), in_=t["ident_in"]),
                      inc="Lconst")
                sc.op("sync", lambda e: e.dma_start(out=maskf.ap(), in_=t["mask_in"]),
                      inc="Lconst")
                sc.op("sync", lambda e: e.dma_start(out=onesf.ap(), in_=t["onesf_in"]),
                      inc="Lconst")
                sc.op("sync", lambda e: e.dma_start(out=onesb.ap(), in_=t["onesb_in"]),
                      inc="Lconst")
                run_block(nc, sc, sems)
                prev = snap(counts)
        if True and "Q" in phases:
         with ExitStack() as ph:
            P = ph.enter_context
            c = {}
            c["q_stg"] = [P(nc.sbuf_tensor(f"q_stg{i}", [128, 4096], F32)).ap() for i in range(2)]
            c["q_bf"] = [P(nc.sbuf_tensor(f"q_bf{i}", [128, 4096], BF16)).ap() for i in range(2)]
            c["q_macc"] = P(nc.sbuf_tensor("q_macc", [128, 4], F32)).ap()
            c["q_tmp"] = P(nc.sbuf_tensor("q_tmp", [128, 4], F32)).ap()
            c["q_s11"] = P(nc.sbuf_tensor("q_s11", [1, 4], F32)).ap()
            c["ones_col"] = onesf.ap()[:, 0:1]
            c["ones_rowf"] = onesf.ap()[0:1, 0:128]
            c["q_p11"] = P(nc.psum_tensor("q_p11", [1, 1], F32))
            c["q_pbc"] = P(nc.psum_tensor("q_pbc", [128, 1], F32))
            c["q_pbc2"] = P(nc.psum_tensor("q_pbc2", [128, 1], F32))
            sc = Sched(counts)
            sc.op("sync", lambda e: e.dma_start(out=ident.ap(), in_=t["ident_in"]), inc="Lconst")
            sc.op("sync", lambda e: e.dma_start(out=maskf.ap(), in_=t["mask_in"]), inc="Lconst")
            sc.op("sync", lambda e: e.dma_start(out=onesf.ap(), in_=t["onesf_in"]), inc="Lconst")
            lcv = sc.op("sync", lambda e: e.dma_start(out=onesb.ap(), in_=t["onesb_in"]),
                        inc="Lconst")
            sc.wait("vector", "Lconst", lcv)
            sc.wait("tensor", "Lconst", lcv)
            mg = t["mgk"]
            fv = sc.op("vector", lambda e: e.tensor_scalar(out=mg[:, 2:3],
                                                           in0=onesf.ap()[:, 0:1],
                                                           scalar1=1e-30, scalar2=None,
                                                           op0=ALU.mult), inc="V")
            sc.wait("vector", "V", fv)
            fv = sc.op("vector", lambda e: e.tensor_scalar(out=mg[:, 2:3], in0=mg[:, 2:3],
                                                           scalar1=1e-30, scalar2=None,
                                                           op0=ALU.mult), inc="V")
            sc.wait("vector", "V", fv)
            fv = sc.op("vector", lambda e: e.tensor_scalar(out=mg[:, 3:4], in0=mg[:, 2:3],
                                                           scalar1=0.0, scalar2=None,
                                                           op0=ALU.is_gt), inc="V")
            sc.wait("vector", "V", fv)
            fv = sc.op("vector", lambda e: e.tensor_scalar(out=mg[:, 0:1], in0=mg[:, 3:4],
                                                           scalar1=M64 - M32, scalar2=M32,
                                                           op0=ALU.mult, op1=ALU.add),
                       inc="V")
            sc.wait("vector", "V", fv)
            fv = sc.op("vector", lambda e: e.tensor_scalar(out=mg[:, 1:2], in0=mg[:, 0:1],
                                                           scalar1=-1.0, scalar2=None,
                                                           op0=ALU.mult), inc="V")
            sc.wait("vector", "V", fv)
            phase_quant(nc, sc, t, c)
            run_block(nc, sc, sems)
            prev = snap(counts)

        with ExitStack() as pha:
            P = pha.enter_context
            a = {}
            a["KT"] = P(nc.sbuf_tensor("KT", [128, DC * 4096], BF16)).ap()
            a["V"] = P(nc.sbuf_tensor("Vres", [128, TC * 1024], BF16)).ap()
            a["ident"] = ident.ap()
            a["maskf"] = maskf.ap()
            a["ones512"] = onesb.ap()
            for s_ in range(nseq):
                with ExitStack() as ph1:
                    P1 = ph1.enter_context
                    c = {}
                    a["W"] = P1(nc.sbuf_tensor(f"Wqkv{s_}", [128, 3 * 8192], BF16)).ap()
                    a["brows"] = P1(nc.sbuf_tensor(f"browsb{s_}", [1, 3 * D], BF16)).ap()
                    c["xb"] = [P1(nc.sbuf_tensor(f"xb{i}_{s_}", [128, D], BF16))
                               for i in range(2)]
                    c["xT"] = [P1(nc.sbuf_tensor(f"xT{i}_{s_}", [128, 2048], BF16)).ap()
                               for i in range(2)]
                    c["qstage"] = P1(nc.sbuf_tensor(f"qstage{s_}", [128, 2048], BF16)).ap()
                    c["ptx"] = [P1(nc.psum_tensor(f"ptx{i}_{s_}", [128, 1024], BF16))
                                for i in range(2)]
                    c["psA"] = [P1(nc.psum_tensor(f"psA{i}_{s_}", [128, 512], F32))
                                for i in range(2)]
                    c["psV"] = [P1(nc.psum_tensor(f"psV{i}_{s_}", [128, 1024], F32))
                                for i in range(2)]
                    sc = Sched(counts)
                    for e_ in ENGINES:
                        sc.wait_all(e_, prev)
                    for proj, wnm in ((0, "wqT"), (1, "wkT"), (2, "wvT")):
                        for dcs in range(DC):
                            sc.op("gpsimd",
                                  (lambda proj=proj, wnm=wnm, dcs=dcs:
                                   lambda e: e.dma_start(
                                       out=a["W"][:, proj * 8192 + dcs * 1024:
                                                  proj * 8192 + (dcs + 1) * 1024],
                                       in_=t[wnm][dcs * 128:(dcs + 1) * 128, :]))(),
                                  inc="LW")
                    sc.op("gpsimd", lambda e: e.dma_start(out=a["brows"], in_=t["brows"]),
                          inc="LW")
                    sc.wait("tensor", "LW", sc.count("LW"))
                    phase_pass1(nc, sc, t, c, a, s_)
                    run_block(nc, sc, sems)
                    prev = snap(counts)
                with ExitStack() as ph2:
                    P2 = ph2.enter_context
                    c = {}
                    c["qtile"] = [P2(nc.sbuf_tensor(f"qtile{i}_{s_}", [128, D], BF16))
                                  for i in range(2)]
                    c["xq"] = [P2(nc.sbuf_tensor(f"xq{i}_{s_}", [128, D], F32))
                               for i in range(2)]
                    c["Ssb"] = P2(nc.sbuf_tensor(f"Ssb{s_}", [128, 4096], BF16)).ap()
                    c["Psb"] = P2(nc.sbuf_tensor(f"Psb{s_}", [128, 4096], BF16)).ap()
                    c["PTs"] = P2(nc.sbuf_tensor(f"PTs{s_}", [128, 1024], BF16)).ap()
                    c["stats"] = P2(nc.sbuf_tensor(f"stats{s_}", [128, 16], F32)).ap()
                    c["hstage"] = [P2(nc.sbuf_tensor(f"hst{i}_{s_}", [128, D], F32))
                                   for i in range(2)]
                    c["hqstage"] = [P2(nc.sbuf_tensor(f"hqst{i}_{s_}", [128, D], BF16))
                                    for i in range(2)]
                    c["hqb"] = P2(nc.sbuf_tensor(f"hqb{s_}", [128, D], BF16)).ap()
                    c["rbuf"] = P2(nc.sbuf_tensor(f"rbuf{s_}", [128, D], F32)).ap()
                    c["dbgbuf"] = P2(nc.sbuf_tensor(f"dbgbuf{s_}", [128, 272], BF16)).ap()
                    c["dbgbuf2"] = P2(nc.sbuf_tensor(f"dbgbuf2{s_}", [128, 1040], F32)).ap()
                    c["sqdump"] = c["Ssb"][:, 0:1024]
                    c["psS"] = [P2(nc.psum_tensor(f"psS{i}_{s_}", [128, 512], F32))
                                for i in range(2)]
                    c["psT"] = [P2(nc.psum_tensor(f"psTq{i}_{s_}", [128, 512], BF16))
                                for i in range(2)]
                    c["psO"] = P2(nc.psum_tensor(f"psO{s_}", [128, 1024], F32))
                    sc = Sched(counts)
                    for e_ in ENGINES:
                        sc.wait_all(e_, prev)
                    phase_qloop(nc, sc, t, c, a, s_)
                    run_block(nc, sc, sems)
                    prev = snap(counts)

        if "B1" in phases:
         with ExitStack() as ph:
            P = ph.enter_context
            c = {}
            c["wgq"] = P(nc.sbuf_tensor("wgq", [128, DC * 4096], BF16)).ap()
            c["wuq"] = P(nc.sbuf_tensor("wuq", [128, DC * 4096], BF16)).ap()
            c["hqT"] = [P(nc.sbuf_tensor(f"hqT{i}", [128, D], BF16)) for i in range(2)]
            c["sg"] = [P(nc.sbuf_tensor(f"sg{i}", [128, 512], BF16)).ap()
                       for i in range(2)]
            c["ybf"] = P(nc.sbuf_tensor("ybf", [128, INNER], BF16)).ap()
            c["yqbf"] = [P(nc.sbuf_tensor(f"yqbf{i}", [128, INNER], BF16)).ap()
                         for i in range(2)]
            c["yqstage"] = P(nc.sbuf_tensor("yqstage", [128, INNER], BF16)).ap()
            c["yf32"] = P(nc.sbuf_tensor("yf32", [128, INNER], F32)).ap()
            c["statsb"] = P(nc.sbuf_tensor("statsb", [128, 8], F32)).ap()
            c["pg"] = [P(nc.psum_tensor(f"pg{i}", [128, 512], F32)) for i in range(2)]
            c["pu"] = [P(nc.psum_tensor(f"pu{i}", [128, 512], F32)) for i in range(2)]
            c["psTb"] = [P(nc.psum_tensor(f"psTb{i}", [128, 512], BF16)) for i in range(2)]
            sc = Sched(counts)
            for e_ in ENGINES:
                sc.wait_all(e_, prev)
            phase_b1(nc, sc, t, c)
            run_block(nc, sc, sems)
            prev = snap(counts)

        if "B2" in phases:
         with ExitStack() as ph:
            P = ph.enter_context
            c = {}
            c["wdq"] = P(nc.sbuf_tensor("wdq", [128, IC * 1024], BF16)).ap()
            c["yqT"] = [P(nc.sbuf_tensor(f"yqT{i}", [128, INNER], BF16)) for i in range(3)]
            c["hres"] = [P(nc.sbuf_tensor(f"hres{i}", [128, D], F32)) for i in range(3)]
            c["ostage"] = [P(nc.sbuf_tensor(f"ost{i}", [128, D], F32)) for i in range(3)]
            c["r2"] = [P(nc.sbuf_tensor(f"r2_{i}", [128, D], F32)).ap() for i in range(2)]
            c["sqd2"] = P(nc.sbuf_tensor("sqd2", [128, D], F32)).ap()
            c["statsd"] = P(nc.sbuf_tensor("statsd", [128, 8], F32)).ap()
            c["pd"] = [P(nc.psum_tensor(f"pd{i}", [128, 1024], F32)) for i in range(3)]
            sc = Sched(counts)
            for e_ in ENGINES:
                sc.wait_all(e_, prev)
            phase_b2(nc, sc, t, c)
            run_block(nc, sc, sems)
    return nc


def make_consts():
    ident = np.eye(128, dtype=ml_dtypes.bfloat16)
    r = np.arange(128)
    mask = np.where(r[None, :] > r[:, None], np.float32(MASKV),
                    np.float32(0.0)).astype(np.float32)
    return ident, mask


def kernel(L0_attn_x, L0_attn_Wq, L0_attn_bq, L0_attn_Wk, L0_attn_bk, L0_attn_Wv,
           L0_attn_bv, L0_ffn_w_gate, L0_ffn_w_up, L0_ffn_w_down):
    x = np.ascontiguousarray(np.asarray(L0_attn_x, dtype=np.float32))
    wqT = np.ascontiguousarray(np.asarray(L0_attn_Wq, dtype=np.float32).T)
    wkT = np.ascontiguousarray(np.asarray(L0_attn_Wk, dtype=np.float32).T)
    wvT = np.ascontiguousarray(np.asarray(L0_attn_Wv, dtype=np.float32).T)
    wgT = np.ascontiguousarray(np.asarray(L0_ffn_w_gate, dtype=np.float32).T)
    wuT = np.ascontiguousarray(np.asarray(L0_ffn_w_up, dtype=np.float32).T)
    wdT = np.ascontiguousarray(np.asarray(L0_ffn_w_down, dtype=np.float32).T)
    brows = np.concatenate([np.asarray(L0_attn_bq, dtype=np.float32),
                            np.asarray(L0_attn_bk, dtype=np.float32),
                            np.asarray(L0_attn_bv, dtype=np.float32)]).reshape(1, 3 * D)
    ident, mask = make_consts()
    onesf = np.ones((128, 128), dtype=np.float32)
    onesb = np.ones((1, 512), dtype=ml_dtypes.bfloat16)

    nc = build_nc()
    in_maps = []
    for cid in range(NCORES):
        xs = np.ascontiguousarray(x[SEQ * cid:SEQ * (cid + 1)].reshape(SEQ * S, D))
        in_maps.append({"x_in": xs, "wqT": wqT, "wkT": wkT, "wvT": wvT,
                        "wgT": wgT, "wuT": wuT, "wdT": wdT, "brows": brows,
                        "ident_in": ident, "mask_in": mask,
                        "onesf_in": onesf, "onesb_in": onesb})
    res = run_bass_kernel_spmd(nc, in_maps, core_ids=list(range(NCORES)))
    out = np.concatenate([r["out"].reshape(SEQ, S, D) for r in res.results], axis=0)
    return out


# version 07
# speedup vs baseline: 1.1702x; 1.1702x over previous
import numpy as np
import ml_dtypes
from contextlib import ExitStack

import concourse.bass as bass
import concourse.mybir as mybir
from concourse.bass_utils import run_bass_kernel_spmd

F32 = mybir.dt.float32
BF16 = mybir.dt.bfloat16
I8 = mybir.dt.int8
AF = mybir.ActivationFunctionType
ALU = mybir.AluOpType
AX = mybir.AxisListType

NCORES = 8
SEQ = 2
S = 4096
D = 1024
INNER = 4096
TC = S // 128
GC = SEQ * TC
DC = D // 128
IC = INNER // 128
MASKV = -2500.0
M32 = 12582912.0
M64 = 6755399441055744.0

ENGINES = ("sync", "gpsimd", "scalar", "vector", "tensor")
ESEM = {"sync": "LX", "gpsimd": "LG", "scalar": "SC", "vector": "V", "tensor": "T"}
def _sem_step(name):
    return 16 if name.startswith("L") else 1


class Sched:

    def __init__(self, counts):
        self.ops = {e: [] for e in ENGINES}
        self.n = counts
        self._last_selfwait = {}

    def op(self, eng, builder, inc=None):
        step = _sem_step(inc) if inc else 0
        if inc:
            self.n[inc] = self.n.get(inc, 0) + step
        self.ops[eng].append([builder, inc, step])
        return self.n.get(inc, 0)

    def wait(self, eng, sem, val):
        if val and val > 0:
            self.ops[eng].append([("W", sem, val), None, 0])

    def wait_all(self, eng, snapshot):
        for s_, v in snapshot.items():
            self.wait(eng, s_, v)

    def count(self, sem):
        return self.n.get(sem, 0)

    def close(self):
        for e in ENGINES:
            for item in reversed(self.ops[e]):
                if not isinstance(item[0], tuple):
                    if item[1] is None:
                        sem = ESEM[e]
                        item[1] = sem
                        item[2] = _sem_step(sem)
                        self.n[sem] = self.n.get(sem, 0) + _sem_step(sem)
                    break


def run_block(nc, sched, sems):
    sched.close()
    with nc.Block() as blk:
        for ename in ENGINES:
            def body(e, _ename=ename):
                for b, inc, step in sched.ops[_ename]:
                    if isinstance(b, tuple):
                        e.wait_ge(sems[b[1]], b[2])
                    else:
                        ins = b(e)
                        if inc is not None:
                            ins.then_inc(sems[inc], step)
            getattr(blk, ename)(body)


def snap(counts):
    return dict(counts)



def phase_quant(nc, sc, t, c):
    mats = [(t["wgT"], t["wgq_s"], 0, 128), (t["wuT"], t["wuq_s"], 1, 128),
            (t["wdT"], t["wdq_s"], 2, 512)]
    stg = c["q_stg"]
    qbf = c["q_bf"]
    macc = c["q_macc"]
    tmp = c["q_tmp"]
    s11 = c["q_s11"]
    onec = c["ones_col"]
    oner = c["ones_rowf"]
    p11 = c["q_p11"]
    pbc = c["q_pbc"]
    pbc2 = c["q_pbc2"]
    wsc, wmean = t["wsc"], t["wmean"]

    vload = {}
    gstore = {}
    for dram, scratch, m, nr in mats:
        isp = nr // 128
        for k in range(8):
            par = k % 2
            if par in vload:
                sc.wait("sync", "V", vload[par])
            lv = sc.op("sync", (lambda dram=dram, k=k, par=par, nr=nr, isp=isp:
                                lambda e: e.dma_start(
                                    out=stg[par][:].rearrange("p (i o) -> p i o", i=isp),
                                    in_=dram[k * nr:(k + 1) * nr, :]
                                    .rearrange("(i p) o -> p i o", p=128)))(),
                       inc="Lstg%d" % par)
            sc.wait("vector", "Lstg%d" % par, lv)
            if k == 0:
                vload[par] = sc.op("vector",
                                   (lambda par=par, m=m:
                                    lambda e: e.tensor_reduce(out=macc[:, m:m + 1],
                                                              in_=stg[par][:], axis=AX.X,
                                                              op=ALU.add,
                                                              apply_absolute_value=True))(),
                                   inc="V")
            else:
                rv = sc.op("vector", (lambda par=par, m=m:
                                      lambda e: e.tensor_reduce(out=tmp[:, m:m + 1],
                                                                in_=stg[par][:],
                                                                axis=AX.X, op=ALU.add,
                                                                apply_absolute_value=True))(),
                           inc="V")
                sc.wait("vector", "V", rv)
                vload[par] = sc.op("vector", (lambda m=m:
                                              lambda e: e.tensor_add(out=macc[:, m:m + 1],
                                                                     in0=macc[:, m:m + 1],
                                                                     in1=tmp[:, m:m + 1]))(),
                                   inc="V")
        sc.wait("tensor", "V", sc.count("V"))
        tt = sc.op("tensor", (lambda m=m: lambda e: e.matmul(p11[:], onec[:],
                                                             macc[:, m:m + 1],
                                                             start=True, stop=True))(), inc="T")
        sc.wait("vector", "T", tt)
        nel = float(8 * 128 * 4096)
        fv = sc.op("vector", (lambda m=m:
                              lambda e: e.tensor_scalar(out=s11[:, m:m + 1], in0=p11[:],
                                                        scalar1=1.0 / nel, scalar2=1e-5,
                                                        op0=ALU.mult, op1=ALU.max))(),
                   inc="V")
        sc.wait("vector", "V", fv)
        v = sc.op("vector", (lambda m=m:
                             lambda e: e.reciprocal(out=s11[:, 3:4], in_=s11[:, m:m + 1]))(),
                  inc="V")
        sc.wait("tensor", "V", v)
        sc.op("tensor", (lambda m=m: lambda e: e.matmul(pbc[:], oner[:], s11[:, m:m + 1],
                                                        start=True, stop=True))())
        tt = sc.op("tensor", lambda e: e.matmul(pbc2[:], oner[:], s11[:, 3:4],
                                                start=True, stop=True), inc="T")
        sc.wait("vector", "T", tt)
        sc.op("vector", (lambda m=m: lambda e: e.tensor_copy(out=wmean[:, m:m + 1],
                                                             in_=pbc[:]))())
        sc.op("vector", (lambda m=m: lambda e: e.tensor_copy(out=wsc[:, m:m + 1],
                                                             in_=pbc2[:]))(), inc="V")
        for k in range(8):
            par = k % 2
            sc.wait("sync", "V", vload[par])
            lv = sc.op("sync", (lambda dram=dram, k=k, par=par, nr=nr, isp=isp:
                                lambda e: e.dma_start(
                                    out=stg[par][:].rearrange("p (i o) -> p i o", i=isp),
                                    in_=dram[k * nr:(k + 1) * nr, :]
                                    .rearrange("(i p) o -> p i o", p=128)))(),
                       inc="Lstg%d" % par)
            sc.wait("vector", "Lstg%d" % par, lv)
            if par in gstore:
                sc.wait("vector", "Lqbf%d" % par, gstore[par])
            sc.op("vector", (lambda par=par, m=m:
                             lambda e: e.tensor_scalar(out=stg[par][:], in0=stg[par][:],
                                                       scalar1=wsc[:, m:m + 1],
                                                       scalar2=None, op0=ALU.mult))())
            sc.op("vector", (lambda par=par:
                             lambda e: e.tensor_scalar(out=stg[par][:], in0=stg[par][:],
                                                       scalar1=t["mgk"][:, 0:1],
                                                       scalar2=None, op0=ALU.add))())
            sc.op("vector", (lambda par=par:
                             lambda e: e.tensor_scalar(out=stg[par][:], in0=stg[par][:],
                                                       scalar1=t["mgk"][:, 1:2], scalar2=1.0,
                                                       op0=ALU.add, op1=ALU.min))())
            vload[par] = sc.op("vector", (lambda par=par:
                                          lambda e: e.tensor_scalar(out=qbf[par][:],
                                                                    in0=stg[par][:],
                                                                    scalar1=-1.0,
                                                                    scalar2=None,
                                                                    op0=ALU.max))(),
                               inc="V")
            sc.wait("gpsimd", "V", vload[par])
            gstore[par] = sc.op("gpsimd",
                                (lambda scratch=scratch, k=k, par=par, nr=nr, isp=isp:
                                 lambda e: e.dma_start(
                                     out=scratch[k * nr:(k + 1) * nr, :]
                                     .rearrange("(i p) o -> p i o", p=128),
                                     in_=qbf[par][:].rearrange("p (i o) -> p i o",
                                                               i=isp)))(), inc="Lqbf%d" % par)



def phase_pass1(nc, sc, t, c, a, s):
    KT, Vr, W = a["KT"], a["V"], a["W"]
    xb, xT = c["xb"], c["xT"]
    qstage = c["qstage"]
    ptx, psA, psV = c["ptx"], c["psA"], c["psV"]
    brows, ones5 = a["brows"], a["ones512"]
    NG = 16

    xb_free, evac_done, mm_done = {}, {}, {}
    qev_done, qst_done, vev_done = {}, {}, {}

    def emit_transposes(g):
        for ts in range(2):
            tc_ = 2 * g + ts
            r0 = (s * TC + tc_) * 128
            par = tc_ % 2
            if tc_ >= 2:
                sc.wait("gpsimd", "T", xb_free[tc_ - 2])
            lv = sc.op("gpsimd", (lambda r0=r0, par=par:
                                  lambda e: e.dma_start(out=xb[par][:],
                                                        in_=t["x_in"][r0:r0 + 128, :]))(),
                       inc="Lxb%d" % par)
            sc.wait("tensor", "Lxb%d" % par, lv)
            if tc_ >= 2:
                sc.wait("tensor", "V", evac_done[tc_ - 2])
            for dcs in range(DC):
                sc.op("tensor", (lambda par=par, dcs=dcs:
                                 lambda e: e.transpose(ptx[par][:, dcs * 128:(dcs + 1) * 128],
                                                       xb[par][:, dcs * 128:(dcs + 1) * 128],
                                                       a["ident"]))(),
                      inc="T" if dcs == DC - 1 else None)
            xb_free[tc_] = sc.count("T")
            sc.wait("vector", "T", sc.count("T"))
            if g >= 2:
                sc.wait("vector", "T", mm_done[g - 2])
            dst = xT[g % 2].rearrange("p (dc two u) -> p dc two u", dc=DC, two=2)
            src = ptx[par].ap().rearrange("p (dc u) -> p dc u", dc=DC)
            evac_done[tc_] = sc.op("vector", (lambda dst=dst, src=src, ts=ts:
                                              lambda e: e.tensor_copy(out=dst[:, :, ts, :],
                                                                      in_=src))(), inc="V")

    def emit_group_mm(g):
        sc.wait("tensor", "V", evac_done[2 * g + 1])
        for proj in (0, 1):
            for oc in range(DC):
                idx = g * 2 * DC + proj * DC + oc
                rot = idx % 2
                sc.wait("tensor", "V", qev_done.get(idx - 2, 0))
                for dcs in range(DC):
                    sc.op("tensor", (lambda rot=rot, proj=proj, oc=oc, dcs=dcs, g=g:
                                     lambda e: e.matmul(psA[rot].ap()[:, 0:256],
                                                        W[:, proj * 8192 + dcs * 1024 + oc * 128:
                                                          proj * 8192 + dcs * 1024 + oc * 128 + 128],
                                                        xT[g % 2][:, dcs * 256:(dcs + 1) * 256],
                                                        start=(dcs == 0), stop=False))())
                tt = sc.op("tensor", (lambda rot=rot, proj=proj, oc=oc:
                                      lambda e: e.matmul(psA[rot].ap()[:, 0:256],
                                                         brows[0:1, proj * 1024 + oc * 128:
                                                               proj * 1024 + oc * 128 + 128],
                                                         ones5[0:1, 0:256],
                                                         start=False, stop=True))(), inc="T")
                sc.wait("vector", "T", tt)
                if proj == 1:
                    qev_done[idx] = sc.op("vector",
                                          (lambda rot=rot, oc=oc, g=g:
                                           lambda e: e.tensor_copy(
                                               out=KT[:, oc * 4096 + g * 256:
                                                      oc * 4096 + g * 256 + 256],
                                               in_=psA[rot].ap()[:, 0:256]))(), inc="V")
                else:
                    if oc == 0 and g >= 1:
                        sc.wait("vector", "Lqst", qst_done[g - 1])
                    dstq = qstage.rearrange("p (two oc u) -> p two oc u", two=2, oc=DC)
                    qev_done[idx] = sc.op("vector",
                                          (lambda rot=rot, oc=oc, dstq=dstq:
                                           lambda e: e.tensor_copy(
                                               out=dstq[:, :, oc, :],
                                               in_=psA[rot].ap()[:, 0:256].rearrange(
                                                   "p (two u) -> p two u", two=2)))(),
                                          inc="V")
        sc.wait("gpsimd", "V", qev_done[g * 2 * DC + DC - 1])
        r0 = (s * TC + 2 * g) * 128
        qst_done[g] = sc.op("gpsimd", (lambda r0=r0:
                                       lambda e: e.dma_start(
                                           out=t["qt_s"][r0:r0 + 256, :]
                                           .rearrange("(i p) o -> p i o", p=128),
                                           in_=qstage.rearrange("p (i o) -> p i o",
                                                               i=2)))(), inc="Lqst")
        for ts in range(2):
            vidx = g * 2 + ts
            rot = vidx % 2
            tc_ = 2 * g + ts
            sc.wait("tensor", "V", vev_done.get(vidx - 2, 0))
            for j in range(2):
                for dcs in range(DC):
                    sc.op("tensor", (lambda rot=rot, ts=ts, j=j, dcs=dcs, g=g:
                                     lambda e: e.matmul(psV[rot].ap()[:, j * 512:(j + 1) * 512],
                                                        xT[g % 2][:, dcs * 256 + ts * 128:
                                                                   dcs * 256 + ts * 128 + 128],
                                                        W[:, 2 * 8192 + dcs * 1024 + j * 512:
                                                          2 * 8192 + dcs * 1024 + j * 512 + 512],
                                                        start=(dcs == 0), stop=False))())
                tt = sc.op("tensor", (lambda rot=rot, j=j:
                                      lambda e: e.matmul(psV[rot].ap()[:, j * 512:(j + 1) * 512],
                                                         ones5[0:1, 0:128],
                                                         brows[0:1, 2048 + j * 512:
                                                               2048 + (j + 1) * 512],
                                                         start=False, stop=True))(),
                           inc="T" if j == 1 else None)
            sc.wait("vector", "T", tt)
            vev_done[vidx] = sc.op("vector",
                                   (lambda rot=rot, tc_=tc_:
                                    lambda e: e.tensor_copy(
                                        out=Vr[:, tc_ * 1024:(tc_ + 1) * 1024],
                                        in_=psV[rot].ap()))(), inc="V")
        mm_done[g] = sc.count("T")

    for g in range(NG):
        emit_transposes(g)
        if g >= 1:
            emit_group_mm(g - 1)
    emit_group_mm(NG - 1)



def phase_qloop(nc, sc, t, c, a, s):
    KT, Vr = a["KT"], a["V"]
    qtile, xq = c["qtile"], c["xq"]
    Ssb, PTs = c["Ssb"], c["PTs"]
    Psb = c["Psb"]
    psS, psT, psO = c["psS"], c["psT"], c["psO"]
    st = c["stats"]
    hst, hqst = c["hstage"], c["hqstage"]
    hqb = c["hqb"]
    sqd, rbuf = c["sqdump"], c["rbuf"]
    mask = a["maskf"]

    score_done = {}
    epi_done = {}
    hstore, hqstore = {}, {}
    last_psS = {}
    last_psT = {}
    last_PTs = {}
    ex_prev = {}
    ssb_rd_T = {0: 0}

    for qt in range(TC):
        gc = s * TC + qt
        r0 = gc * 128
        nkc = qt + 1
        ngrp = (nkc + 3) // 4
        W_ = nkc * 128
        par = qt % 2
        if qt >= 2:
            sc.wait("sync", "T", score_done[qt - 2])
            sc.wait("sync", "V", epi_done.get(qt - 2, 0))
        qld = sc.op("sync", (lambda r0=r0, par=par:
                             lambda e: e.dma_start(out=qtile[par][:],
                                                   in_=t["qt_s"][r0:r0 + 128, :]))(), inc="Lqt%d" % par)
        xld = sc.op("sync", (lambda r0=r0, par=par:
                             lambda e: e.dma_start(out=xq[par][:],
                                                   in_=t["x_in"][r0:r0 + 128, :]))(), inc="Lxq%d" % par)
        sc.wait("tensor", "Lqt%d" % par, qld)
        ev_grp = {}
        for j in range(ngrp):
            w = min(512, (nkc - 4 * j) * 128)
            rot = j % 2
            sc.wait("tensor", "SC", last_psS.get(rot, 0))
            for dcs in range(DC):
                tt = sc.op("tensor", (lambda rot=rot, j=j, w=w, dcs=dcs, par=par:
                                      lambda e: e.matmul(psS[rot].ap()[:, 0:w],
                                                         qtile[par][:, dcs * 128:(dcs + 1) * 128],
                                                         KT[:, dcs * 4096 + j * 512:
                                                            dcs * 4096 + j * 512 + w],
                                                         start=(dcs == 0),
                                                         stop=(dcs == DC - 1)))(),
                           inc="T" if dcs == DC - 1 else None)
            sc.wait("vector", "T", tt)
            if j == ngrp - 1:
                sc.op("vector", (lambda rot=rot, w=w:
                                 lambda e: e.tensor_add(out=psS[rot].ap()[:, w - 128:w],
                                                        in0=psS[rot].ap()[:, w - 128:w],
                                                        in1=mask))())
            rv = sc.op("vector", (lambda rot=rot, w=w:
                                  lambda e: e.tensor_reduce(out=st[:, 0:1],
                                                            in_=psS[rot].ap()[:, 0:w],
                                                            axis=AX.X, op=ALU.max))(),
                       inc="V")
            sc.wait("vector", "V", rv)
            if j == 0:
                mv = sc.op("vector", lambda e: e.tensor_copy(out=st[:, 1:2], in_=st[:, 0:1]),
                           inc="V")
            else:
                mv = sc.op("vector", lambda e: e.tensor_max(out=st[:, 1:2], in0=st[:, 1:2],
                                                            in1=st[:, 0:1]), inc="V")
            sc.wait("scalar", "V", mv)
            if j <= 1:
                sc.wait("scalar", "T", ssb_rd_T.get(qt - 1, 0))
            ev_grp[j] = sc.op("scalar", (lambda rot=rot, j=j, w=w:
                                         lambda e: e.activation(out=Ssb[:, j * 512:j * 512 + w],
                                                                in_=psS[rot].ap()[:, 0:w],
                                                                func=AF.Copy,
                                                                scale=1.0 / 32.0))(), inc="SC")
            last_psS[rot] = ev_grp[j]
        score_done[qt] = sc.count("T")
        sc.wait("vector", "SC", ex_prev.get(qt - 1, 0))
        sc.wait("vector", "V", mv)
        bv = sc.op("vector", lambda e: e.tensor_scalar(out=st[:, 2:3], in0=st[:, 1:2],
                                                       scalar1=-1.0 / 32.0, scalar2=None,
                                                       op0=ALU.mult), inc="V")
        sc.wait("scalar", "V", bv)
        ex = sc.op("scalar", (lambda W_=W_:
                              lambda e: e.activation(out=Psb[:, 0:W_], in_=Ssb[:, 0:W_],
                                                     func=AF.Exp, bias=st[:, 2:3], scale=1.0,
                                                     accum_out=st[:, 3:4]))(), inc="SC")
        ex_prev[qt] = ex
        sc.wait("vector", "SC", ex)
        lv = sc.op("vector", lambda e: e.reciprocal(out=st[:, 4:5], in_=st[:, 3:4]), inc="V")
        sc.wait("vector", "V", lv)
        if t.get("dbg_p") is not None and s == 0 and qt == 0:
            sc.op("vector", lambda e: e.tensor_copy(out=c["dbgbuf"][:, 0:128],
                                                    in_=Ssb[:, 0:128]))
            sc.op("vector", lambda e: e.tensor_copy(out=c["dbgbuf"][:, 128:256],
                                                    in_=Psb[:, 0:128]))
            dv = sc.op("vector", lambda e: e.tensor_copy(out=c["dbgbuf"][:, 256:272],
                                                         in_=st[:, 0:16]), inc="V")
            sc.wait("gpsimd", "V", dv)
            sc.op("gpsimd", lambda e: e.dma_start(out=t["dbg_p"][0:128, 0:272],
                                                  in_=c["dbgbuf"][:, 0:272]), inc="LG")
            dbg_qt0 = True
        else:
            dbg_qt0 = False
        nb = (nkc + 3) // 4
        pt_ev = {}
        sc.wait("tensor", "SC", ex)
        sc.wait("tensor", "V", epi_done.get(qt - 1, 0))

        def emit_pt(b, qt=qt):
            rot = b % 2
            sc.wait("tensor", "V", last_psT.get(rot, 0))
            nch = min(4, nkc - 4 * b)
            for u in range(nch):
                cidx = 4 * b + u
                tt = sc.op("tensor", (lambda rot=rot, u=u, cidx=cidx:
                                      lambda e: e.transpose(psT[rot].ap()[:, u * 128:(u + 1) * 128],
                                                            Psb[:, cidx * 128:(cidx + 1) * 128],
                                                            a["ident"]))(),
                           inc="T" if u == nch - 1 else None)
            ssb_rd_T[qt] = tt
            sc.wait("vector", "T", tt)
            sc.wait("vector", "T", last_PTs.get(rot, 0))
            pt_ev[b] = sc.op("vector", (lambda rot=rot, nch=nch:
                                        lambda e: e.tensor_copy(
                                            out=PTs[:, rot * 512:rot * 512 + nch * 128],
                                            in_=psT[rot].ap()[:, 0:nch * 128]))(), inc="V")
            last_psT[rot] = pt_ev[b]

        def emit_pv(b, final):
            rot = b % 2
            sc.wait("tensor", "V", pt_ev[b])
            nch = min(4, nkc - 4 * b)
            tt = None
            for u in range(nch):
                cidx = 4 * b + u
                for j in range(2):
                    batch_last = u == nch - 1 and j == 1
                    tt = sc.op("tensor",
                               (lambda rot=rot, u=u, cidx=cidx, j=j, final=final:
                                lambda e: e.matmul(psO.ap()[:, j * 512:(j + 1) * 512],
                                                   PTs[:, rot * 512 + u * 128:
                                                       rot * 512 + (u + 1) * 128],
                                                   Vr[:, cidx * 1024 + j * 512:
                                                      cidx * 1024 + j * 512 + 512],
                                                   start=(cidx == 0),
                                                   stop=(final and u == nch - 1)))(),
                               inc="T" if batch_last else None)
            last_PTs[rot] = tt
            return tt

        for b in range(nb):
            emit_pt(b)
            if b >= 1:
                emit_pv(b - 1, final=False)
        pv_done = emit_pv(nb - 1, final=True)
        sc.wait("vector", "T", pv_done)
        sc.wait("vector", "Lxq%d" % par, xld)
        ep = sc.op("vector", (lambda par=par:
                              lambda e: e.scalar_tensor_tensor(out=rbuf, in0=psO.ap(),
                                                               scalar=st[:, 4:5], in1=xq[par][:],
                                                               op0=ALU.mult, op1=ALU.add,
                                                               accum_out=st[:, 5:6]))(), inc="V")
        epi_done[qt] = ep
        sc.wait("vector", "V", ep)
        if t.get("dbg_p") is not None and s == 0 and qt == 0:
            sc.op("vector", lambda e: e.tensor_copy(out=c["dbgbuf2"][:, 0:1024],
                                                    in_=rbuf))
        fv = sc.op("vector", lambda e: e.tensor_scalar(out=st[:, 6:7], in0=st[:, 5:6],
                                                       scalar1=1.0 / 1024.0, scalar2=None,
                                                       op0=ALU.mult), inc="V")
        sc.wait("vector", "V", fv)
        mv = sc.op("vector", lambda e: e.tensor_mul(out=st[:, 7:8], in0=st[:, 6:7],
                                                    in1=st[:, 6:7]), inc="V")
        sc.wait("scalar", "V", mv)
        sq = sc.op("scalar", lambda e: e.activation(out=sqd, in_=rbuf, func=AF.Square,
                                                    accum_out=st[:, 8:9]), inc="SC")
        sc.wait("vector", "SC", sq)
        fv = sc.op("vector", lambda e: e.tensor_scalar(out=st[:, 9:10], in0=st[:, 8:9],
                                                       scalar1=1.0 / 1024.0, scalar2=1e-5,
                                                       op0=ALU.mult, op1=ALU.add), inc="V")
        sc.wait("vector", "V", fv)
        vv = sc.op("vector", lambda e: e.tensor_sub(out=st[:, 10:11], in0=st[:, 9:10],
                                                    in1=st[:, 7:8]), inc="V")
        sc.wait("scalar", "V", vv)
        sc.op("scalar", lambda e: e.activation(out=st[:, 11:12], in_=st[:, 10:11],
                                               func=AF.Ln, scale=1.0))
        rs = sc.op("scalar", lambda e: e.activation(out=st[:, 12:13], in_=st[:, 11:12],
                                                    func=AF.Exp, scale=-0.5), inc="SC")
        sc.wait("vector", "SC", rs)
        nm = sc.op("vector", lambda e: e.scalar_tensor_tensor(out=st[:, 13:14], in0=st[:, 6:7],
                                                              scalar=-1.0, in1=st[:, 12:13],
                                                              op0=ALU.mult, op1=ALU.mult),
                   inc="V")
        sc.wait("scalar", "V", nm)
        if t.get("dbg_p") is not None and s == 0 and qt == 0:
            dv2 = sc.op("vector", lambda e: e.tensor_copy(out=c["dbgbuf2"][:, 1024:1040],
                                                          in_=st[:, 0:16]), inc="V")
            sc.wait("gpsimd", "V", dv2)
            sc.op("gpsimd", lambda e: e.dma_start(out=t["dbg_s"][0:128, 0:1040].cast(F32)
                                                  if False else t["dbg_s2"][0:128, :],
                                                  in_=c["dbgbuf2"][:, 0:1040]), inc="LG")
        if qt >= 2:
            sc.wait("scalar", "Lh%d" % par, hstore[qt - 2])
        hv = sc.op("scalar", (lambda par=par:
                              lambda e: e.activation(out=hst[par][:], in_=rbuf, func=AF.Identity,
                                                     scale=st[:, 12:13], bias=st[:, 13:14]))(),
                   inc="SC")
        sc.wait("gpsimd", "SC", hv)
        hstore[qt] = sc.op("gpsimd", (lambda r0=r0, par=par:
                                      lambda e: e.dma_start(out=t["h_s"][r0:r0 + 128, :],
                                                            in_=hst[par][:]))(), inc="Lh%d" % par)
        sc.wait("vector", "SC", hv)
        rv = sc.op("vector", (lambda par=par:
                              lambda e: e.tensor_reduce(out=st[:, 14:15], in_=hst[par][:],
                                                        axis=AX.X, op=ALU.max,
                                                        apply_absolute_value=True))(), inc="V")
        sc.wait("vector", "V", rv)
        fv = sc.op("vector", lambda e: e.tensor_scalar(out=st[:, 14:15], in0=st[:, 14:15],
                                                       scalar1=1e-5, scalar2=None,
                                                       op0=ALU.max), inc="V")
        sc.wait("vector", "V", fv)
        sc.op("vector", (lambda gc=gc:
                         lambda e: e.tensor_scalar(out=t["rs_h"][:, gc:gc + 1],
                                                   in0=st[:, 14:15], scalar1=1.0 / 127.0,
                                                   scalar2=None, op0=ALU.mult))())
        rv = sc.op("vector", lambda e: e.reciprocal(out=st[:, 15:16], in_=st[:, 14:15]),
                   inc="V")
        sc.wait("vector", "V", rv)
        fv = sc.op("vector", lambda e: e.tensor_scalar(out=st[:, 15:16], in0=st[:, 15:16],
                                                       scalar1=127.0, scalar2=None,
                                                       op0=ALU.mult), inc="V")
        sc.wait("vector", "V", fv)
        sc.op("vector", (lambda par=par:
                         lambda e: e.tensor_scalar(out=rbuf, in0=hst[par][:],
                                                   scalar1=st[:, 15:16], scalar2=None,
                                                   op0=ALU.mult))())
        sc.op("vector", lambda e: e.tensor_scalar(out=rbuf, in0=rbuf,
                                                  scalar1=t["mgk"][:, 0:1], scalar2=None,
                                                  op0=ALU.add))
        sc.op("vector", lambda e: e.tensor_scalar(out=rbuf, in0=rbuf,
                                                  scalar1=t["mgk"][:, 1:2],
                                                  scalar2=127.0, op0=ALU.add, op1=ALU.min))
        hb = sc.op("vector", lambda e: e.tensor_scalar(out=hqb, in0=rbuf, scalar1=-128.0,
                                                       scalar2=None, op0=ALU.max), inc="V")
        sc.wait("tensor", "V", hb)
        for b in range(2):
            rot = b % 2
            sc.wait("tensor", "V", last_psT.get(rot, 0))
            for u in range(4):
                dcs = 4 * b + u
                tt = sc.op("tensor", (lambda rot=rot, dcs=dcs, u=u:
                                      lambda e: e.transpose(psT[rot].ap()[:, u * 128:(u + 1) * 128],
                                                            hqb[:, dcs * 128:(dcs + 1) * 128],
                                                            a["ident"]))(),
                           inc="T" if u == 3 else None)
            sc.wait("vector", "T", tt)
            if b == 0 and qt >= 2:
                sc.wait("vector", "Lhq%d" % par, hqstore[qt - 2])
            hev = sc.op("vector", (lambda rot=rot, b=b, par=par:
                                   lambda e: e.tensor_copy(
                                       out=hqst[par][:, b * 512:(b + 1) * 512],
                                       in_=psT[rot].ap()))(), inc="V")
            last_psT[rot] = hev
        sc.wait("gpsimd", "V", hev)
        hqstore[qt] = sc.op("gpsimd", (lambda r0=r0, par=par:
                                       lambda e: e.dma_start(out=t["hq_s"][r0:r0 + 128, :],
                                                             in_=hqst[par][:]))(), inc="Lhq%d" % par)
        if t.get("dbg_st") is not None:
            sc.op("gpsimd", (lambda r0=r0:
                             lambda e: e.dma_start(out=t["dbg_st"][r0:r0 + 128, :],
                                                   in_=st[:, 0:16]))(), inc="LG")
            if s == 0 and qt == 20:
                sc.op("gpsimd", lambda e: e.dma_start(out=t["dbg_p"], in_=Psb), inc="LG")
                sc.op("gpsimd", lambda e: e.dma_start(out=t["dbg_s"], in_=Ssb), inc="LG")



def phase_b1(nc, sc, t, c):
    wgq, wuq = c["wgq"], c["wuq"]
    hqT, sg = c["hqT"], c["sg"]
    ybf, yqb = c["ybf"], c["yqstage"]
    yqbf = c["yqbf"]
    pg, pu, psT = c["pg"], c["pu"], c["psTb"]
    stb = c["statsb"]

    for dcs in range(DC):
        sc.op("sync", (lambda dcs=dcs:
                       lambda e: e.dma_start(out=wgq[:, dcs * 4096:(dcs + 1) * 4096],
                                             in_=t["wgq_s"][dcs * 128:(dcs + 1) * 128, :]))(),
              inc="LWB1")
        sc.op("sync", (lambda dcs=dcs:
                       lambda e: e.dma_start(out=wuq[:, dcs * 4096:(dcs + 1) * 4096],
                                             in_=t["wuq_s"][dcs * 128:(dcs + 1) * 128, :]))(),
              inc="LWB1")
    sc.wait("tensor", "LWB1", sc.count("LWB1"))

    b1mm = {}
    ymul = {}
    yb_done = {}
    yst = {}
    last_psT = {}

    def emit_windows(gc):
        r0 = gc * 128
        par = gc % 2
        if gc >= 2:
            sc.wait("sync", "T", b1mm[gc - 2])
        hld = sc.op("sync", (lambda r0=r0, par=par:
                             lambda e: e.dma_start(out=hqT[par][:],
                                                   in_=t["hq_s"][r0:r0 + 128, :]))(),
                    inc="Lhqt%d" % par)
        sc.wait("tensor", "Lhqt%d" % par, hld)
        sc.op("vector", (lambda gc=gc:
                         lambda e: e.tensor_mul(out=stb[:, 0:1], in0=t["rs_h"][:, gc:gc + 1],
                                                in1=t["wmean"][:, 0:1]))())
        cgv = sc.op("vector", (lambda gc=gc:
                               lambda e: e.tensor_mul(out=stb[:, 1:2],
                                                      in0=t["rs_h"][:, gc:gc + 1],
                                                      in1=t["wmean"][:, 1:2]))(), inc="V")
        sc.wait("vector", "V", cgv)
        for win in range(8):
            wi = gc * 8 + win
            rot = win % 3
            sc.wait("tensor", "V", ymul.get(wi - 3, 0))
            tt = None
            for wq_, ps_ in ((wgq, pg), (wuq, pu)):
                for dcs in range(DC):
                    last = dcs == DC - 1 and ps_ is pu
                    tt = sc.op("tensor",
                               (lambda wq_=wq_, ps_=ps_, rot=rot, win=win, dcs=dcs, par=par:
                                lambda e: e.matmul(ps_[rot].ap(),
                                                   hqT[par][:, dcs * 128:(dcs + 1) * 128],
                                                   wq_[:, dcs * 4096 + win * 512:
                                                       dcs * 4096 + win * 512 + 512],
                                                   start=(dcs == 0), stop=(dcs == DC - 1)))(),
                               inc="T" if last else None)
            sc.wait("scalar", "T", tt)
            if win == 0:
                sc.wait("scalar", "V", cgv)
            sl = sc.op("scalar", (lambda rot=rot:
                                  lambda e: e.activation(out=sg[rot], in_=pg[rot].ap(),
                                                         func=AF.Silu, scale=stb[:, 0:1]))(),
                       inc="SC")
            sc.wait("vector", "SC", sl)
            ymul[wi] = sc.op("vector",
                             (lambda rot=rot, win=win:
                              lambda e: e.scalar_tensor_tensor(
                                  out=ybf[:, win * 512:(win + 1) * 512],
                                  in0=pu[rot].ap(), scalar=stb[:, 1:2], in1=sg[rot],
                                  op0=ALU.mult, op1=ALU.mult))(), inc="V")
        b1mm[gc] = sc.count("T")
        rv = sc.op("vector", lambda e: e.tensor_reduce(out=stb[:, 2:3], in_=ybf, axis=AX.X,
                                                       op=ALU.max, apply_absolute_value=True),
                   inc="V")
        sc.wait("vector", "V", rv)
        fv = sc.op("vector", lambda e: e.tensor_scalar(out=stb[:, 2:3], in0=stb[:, 2:3],
                                                       scalar1=1e-5, scalar2=None,
                                                       op0=ALU.max), inc="V")
        sc.wait("vector", "V", fv)
        sc.op("vector", (lambda gc=gc:
                         lambda e: e.tensor_scalar(out=t["rs_y"][:, gc:gc + 1],
                                                   in0=stb[:, 2:3], scalar1=1.0 / 127.0,
                                                   scalar2=None, op0=ALU.mult))())
        rv = sc.op("vector", lambda e: e.reciprocal(out=stb[:, 3:4], in_=stb[:, 2:3]),
                   inc="V")
        sc.wait("vector", "V", rv)
        fv = sc.op("vector", lambda e: e.tensor_scalar(out=stb[:, 3:4], in0=stb[:, 3:4],
                                                       scalar1=127.0, scalar2=None,
                                                       op0=ALU.mult), inc="V")
        sc.wait("vector", "V", fv)
        if gc >= 2:
            sc.wait("vector", "T", tq_done.get(gc - 2, 0))
        sc.op("vector", lambda e: e.tensor_scalar(out=c["yf32"], in0=ybf,
                                                  scalar1=stb[:, 3:4], scalar2=None,
                                                  op0=ALU.mult))
        sc.op("vector", lambda e: e.tensor_scalar(out=c["yf32"], in0=c["yf32"],
                                                  scalar1=t["mgk"][:, 0:1], scalar2=None,
                                                  op0=ALU.add))
        sc.op("vector", lambda e: e.tensor_scalar(out=c["yf32"], in0=c["yf32"],
                                                  scalar1=t["mgk"][:, 1:2],
                                                  scalar2=127.0, op0=ALU.add, op1=ALU.min))
        yb_done[gc] = sc.op("vector", (lambda gc=gc:
                                       lambda e: e.tensor_scalar(out=yqbf[gc % 2],
                                                                 in0=c["yf32"],
                                                                 scalar1=-128.0, scalar2=None,
                                                                 op0=ALU.max))(), inc="V")

    tq_done = {}

    def emit_yq_T(gc):
        r0 = gc * 128
        sc.wait("tensor", "V", yb_done[gc])
        for b in range(8):
            rot = b % 2
            sc.wait("tensor", "V", last_psT.get(rot, 0))
            for u in range(4):
                icx = 4 * b + u
                tt = sc.op("tensor", (lambda rot=rot, icx=icx, u=u, gc=gc:
                                      lambda e: e.transpose(psT[rot].ap()[:, u * 128:(u + 1) * 128],
                                                            yqbf[gc % 2][:, icx * 128:(icx + 1) * 128],
                                                            t["identb"]))(),
                           inc="T" if u == 3 else None)
            if b == 7:
                tq_done[gc] = tt
            sc.wait("vector", "T", tt)
            if b == 0 and gc >= 1:
                sc.wait("vector", "Lyq", yst[gc - 1])
            pe = sc.op("vector", (lambda rot=rot, b=b:
                                  lambda e: e.tensor_copy(out=yqb[:, b * 512:(b + 1) * 512],
                                                          in_=psT[rot].ap()))(), inc="V")
            last_psT[rot] = pe
        sc.wait("gpsimd", "V", pe)
        yst[gc] = sc.op("gpsimd", (lambda r0=r0:
                                   lambda e: e.dma_start(out=t["yq_s"][r0:r0 + 128, :],
                                                         in_=yqb))(), inc="Lyq")

    import os as _os
    _gcl = int(_os.environ.get("B1GC", GC))
    for gc in range(_gcl):
        emit_windows(gc)
        if gc >= 1:
            emit_yq_T(gc - 1)
    emit_yq_T(_gcl - 1)



def phase_b2(nc, sc, t, c):
    wdq = c["wdq"]
    yqT, hres, ost = c["yqT"], c["hres"], c["ostage"]
    r2, sqd2 = c["r2"], c["sqd2"]
    pd = c["pd"]
    stb = c["statsd"]

    for icx in range(IC):
        sc.op("sync", (lambda icx=icx:
                       lambda e: e.dma_start(out=wdq[:, icx * 1024:(icx + 1) * 1024],
                                             in_=t["wdq_s"][icx * 128:(icx + 1) * 128, :]))(),
              inc="LWB2")
    sc.wait("tensor", "LWB2", sc.count("LWB2"))

    dmm, ep2, odone, ostore = {}, {}, {}, {}
    for gc in range(GC):
        r0 = gc * 128
        par = gc % 3
        rot = gc % 3
        sb = (gc % 2) * 8
        if gc >= 3:
            sc.wait("sync", "T", dmm[gc - 3])
            sc.wait("sync", "V", ep2[gc - 3])
        yl = sc.op("sync", (lambda r0=r0, par=par:
                            lambda e: e.dma_start(out=yqT[par][:],
                                                  in_=t["yq_s"][r0:r0 + 128, :]))(), inc="Lyqt%d" % par)
        hl = sc.op("sync", (lambda r0=r0, par=par:
                            lambda e: e.dma_start(out=hres[par][:],
                                                  in_=t["h_s"][r0:r0 + 128, :]))(), inc="Lhr%d" % par)
        sc.wait("tensor", "Lyqt%d" % par, yl)
        sc.wait("tensor", "V", ep2.get(gc - 3, 0))
        tt = None
        for j in range(2):
            for icx in range(IC):
                last = icx == IC - 1 and j == 1
                tt = sc.op("tensor", (lambda rot=rot, j=j, icx=icx, par=par:
                                      lambda e: e.matmul(pd[rot].ap()[:, j * 512:(j + 1) * 512],
                                                         yqT[par][:, icx * 128:(icx + 1) * 128],
                                                         wdq[:, icx * 1024 + j * 512:
                                                             icx * 1024 + j * 512 + 512],
                                                         start=(icx == 0),
                                                         stop=(icx == IC - 1)))(),
                           inc="T" if last else None)
        dmm[gc] = tt
        cdv = sc.op("vector", (lambda gc=gc:
                               lambda e: e.tensor_mul(out=stb[:, sb + 0:sb + 1],
                                                      in0=t["rs_y"][:, gc:gc + 1],
                                                      in1=t["wmean"][:, 2:3]))(), inc="V")
        sc.wait("vector", "T", dmm[gc])
        sc.wait("vector", "Lhr%d" % par, hl)
        sc.wait("vector", "V", cdv)
        sc.wait("vector", "SC", odone.get(gc - 2, 0))
        r2c = r2[gc % 2]
        ep = sc.op("vector", (lambda rot=rot, par=par, r2c=r2c:
                              lambda e: e.scalar_tensor_tensor(out=r2c, in0=pd[rot].ap(),
                                                               scalar=stb[:, sb + 0:sb + 1],
                                                               in1=hres[par][:],
                                                               op0=ALU.mult, op1=ALU.add,
                                                               accum_out=stb[:, sb + 1:sb + 2]))(),
                   inc="V")
        ep2[gc] = ep
        sc.wait("vector", "V", ep)
        fv = sc.op("vector", lambda e: e.tensor_scalar(out=stb[:, sb + 2:sb + 3], in0=stb[:, sb + 1:sb + 2],
                                                       scalar1=1.0 / 1024.0, scalar2=None,
                                                       op0=ALU.mult), inc="V")
        sc.wait("vector", "V", fv)
        mv = sc.op("vector", lambda e: e.tensor_mul(out=stb[:, sb + 3:sb + 4], in0=stb[:, sb + 2:sb + 3],
                                                    in1=stb[:, sb + 2:sb + 3]), inc="V")
        sc.wait("scalar", "V", mv)
        sq = sc.op("scalar", (lambda r2c=r2c:
                              lambda e: e.activation(out=sqd2, in_=r2c, func=AF.Square,
                                                     accum_out=stb[:, sb + 4:sb + 5]))(), inc="SC")
        sc.wait("vector", "SC", sq)
        fv = sc.op("vector", lambda e: e.tensor_scalar(out=stb[:, sb + 5:sb + 6], in0=stb[:, sb + 4:sb + 5],
                                                       scalar1=1.0 / 1024.0, scalar2=1e-5,
                                                       op0=ALU.mult, op1=ALU.add), inc="V")
        sc.wait("vector", "V", fv)
        vv = sc.op("vector", lambda e: e.tensor_sub(out=stb[:, sb + 5:sb + 6], in0=stb[:, sb + 5:sb + 6],
                                                    in1=stb[:, sb + 3:sb + 4]), inc="V")
        sc.wait("scalar", "V", vv)
        sc.op("scalar", lambda e: e.activation(out=stb[:, sb + 6:sb + 7], in_=stb[:, sb + 5:sb + 6],
                                               func=AF.Ln, scale=1.0))
        rs = sc.op("scalar", lambda e: e.activation(out=stb[:, sb + 7:sb + 8], in_=stb[:, sb + 6:sb + 7],
                                                    func=AF.Exp, scale=-0.5), inc="SC")
        sc.wait("vector", "SC", rs)
        nm = sc.op("vector", lambda e: e.scalar_tensor_tensor(out=stb[:, sb + 2:sb + 3], in0=stb[:, sb + 2:sb + 3],
                                                              scalar=-1.0, in1=stb[:, sb + 7:sb + 8],
                                                              op0=ALU.mult, op1=ALU.mult),
                   inc="V")
        sc.wait("scalar", "V", nm)
        if gc >= 3:
            sc.wait("scalar", "Lout%d" % par, ostore[gc - 3])
        ov = sc.op("scalar", (lambda par=par, r2c=r2c:
                              lambda e: e.activation(out=ost[par][:], in_=r2c,
                                                     func=AF.Identity,
                                                     scale=stb[:, sb + 7:sb + 8], bias=stb[:, sb + 2:sb + 3]))(),
                   inc="SC")
        odone[gc] = ov
        sc.wait("sync", "SC", ov)
        ostore[gc] = sc.op("sync", (lambda r0=r0, par=par:
                                    lambda e: e.dma_start(out=t["out"][r0:r0 + 128, :],
                                                          in_=ost[par][:]))(), inc="Lout%d" % par)
    sc.wait("sync", "Lout0", sc.count("Lout0"))
    sc.wait("sync", "Lout1", sc.count("Lout1"))
    sc.wait("sync", "Lout2", sc.count("Lout2"))



def build_nc(dbg=False, phases=("Q", "A", "B1", "B2"), nseq=SEQ):
    nc = bass.Bass("TRN2", target_bir_lowering=False, detect_race_conditions=False)
    skind = "ExternalOutput" if dbg else "Internal"
    t = {}
    t["x_in"] = nc.dram_tensor("x_in", [SEQ * S, D], F32, kind="ExternalInput").ap()
    for nm, shp in (("wqT", [D, D]), ("wkT", [D, D]), ("wvT", [D, D]),
                    ("wgT", [D, INNER]), ("wuT", [D, INNER]), ("wdT", [INNER, D])):
        t[nm] = nc.dram_tensor(nm, shp, F32, kind="ExternalInput").ap()
    t["brows"] = nc.dram_tensor("brows", [1, 3 * D], F32, kind="ExternalInput").ap()
    t["ident_in"] = nc.dram_tensor("ident_in", [128, 128], BF16, kind="ExternalInput").ap()
    t["onesf_in"] = nc.dram_tensor("onesf_in", [128, 128], F32, kind="ExternalInput").ap()
    t["onesb_in"] = nc.dram_tensor("onesb_in", [1, 512], BF16, kind="ExternalInput").ap()
    t["mask_in"] = nc.dram_tensor("mask_in", [128, 128], F32, kind="ExternalInput").ap()
    t["out"] = nc.dram_tensor("out", [SEQ * S, D], F32, kind="ExternalOutput").ap()
    t["qt_s"] = nc.dram_tensor("qt_s", [SEQ * S, D], BF16, kind=skind).ap()
    if dbg:
        t["dbg_st"] = nc.dram_tensor("dbg_st", [SEQ * S, 16], F32, kind=skind).ap()
        t["dbg_p"] = nc.dram_tensor("dbg_p", [128, 4096], BF16, kind=skind).ap()
        t["dbg_s"] = nc.dram_tensor("dbg_s", [128, 4096], BF16, kind=skind).ap()
        t["dbg_s2"] = nc.dram_tensor("dbg_s2", [128, 1040], F32, kind=skind).ap()
    t["h_s"] = nc.dram_tensor("h_s", [SEQ * S, D], F32, kind=skind).ap()
    t["hq_s"] = nc.dram_tensor("hq_s", [SEQ * S, D], BF16, kind=skind).ap()
    t["yq_s"] = nc.dram_tensor("yq_s", [SEQ * S, INNER], BF16, kind=skind).ap()
    t["wgq_s"] = nc.dram_tensor("wgq_s", [D, INNER], BF16, kind=skind).ap()
    t["wuq_s"] = nc.dram_tensor("wuq_s", [D, INNER], BF16, kind=skind).ap()
    t["wdq_s"] = nc.dram_tensor("wdq_s", [INNER, D], BF16, kind=skind).ap()

    counts = {}
    with ExitStack() as top:
        E = top.enter_context
        sem_names = ["LX", "LG", "SC", "V", "T", "Lconst", "LW", "Lqst", "Lyq",
                     "LWB1", "LWB2"]
        for pfx in ("Lstg", "Lqbf", "Lxb", "Lqt", "Lxq", "Lh", "Lhq", "Lhqt",
                    "Lyqt", "Lhr", "Lout"):
            sem_names += [pfx + "0", pfx + "1"]
        sem_names += ["Lyqt2", "Lhr2", "Lout2"]
        sems = {s_: E(nc.semaphore(s_)) for s_ in sem_names}
        ident = E(nc.sbuf_tensor("ident", [128, 128], BF16))
        maskf = E(nc.sbuf_tensor("maskf", [128, 128], F32))
        onesf = E(nc.sbuf_tensor("onesf", [128, 128], F32))
        onesb = E(nc.sbuf_tensor("onesb", [1, 512], BF16))
        t["wsc"] = E(nc.sbuf_tensor("wsc", [128, 4], F32)).ap()
        t["wmean"] = E(nc.sbuf_tensor("wmean", [128, 4], F32)).ap()
        t["rs_h"] = E(nc.sbuf_tensor("rs_h", [128, GC], F32)).ap()
        t["rs_y"] = E(nc.sbuf_tensor("rs_y", [128, GC], F32)).ap()
        t["mgk"] = E(nc.sbuf_tensor("mgk", [128, 4], F32)).ap()
        t["identb"] = ident.ap()

        if "Q" not in phases:
            with ExitStack() as ph:
                P = ph.enter_context
                sc = Sched(counts)
                sc.op("sync", lambda e: e.dma_start(out=ident.ap(), in_=t["ident_in"]),
                      inc="Lconst")
                sc.op("sync", lambda e: e.dma_start(out=maskf.ap(), in_=t["mask_in"]),
                      inc="Lconst")
                sc.op("sync", lambda e: e.dma_start(out=onesf.ap(), in_=t["onesf_in"]),
                      inc="Lconst")
                sc.op("sync", lambda e: e.dma_start(out=onesb.ap(), in_=t["onesb_in"]),
                      inc="Lconst")
                run_block(nc, sc, sems)
                prev = snap(counts)
        if True and "Q" in phases:
         with ExitStack() as ph:
            P = ph.enter_context
            c = {}
            c["q_stg"] = [P(nc.sbuf_tensor(f"q_stg{i}", [128, 4096], F32)).ap() for i in range(2)]
            c["q_bf"] = [P(nc.sbuf_tensor(f"q_bf{i}", [128, 4096], BF16)).ap() for i in range(2)]
            c["q_macc"] = P(nc.sbuf_tensor("q_macc", [128, 4], F32)).ap()
            c["q_tmp"] = P(nc.sbuf_tensor("q_tmp", [128, 4], F32)).ap()
            c["q_s11"] = P(nc.sbuf_tensor("q_s11", [1, 4], F32)).ap()
            c["ones_col"] = onesf.ap()[:, 0:1]
            c["ones_rowf"] = onesf.ap()[0:1, 0:128]
            c["q_p11"] = P(nc.psum_tensor("q_p11", [1, 1], F32))
            c["q_pbc"] = P(nc.psum_tensor("q_pbc", [128, 1], F32))
            c["q_pbc2"] = P(nc.psum_tensor("q_pbc2", [128, 1], F32))
            sc = Sched(counts)
            sc.op("sync", lambda e: e.dma_start(out=ident.ap(), in_=t["ident_in"]), inc="Lconst")
            sc.op("sync", lambda e: e.dma_start(out=maskf.ap(), in_=t["mask_in"]), inc="Lconst")
            sc.op("sync", lambda e: e.dma_start(out=onesf.ap(), in_=t["onesf_in"]), inc="Lconst")
            lcv = sc.op("sync", lambda e: e.dma_start(out=onesb.ap(), in_=t["onesb_in"]),
                        inc="Lconst")
            sc.wait("vector", "Lconst", lcv)
            sc.wait("tensor", "Lconst", lcv)
            mg = t["mgk"]
            fv = sc.op("vector", lambda e: e.tensor_scalar(out=mg[:, 2:3],
                                                           in0=onesf.ap()[:, 0:1],
                                                           scalar1=1e-30, scalar2=None,
                                                           op0=ALU.mult), inc="V")
            sc.wait("vector", "V", fv)
            fv = sc.op("vector", lambda e: e.tensor_scalar(out=mg[:, 2:3], in0=mg[:, 2:3],
                                                           scalar1=1e-30, scalar2=None,
                                                           op0=ALU.mult), inc="V")
            sc.wait("vector", "V", fv)
            fv = sc.op("vector", lambda e: e.tensor_scalar(out=mg[:, 3:4], in0=mg[:, 2:3],
                                                           scalar1=0.0, scalar2=None,
                                                           op0=ALU.is_gt), inc="V")
            sc.wait("vector", "V", fv)
            fv = sc.op("vector", lambda e: e.tensor_scalar(out=mg[:, 0:1], in0=mg[:, 3:4],
                                                           scalar1=M64 - M32, scalar2=M32,
                                                           op0=ALU.mult, op1=ALU.add),
                       inc="V")
            sc.wait("vector", "V", fv)
            fv = sc.op("vector", lambda e: e.tensor_scalar(out=mg[:, 1:2], in0=mg[:, 0:1],
                                                           scalar1=-1.0, scalar2=None,
                                                           op0=ALU.mult), inc="V")
            sc.wait("vector", "V", fv)
            phase_quant(nc, sc, t, c)
            run_block(nc, sc, sems)
            prev = snap(counts)

        with ExitStack() as pha:
            P = pha.enter_context
            a = {}
            a["KT"] = P(nc.sbuf_tensor("KT", [128, DC * 4096], BF16)).ap()
            a["V"] = P(nc.sbuf_tensor("Vres", [128, TC * 1024], BF16)).ap()
            a["ident"] = ident.ap()
            a["maskf"] = maskf.ap()
            a["ones512"] = onesb.ap()
            for s_ in range(nseq):
                with ExitStack() as ph1:
                    P1 = ph1.enter_context
                    c = {}
                    a["W"] = P1(nc.sbuf_tensor(f"Wqkv{s_}", [128, 3 * 8192], BF16)).ap()
                    a["brows"] = P1(nc.sbuf_tensor(f"browsb{s_}", [1, 3 * D], BF16)).ap()
                    c["xb"] = [P1(nc.sbuf_tensor(f"xb{i}_{s_}", [128, D], BF16))
                               for i in range(2)]
                    c["xT"] = [P1(nc.sbuf_tensor(f"xT{i}_{s_}", [128, 2048], BF16)).ap()
                               for i in range(2)]
                    c["qstage"] = P1(nc.sbuf_tensor(f"qstage{s_}", [128, 2048], BF16)).ap()
                    c["ptx"] = [P1(nc.psum_tensor(f"ptx{i}_{s_}", [128, 1024], BF16))
                                for i in range(2)]
                    c["psA"] = [P1(nc.psum_tensor(f"psA{i}_{s_}", [128, 512], F32))
                                for i in range(2)]
                    c["psV"] = [P1(nc.psum_tensor(f"psV{i}_{s_}", [128, 1024], F32))
                                for i in range(2)]
                    sc = Sched(counts)
                    for e_ in ENGINES:
                        sc.wait_all(e_, prev)
                    for proj, wnm in ((0, "wqT"), (1, "wkT"), (2, "wvT")):
                        for dcs in range(DC):
                            sc.op("gpsimd",
                                  (lambda proj=proj, wnm=wnm, dcs=dcs:
                                   lambda e: e.dma_start(
                                       out=a["W"][:, proj * 8192 + dcs * 1024:
                                                  proj * 8192 + (dcs + 1) * 1024],
                                       in_=t[wnm][dcs * 128:(dcs + 1) * 128, :]))(),
                                  inc="LW")
                    sc.op("gpsimd", lambda e: e.dma_start(out=a["brows"], in_=t["brows"]),
                          inc="LW")
                    sc.wait("tensor", "LW", sc.count("LW"))
                    phase_pass1(nc, sc, t, c, a, s_)
                    run_block(nc, sc, sems)
                    prev = snap(counts)
                with ExitStack() as ph2:
                    P2 = ph2.enter_context
                    c = {}
                    c["qtile"] = [P2(nc.sbuf_tensor(f"qtile{i}_{s_}", [128, D], BF16))
                                  for i in range(2)]
                    c["xq"] = [P2(nc.sbuf_tensor(f"xq{i}_{s_}", [128, D], F32))
                               for i in range(2)]
                    c["Ssb"] = P2(nc.sbuf_tensor(f"Ssb{s_}", [128, 4096], BF16)).ap()
                    c["Psb"] = P2(nc.sbuf_tensor(f"Psb{s_}", [128, 4096], BF16)).ap()
                    c["PTs"] = P2(nc.sbuf_tensor(f"PTs{s_}", [128, 1024], BF16)).ap()
                    c["stats"] = P2(nc.sbuf_tensor(f"stats{s_}", [128, 16], F32)).ap()
                    c["hstage"] = [P2(nc.sbuf_tensor(f"hst{i}_{s_}", [128, D], F32))
                                   for i in range(2)]
                    c["hqstage"] = [P2(nc.sbuf_tensor(f"hqst{i}_{s_}", [128, D], BF16))
                                    for i in range(2)]
                    c["hqb"] = P2(nc.sbuf_tensor(f"hqb{s_}", [128, D], BF16)).ap()
                    c["rbuf"] = P2(nc.sbuf_tensor(f"rbuf{s_}", [128, D], F32)).ap()
                    c["dbgbuf"] = P2(nc.sbuf_tensor(f"dbgbuf{s_}", [128, 272], BF16)).ap()
                    c["dbgbuf2"] = P2(nc.sbuf_tensor(f"dbgbuf2{s_}", [128, 1040], F32)).ap()
                    c["sqdump"] = c["Ssb"][:, 0:1024]
                    c["psS"] = [P2(nc.psum_tensor(f"psS{i}_{s_}", [128, 512], F32))
                                for i in range(2)]
                    c["psT"] = [P2(nc.psum_tensor(f"psTq{i}_{s_}", [128, 512], BF16))
                                for i in range(2)]
                    c["psO"] = P2(nc.psum_tensor(f"psO{s_}", [128, 1024], F32))
                    sc = Sched(counts)
                    for e_ in ENGINES:
                        sc.wait_all(e_, prev)
                    phase_qloop(nc, sc, t, c, a, s_)
                    run_block(nc, sc, sems)
                    prev = snap(counts)

        if "B1" in phases:
         with ExitStack() as ph:
            P = ph.enter_context
            c = {}
            c["wgq"] = P(nc.sbuf_tensor("wgq", [128, DC * 4096], BF16)).ap()
            c["wuq"] = P(nc.sbuf_tensor("wuq", [128, DC * 4096], BF16)).ap()
            c["hqT"] = [P(nc.sbuf_tensor(f"hqT{i}", [128, D], BF16)) for i in range(2)]
            c["sg"] = [P(nc.sbuf_tensor(f"sg{i}", [128, 512], BF16)).ap()
                       for i in range(3)]
            c["ybf"] = P(nc.sbuf_tensor("ybf", [128, INNER], BF16)).ap()
            c["yqbf"] = [P(nc.sbuf_tensor(f"yqbf{i}", [128, INNER], BF16)).ap()
                         for i in range(2)]
            c["yqstage"] = P(nc.sbuf_tensor("yqstage", [128, INNER], BF16)).ap()
            c["yf32"] = P(nc.sbuf_tensor("yf32", [128, INNER], F32)).ap()
            c["statsb"] = P(nc.sbuf_tensor("statsb", [128, 8], F32)).ap()
            c["pg"] = [P(nc.psum_tensor(f"pg{i}", [128, 512], F32)) for i in range(3)]
            c["pu"] = [P(nc.psum_tensor(f"pu{i}", [128, 512], F32)) for i in range(3)]
            c["psTb"] = [P(nc.psum_tensor(f"psTb{i}", [128, 512], BF16)) for i in range(2)]
            sc = Sched(counts)
            for e_ in ENGINES:
                sc.wait_all(e_, prev)
            phase_b1(nc, sc, t, c)
            run_block(nc, sc, sems)
            prev = snap(counts)

        if "B2" in phases:
         with ExitStack() as ph:
            P = ph.enter_context
            c = {}
            c["wdq"] = P(nc.sbuf_tensor("wdq", [128, IC * 1024], BF16)).ap()
            c["yqT"] = [P(nc.sbuf_tensor(f"yqT{i}", [128, INNER], BF16)) for i in range(3)]
            c["hres"] = [P(nc.sbuf_tensor(f"hres{i}", [128, D], F32)) for i in range(3)]
            c["ostage"] = [P(nc.sbuf_tensor(f"ost{i}", [128, D], F32)) for i in range(3)]
            c["r2"] = [P(nc.sbuf_tensor(f"r2_{i}", [128, D], F32)).ap() for i in range(2)]
            c["sqd2"] = P(nc.sbuf_tensor("sqd2", [128, D], F32)).ap()
            c["statsd"] = P(nc.sbuf_tensor("statsd", [128, 16], F32)).ap()
            c["pd"] = [P(nc.psum_tensor(f"pd{i}", [128, 1024], F32)) for i in range(3)]
            sc = Sched(counts)
            for e_ in ENGINES:
                sc.wait_all(e_, prev)
            phase_b2(nc, sc, t, c)
            run_block(nc, sc, sems)
    return nc


def make_consts():
    ident = np.eye(128, dtype=ml_dtypes.bfloat16)
    r = np.arange(128)
    mask = np.where(r[None, :] > r[:, None], np.float32(MASKV),
                    np.float32(0.0)).astype(np.float32)
    return ident, mask


def kernel(L0_attn_x, L0_attn_Wq, L0_attn_bq, L0_attn_Wk, L0_attn_bk, L0_attn_Wv,
           L0_attn_bv, L0_ffn_w_gate, L0_ffn_w_up, L0_ffn_w_down):
    x = np.ascontiguousarray(np.asarray(L0_attn_x, dtype=np.float32))
    wqT = np.ascontiguousarray(np.asarray(L0_attn_Wq, dtype=np.float32).T)
    wkT = np.ascontiguousarray(np.asarray(L0_attn_Wk, dtype=np.float32).T)
    wvT = np.ascontiguousarray(np.asarray(L0_attn_Wv, dtype=np.float32).T)
    wgT = np.ascontiguousarray(np.asarray(L0_ffn_w_gate, dtype=np.float32).T)
    wuT = np.ascontiguousarray(np.asarray(L0_ffn_w_up, dtype=np.float32).T)
    wdT = np.ascontiguousarray(np.asarray(L0_ffn_w_down, dtype=np.float32).T)
    brows = np.concatenate([np.asarray(L0_attn_bq, dtype=np.float32),
                            np.asarray(L0_attn_bk, dtype=np.float32),
                            np.asarray(L0_attn_bv, dtype=np.float32)]).reshape(1, 3 * D)
    ident, mask = make_consts()
    onesf = np.ones((128, 128), dtype=np.float32)
    onesb = np.ones((1, 512), dtype=ml_dtypes.bfloat16)

    nc = build_nc()
    in_maps = []
    for cid in range(NCORES):
        xs = np.ascontiguousarray(x[SEQ * cid:SEQ * (cid + 1)].reshape(SEQ * S, D))
        in_maps.append({"x_in": xs, "wqT": wqT, "wkT": wkT, "wvT": wvT,
                        "wgT": wgT, "wuT": wuT, "wdT": wdT, "brows": brows,
                        "ident_in": ident, "mask_in": mask,
                        "onesf_in": onesf, "onesb_in": onesb})
    res = run_bass_kernel_spmd(nc, in_maps, core_ids=list(range(NCORES)))
    out = np.concatenate([r["out"].reshape(SEQ, S, D) for r in res.results], axis=0)
    return out


# version 08
# speedup vs baseline: 1.5373x; 1.3137x over previous
import numpy as np
import ml_dtypes
from contextlib import ExitStack

import concourse.bass as bass
import concourse.mybir as mybir
from concourse.bass_utils import run_bass_kernel_spmd

F32 = mybir.dt.float32
BF16 = mybir.dt.bfloat16
I8 = mybir.dt.int8
AF = mybir.ActivationFunctionType
ALU = mybir.AluOpType
AX = mybir.AxisListType

NCORES = 8
SEQ = 2
S = 4096
D = 1024
INNER = 4096
TC = S // 128
GC = SEQ * TC
DC = D // 128
IC = INNER // 128
MASKV = -2500.0
M32 = 12582912.0
M64 = 6755399441055744.0

ENGINES = ("sync", "gpsimd", "scalar", "vector", "tensor")
ESEM = {"sync": "LX", "gpsimd": "LG", "scalar": "SC", "vector": "V", "tensor": "T"}
def _sem_step(name):
    return 16 if name.startswith("L") else 1


class Sched:

    def __init__(self, counts):
        self.ops = {e: [] for e in ENGINES}
        self.n = counts
        self._last_selfwait = {}

    def op(self, eng, builder, inc=None):
        step = _sem_step(inc) if inc else 0
        if inc:
            self.n[inc] = self.n.get(inc, 0) + step
        self.ops[eng].append([builder, inc, step])
        return self.n.get(inc, 0)

    def wait(self, eng, sem, val):
        if val and val > 0:
            self.ops[eng].append([("W", sem, val), None, 0])

    def wait_all(self, eng, snapshot):
        for s_, v in snapshot.items():
            self.wait(eng, s_, v)

    def count(self, sem):
        return self.n.get(sem, 0)

    def close(self):
        for e in ENGINES:
            for item in reversed(self.ops[e]):
                if not isinstance(item[0], tuple):
                    if item[1] is None:
                        sem = ESEM[e]
                        item[1] = sem
                        item[2] = _sem_step(sem)
                        self.n[sem] = self.n.get(sem, 0) + _sem_step(sem)
                    break


def run_block(nc, sched, sems):
    sched.close()
    with nc.Block() as blk:
        for ename in ENGINES:
            def body(e, _ename=ename):
                for b, inc, step in sched.ops[_ename]:
                    if isinstance(b, tuple):
                        e.wait_ge(sems[b[1]], b[2])
                    else:
                        ins = b(e)
                        if inc is not None:
                            ins.then_inc(sems[inc], step)
            getattr(blk, ename)(body)


def snap(counts):
    return dict(counts)



def phase_quant(nc, sc, t, c):
    mats = [(t["wgT"], t["wgq_s"], 0, 128), (t["wuT"], t["wuq_s"], 1, 128),
            (t["wdT"], t["wdq_s"], 2, 512)]
    stg = c["q_stg"]
    qbf = c["q_bf"]
    macc = c["q_macc"]
    tmp = c["q_tmp"]
    s11 = c["q_s11"]
    onec = c["ones_col"]
    oner = c["ones_rowf"]
    p11 = c["q_p11"]
    pbc = c["q_pbc"]
    pbc2 = c["q_pbc2"]
    wsc, wmean = t["wsc"], t["wmean"]

    vload = {}
    gstore = {}
    for dram, scratch, m, nr in mats:
        isp = nr // 128
        for k in range(8):
            par = k % 2
            if par in vload:
                sc.wait("sync", "V", vload[par])
            lv = sc.op("sync", (lambda dram=dram, k=k, par=par, nr=nr, isp=isp:
                                lambda e: e.dma_start(
                                    out=stg[par][:].rearrange("p (i o) -> p i o", i=isp),
                                    in_=dram[k * nr:(k + 1) * nr, :]
                                    .rearrange("(i p) o -> p i o", p=128)))(),
                       inc="Lstg%d" % par)
            sc.wait("vector", "Lstg%d" % par, lv)
            if k == 0:
                vload[par] = sc.op("vector",
                                   (lambda par=par, m=m:
                                    lambda e: e.tensor_reduce(out=macc[:, m:m + 1],
                                                              in_=stg[par][:], axis=AX.X,
                                                              op=ALU.add,
                                                              apply_absolute_value=True))(),
                                   inc="V")
            else:
                rv = sc.op("vector", (lambda par=par, m=m:
                                      lambda e: e.tensor_reduce(out=tmp[:, m:m + 1],
                                                                in_=stg[par][:],
                                                                axis=AX.X, op=ALU.add,
                                                                apply_absolute_value=True))(),
                           inc="V")
                sc.wait("vector", "V", rv)
                vload[par] = sc.op("vector", (lambda m=m:
                                              lambda e: e.tensor_add(out=macc[:, m:m + 1],
                                                                     in0=macc[:, m:m + 1],
                                                                     in1=tmp[:, m:m + 1]))(),
                                   inc="V")
        sc.wait("tensor", "V", sc.count("V"))
        tt = sc.op("tensor", (lambda m=m: lambda e: e.matmul(p11[:], onec[:],
                                                             macc[:, m:m + 1],
                                                             start=True, stop=True))(), inc="T")
        sc.wait("vector", "T", tt)
        nel = float(8 * 128 * 4096)
        fv = sc.op("vector", (lambda m=m:
                              lambda e: e.tensor_scalar(out=s11[:, m:m + 1], in0=p11[:],
                                                        scalar1=1.0 / nel, scalar2=1e-5,
                                                        op0=ALU.mult, op1=ALU.max))(),
                   inc="V")
        sc.wait("vector", "V", fv)
        v = sc.op("vector", (lambda m=m:
                             lambda e: e.reciprocal(out=s11[:, 3:4], in_=s11[:, m:m + 1]))(),
                  inc="V")
        sc.wait("tensor", "V", v)
        sc.op("tensor", (lambda m=m: lambda e: e.matmul(pbc[:], oner[:], s11[:, m:m + 1],
                                                        start=True, stop=True))())
        tt = sc.op("tensor", lambda e: e.matmul(pbc2[:], oner[:], s11[:, 3:4],
                                                start=True, stop=True), inc="T")
        sc.wait("vector", "T", tt)
        sc.op("vector", (lambda m=m: lambda e: e.tensor_copy(out=wmean[:, m:m + 1],
                                                             in_=pbc[:]))())
        sc.op("vector", (lambda m=m: lambda e: e.tensor_copy(out=wsc[:, m:m + 1],
                                                             in_=pbc2[:]))(), inc="V")
        for k in range(8):
            par = k % 2
            sc.wait("sync", "V", vload[par])
            lv = sc.op("sync", (lambda dram=dram, k=k, par=par, nr=nr, isp=isp:
                                lambda e: e.dma_start(
                                    out=stg[par][:].rearrange("p (i o) -> p i o", i=isp),
                                    in_=dram[k * nr:(k + 1) * nr, :]
                                    .rearrange("(i p) o -> p i o", p=128)))(),
                       inc="Lstg%d" % par)
            sc.wait("vector", "Lstg%d" % par, lv)
            if par in gstore:
                sc.wait("vector", "Lqbf%d" % par, gstore[par])
            sc.op("vector", (lambda par=par, m=m:
                             lambda e: e.tensor_scalar(out=stg[par][:], in0=stg[par][:],
                                                       scalar1=wsc[:, m:m + 1],
                                                       scalar2=None, op0=ALU.mult))())
            sc.op("vector", (lambda par=par:
                             lambda e: e.tensor_scalar(out=stg[par][:], in0=stg[par][:],
                                                       scalar1=t["mgk"][:, 0:1],
                                                       scalar2=None, op0=ALU.add))())
            sc.op("vector", (lambda par=par:
                             lambda e: e.tensor_scalar(out=stg[par][:], in0=stg[par][:],
                                                       scalar1=t["mgk"][:, 1:2], scalar2=1.0,
                                                       op0=ALU.add, op1=ALU.min))())
            vload[par] = sc.op("vector", (lambda par=par:
                                          lambda e: e.tensor_scalar(out=qbf[par][:],
                                                                    in0=stg[par][:],
                                                                    scalar1=-1.0,
                                                                    scalar2=None,
                                                                    op0=ALU.max))(),
                               inc="V")
            sc.wait("gpsimd", "V", vload[par])
            gstore[par] = sc.op("gpsimd",
                                (lambda scratch=scratch, k=k, par=par, nr=nr, isp=isp:
                                 lambda e: e.dma_start(
                                     out=scratch[k * nr:(k + 1) * nr, :]
                                     .rearrange("(i p) o -> p i o", p=128),
                                     in_=qbf[par][:].rearrange("p (i o) -> p i o",
                                                               i=isp)))(), inc="Lqbf%d" % par)



def phase_pass1(nc, sc, t, c, a, s):
    KT, Vr, W = a["KT"], a["V"], a["W"]
    xb, xT = c["xb"], c["xT"]
    qstage = c["qstage"]
    ptx, psA, psV = c["ptx"], c["psA"], c["psV"]
    brows, ones5 = a["brows"], a["ones512"]
    NG = 16

    xb_free, evac_done, mm_done = {}, {}, {}
    qev_done, qst_done, vev_done = {}, {}, {}

    def emit_transposes(g):
        for ts in range(2):
            tc_ = 2 * g + ts
            r0 = (s * TC + tc_) * 128
            par = tc_ % 2
            if tc_ >= 2:
                sc.wait("gpsimd", "T", xb_free[tc_ - 2])
            lv = sc.op("gpsimd", (lambda r0=r0, par=par:
                                  lambda e: e.dma_start(out=xb[par][:],
                                                        in_=t["x_in"][r0:r0 + 128, :]))(),
                       inc="Lxb%d" % par)
            sc.wait("tensor", "Lxb%d" % par, lv)
            if tc_ >= 2:
                sc.wait("tensor", "V", evac_done[tc_ - 2])
            for dcs in range(DC):
                sc.op("tensor", (lambda par=par, dcs=dcs:
                                 lambda e: e.transpose(ptx[par][:, dcs * 128:(dcs + 1) * 128],
                                                       xb[par][:, dcs * 128:(dcs + 1) * 128],
                                                       a["ident"]))(),
                      inc="T" if dcs == DC - 1 else None)
            xb_free[tc_] = sc.count("T")
            sc.wait("vector", "T", sc.count("T"))
            if g >= 2:
                sc.wait("vector", "T", mm_done[g - 2])
            dst = xT[g % 2].rearrange("p (dc two u) -> p dc two u", dc=DC, two=2)
            src = ptx[par].ap().rearrange("p (dc u) -> p dc u", dc=DC)
            evac_done[tc_] = sc.op("vector", (lambda dst=dst, src=src, ts=ts:
                                              lambda e: e.tensor_copy(out=dst[:, :, ts, :],
                                                                      in_=src))(), inc="V")

    def emit_group_mm(g):
        sc.wait("tensor", "V", evac_done[2 * g + 1])
        for proj in (0, 1):
            for oc in range(DC):
                idx = g * 2 * DC + proj * DC + oc
                rot = idx % 2
                sc.wait("tensor", "V", qev_done.get(idx - 2, 0))
                for dcs in range(DC):
                    sc.op("tensor", (lambda rot=rot, proj=proj, oc=oc, dcs=dcs, g=g:
                                     lambda e: e.matmul(psA[rot].ap()[:, 0:256],
                                                        W[:, proj * 8192 + dcs * 1024 + oc * 128:
                                                          proj * 8192 + dcs * 1024 + oc * 128 + 128],
                                                        xT[g % 2][:, dcs * 256:(dcs + 1) * 256],
                                                        start=(dcs == 0), stop=False))())
                tt = sc.op("tensor", (lambda rot=rot, proj=proj, oc=oc:
                                      lambda e: e.matmul(psA[rot].ap()[:, 0:256],
                                                         brows[0:1, proj * 1024 + oc * 128:
                                                               proj * 1024 + oc * 128 + 128],
                                                         ones5[0:1, 0:256],
                                                         start=False, stop=True))(), inc="T")
                sc.wait("vector", "T", tt)
                if proj == 1:
                    qev_done[idx] = sc.op("vector",
                                          (lambda rot=rot, oc=oc, g=g:
                                           lambda e: e.tensor_copy(
                                               out=KT[:, oc * 4096 + g * 256:
                                                      oc * 4096 + g * 256 + 256],
                                               in_=psA[rot].ap()[:, 0:256]))(), inc="V")
                else:
                    if oc == 0 and g >= 1:
                        sc.wait("vector", "Lqst", qst_done[g - 1])
                    dstq = qstage.rearrange("p (two oc u) -> p two oc u", two=2, oc=DC)
                    qev_done[idx] = sc.op("vector",
                                          (lambda rot=rot, oc=oc, dstq=dstq:
                                           lambda e: e.tensor_copy(
                                               out=dstq[:, :, oc, :],
                                               in_=psA[rot].ap()[:, 0:256].rearrange(
                                                   "p (two u) -> p two u", two=2)))(),
                                          inc="V")
        sc.wait("gpsimd", "V", qev_done[g * 2 * DC + DC - 1])
        r0 = (s * TC + 2 * g) * 128
        qst_done[g] = sc.op("gpsimd", (lambda r0=r0:
                                       lambda e: e.dma_start(
                                           out=t["qt_s"][r0:r0 + 256, :]
                                           .rearrange("(i p) o -> p i o", p=128),
                                           in_=qstage.rearrange("p (i o) -> p i o",
                                                               i=2)))(), inc="Lqst")
        for ts in range(2):
            vidx = g * 2 + ts
            rot = vidx % 2
            tc_ = 2 * g + ts
            sc.wait("tensor", "V", vev_done.get(vidx - 2, 0))
            for j in range(2):
                for dcs in range(DC):
                    sc.op("tensor", (lambda rot=rot, ts=ts, j=j, dcs=dcs, g=g:
                                     lambda e: e.matmul(psV[rot].ap()[:, j * 512:(j + 1) * 512],
                                                        xT[g % 2][:, dcs * 256 + ts * 128:
                                                                   dcs * 256 + ts * 128 + 128],
                                                        W[:, 2 * 8192 + dcs * 1024 + j * 512:
                                                          2 * 8192 + dcs * 1024 + j * 512 + 512],
                                                        start=(dcs == 0), stop=False))())
                tt = sc.op("tensor", (lambda rot=rot, j=j:
                                      lambda e: e.matmul(psV[rot].ap()[:, j * 512:(j + 1) * 512],
                                                         ones5[0:1, 0:128],
                                                         brows[0:1, 2048 + j * 512:
                                                               2048 + (j + 1) * 512],
                                                         start=False, stop=True))(),
                           inc="T" if j == 1 else None)
            sc.wait("vector", "T", tt)
            vev_done[vidx] = sc.op("vector",
                                   (lambda rot=rot, tc_=tc_:
                                    lambda e: e.tensor_copy(
                                        out=Vr[:, tc_ * 1024:(tc_ + 1) * 1024],
                                        in_=psV[rot].ap()))(), inc="V")
        mm_done[g] = sc.count("T")

    for g in range(NG):
        emit_transposes(g)
        if g >= 1:
            emit_group_mm(g - 1)
    emit_group_mm(NG - 1)



def phase_qloop(nc, sc, t, c, a, s):
    KT, Vr = a["KT"], a["V"]
    qtile, xq = c["qtile"], c["xq"]
    Ssb, PTs = c["Ssb"], c["PTs"]
    Psb = c["Psb"]
    psS, psT, psO = c["psS"], c["psT"], c["psO"]
    st = c["stats"]
    hst, hqst = c["hstage"], c["hqstage"]
    hqb = c["hqb"]
    sqd, rbuf = c["sqdump"], c["rbuf"]
    mask = a["maskf"]

    score_done = {}
    epi_done = {}
    hstore, hqstore = {}, {}
    last_psS = {}
    last_psT = {}
    last_PTs = {}
    ex_prev = {}
    ssb_rd_T = {0: 0}
    hb_of = {}
    hqT_done = {}
    hqtr_t = {}

    def hqT_tensor(tqt):
        tpar = tqt % 2
        sc.wait("tensor", "V", hb_of[tqt])
        for b in range(2):
            rot = b % 2
            sc.wait("tensor", "V", last_psT.get(rot, 0))
            for u in range(4):
                dcs = 4 * b + u
                tt = sc.op("tensor", (lambda rot=rot, dcs=dcs, u=u, tpar=tpar:
                                      lambda e: e.transpose(
                                          psT[rot].ap()[:, u * 128:(u + 1) * 128],
                                          hqb[tpar][:, dcs * 128:(dcs + 1) * 128],
                                          a["ident"]))(),
                           inc="T" if u == 3 else None)
            hqtr_t[(tqt, b)] = tt
        hqT_done[tqt] = sc.count("T")

    def hqT_evac(tqt):
        tpar = tqt % 2
        r0t = (s * TC + tqt) * 128
        for b in range(2):
            rot = b % 2
            sc.wait("vector", "T", hqtr_t[(tqt, b)])
            if b == 0 and tqt >= 2:
                sc.wait("vector", "Lhq%d" % tpar, hqstore[tqt - 2])
            hev = sc.op("vector", (lambda rot=rot, b=b, tpar=tpar:
                                   lambda e: e.tensor_copy(
                                       out=hqst[tpar][:, b * 512:(b + 1) * 512],
                                       in_=psT[rot].ap()))(), inc="V")
            last_psT[rot] = hev
        sc.wait("gpsimd", "V", hev)
        hqstore[tqt] = sc.op("gpsimd", (lambda r0t=r0t, tpar=tpar:
                                        lambda e: e.dma_start(
                                            out=t["hq_s"][r0t:r0t + 128, :],
                                            in_=hqst[tpar][:]))(), inc="Lhq%d" % tpar)

    for qt in range(TC):
        gc = s * TC + qt
        r0 = gc * 128
        nkc = qt + 1
        ngrp = (nkc + 3) // 4
        W_ = nkc * 128
        par = qt % 2
        if qt >= 2:
            sc.wait("sync", "T", score_done[qt - 2])
            sc.wait("sync", "V", epi_done.get(qt - 2, 0))
        qld = sc.op("sync", (lambda r0=r0, par=par:
                             lambda e: e.dma_start(out=qtile[par][:],
                                                   in_=t["qt_s"][r0:r0 + 128, :]))(), inc="Lqt%d" % par)
        xld = sc.op("sync", (lambda r0=r0, par=par:
                             lambda e: e.dma_start(out=xq[par][:],
                                                   in_=t["x_in"][r0:r0 + 128, :]))(), inc="Lxq%d" % par)
        sc.wait("tensor", "Lqt%d" % par, qld)
        ev_grp = {}
        for j in range(ngrp):
            w = min(512, (nkc - 4 * j) * 128)
            rot = j % 2
            sc.wait("tensor", "SC", last_psS.get(rot, 0))
            for dcs in range(DC):
                tt = sc.op("tensor", (lambda rot=rot, j=j, w=w, dcs=dcs, par=par:
                                      lambda e: e.matmul(psS[rot].ap()[:, 0:w],
                                                         qtile[par][:, dcs * 128:(dcs + 1) * 128],
                                                         KT[:, dcs * 4096 + j * 512:
                                                            dcs * 4096 + j * 512 + w],
                                                         start=(dcs == 0),
                                                         stop=(dcs == DC - 1)))(),
                           inc="T" if dcs == DC - 1 else None)
            sc.wait("vector", "T", tt)
            if j == ngrp - 1:
                sc.op("vector", (lambda rot=rot, w=w:
                                 lambda e: e.tensor_add(out=psS[rot].ap()[:, w - 128:w],
                                                        in0=psS[rot].ap()[:, w - 128:w],
                                                        in1=mask))())
            rv = sc.op("vector", (lambda rot=rot, w=w:
                                  lambda e: e.tensor_reduce(out=st[:, 0:1],
                                                            in_=psS[rot].ap()[:, 0:w],
                                                            axis=AX.X, op=ALU.max))(),
                       inc="V")
            sc.wait("vector", "V", rv)
            if j == 0:
                mv = sc.op("vector", lambda e: e.tensor_copy(out=st[:, 1:2], in_=st[:, 0:1]),
                           inc="V")
            else:
                mv = sc.op("vector", lambda e: e.tensor_max(out=st[:, 1:2], in0=st[:, 1:2],
                                                            in1=st[:, 0:1]), inc="V")
            sc.wait("scalar", "V", mv)
            if j <= 1:
                sc.wait("scalar", "T", ssb_rd_T.get(qt - 1, 0))
            ev_grp[j] = sc.op("scalar", (lambda rot=rot, j=j, w=w:
                                         lambda e: e.activation(out=Ssb[:, j * 512:j * 512 + w],
                                                                in_=psS[rot].ap()[:, 0:w],
                                                                func=AF.Copy,
                                                                scale=1.0 / 32.0))(), inc="SC")
            last_psS[rot] = ev_grp[j]
        score_done[qt] = sc.count("T")
        if qt >= 1:
            hqT_tensor(qt - 1)
        sc.wait("vector", "SC", ex_prev.get(qt - 1, 0))
        sc.wait("vector", "V", mv)
        bv = sc.op("vector", lambda e: e.tensor_scalar(out=st[:, 2:3], in0=st[:, 1:2],
                                                       scalar1=-1.0 / 32.0, scalar2=None,
                                                       op0=ALU.mult), inc="V")
        sc.wait("scalar", "V", bv)
        ex = sc.op("scalar", (lambda W_=W_:
                              lambda e: e.activation(out=Psb[:, 0:W_], in_=Ssb[:, 0:W_],
                                                     func=AF.Exp, bias=st[:, 2:3], scale=1.0,
                                                     accum_out=st[:, 3:4]))(), inc="SC")
        ex_prev[qt] = ex
        if qt >= 1:
            hqT_evac(qt - 1)
        sc.wait("vector", "SC", ex)
        lv = sc.op("vector", lambda e: e.reciprocal(out=st[:, 4:5], in_=st[:, 3:4]), inc="V")
        sc.wait("vector", "V", lv)
        if t.get("dbg_p") is not None and s == 0 and qt == 0:
            sc.op("vector", lambda e: e.tensor_copy(out=c["dbgbuf"][:, 0:128],
                                                    in_=Ssb[:, 0:128]))
            sc.op("vector", lambda e: e.tensor_copy(out=c["dbgbuf"][:, 128:256],
                                                    in_=Psb[:, 0:128]))
            dv = sc.op("vector", lambda e: e.tensor_copy(out=c["dbgbuf"][:, 256:272],
                                                         in_=st[:, 0:16]), inc="V")
            sc.wait("gpsimd", "V", dv)
            sc.op("gpsimd", lambda e: e.dma_start(out=t["dbg_p"][0:128, 0:272],
                                                  in_=c["dbgbuf"][:, 0:272]), inc="LG")
            dbg_qt0 = True
        else:
            dbg_qt0 = False
        nb = (nkc + 3) // 4
        pt_ev = {}
        sc.wait("tensor", "SC", ex)
        sc.wait("tensor", "V", epi_done.get(qt - 1, 0))

        def emit_pt(b, qt=qt):
            rot = b % 2
            sc.wait("tensor", "V", last_psT.get(rot, 0))
            nch = min(4, nkc - 4 * b)
            for u in range(nch):
                cidx = 4 * b + u
                tt = sc.op("tensor", (lambda rot=rot, u=u, cidx=cidx:
                                      lambda e: e.transpose(psT[rot].ap()[:, u * 128:(u + 1) * 128],
                                                            Psb[:, cidx * 128:(cidx + 1) * 128],
                                                            a["ident"]))(),
                           inc="T" if u == nch - 1 else None)
            ssb_rd_T[qt] = tt
            sc.wait("vector", "T", tt)
            sc.wait("vector", "T", last_PTs.get(rot, 0))
            pt_ev[b] = sc.op("vector", (lambda rot=rot, nch=nch:
                                        lambda e: e.tensor_copy(
                                            out=PTs[:, rot * 512:rot * 512 + nch * 128],
                                            in_=psT[rot].ap()[:, 0:nch * 128]))(), inc="V")
            last_psT[rot] = pt_ev[b]

        def emit_pv(b, final):
            rot = b % 2
            sc.wait("tensor", "V", pt_ev[b])
            nch = min(4, nkc - 4 * b)
            tt = None
            for u in range(nch):
                cidx = 4 * b + u
                for j in range(2):
                    batch_last = u == nch - 1 and j == 1
                    tt = sc.op("tensor",
                               (lambda rot=rot, u=u, cidx=cidx, j=j, final=final:
                                lambda e: e.matmul(psO.ap()[:, j * 512:(j + 1) * 512],
                                                   PTs[:, rot * 512 + u * 128:
                                                       rot * 512 + (u + 1) * 128],
                                                   Vr[:, cidx * 1024 + j * 512:
                                                      cidx * 1024 + j * 512 + 512],
                                                   start=(cidx == 0),
                                                   stop=(final and u == nch - 1)))(),
                               inc="T" if batch_last else None)
            last_PTs[rot] = tt
            return tt

        for b in range(nb):
            emit_pt(b)
            if b >= 1:
                emit_pv(b - 1, final=False)
        pv_done = emit_pv(nb - 1, final=True)
        sc.wait("vector", "T", pv_done)
        sc.wait("vector", "Lxq%d" % par, xld)
        ep = sc.op("vector", (lambda par=par:
                              lambda e: e.scalar_tensor_tensor(out=rbuf, in0=psO.ap(),
                                                               scalar=st[:, 4:5], in1=xq[par][:],
                                                               op0=ALU.mult, op1=ALU.add,
                                                               accum_out=st[:, 5:6]))(), inc="V")
        epi_done[qt] = ep
        sc.wait("vector", "V", ep)
        if t.get("dbg_p") is not None and s == 0 and qt == 0:
            sc.op("vector", lambda e: e.tensor_copy(out=c["dbgbuf2"][:, 0:1024],
                                                    in_=rbuf))
        fv = sc.op("vector", lambda e: e.tensor_scalar(out=st[:, 6:7], in0=st[:, 5:6],
                                                       scalar1=1.0 / 1024.0, scalar2=None,
                                                       op0=ALU.mult), inc="V")
        sc.wait("vector", "V", fv)
        mv = sc.op("vector", lambda e: e.tensor_mul(out=st[:, 7:8], in0=st[:, 6:7],
                                                    in1=st[:, 6:7]), inc="V")
        sc.wait("scalar", "V", mv)
        sq = sc.op("scalar", lambda e: e.activation(out=sqd, in_=rbuf, func=AF.Square,
                                                    accum_out=st[:, 8:9]), inc="SC")
        sc.wait("vector", "SC", sq)
        fv = sc.op("vector", lambda e: e.tensor_scalar(out=st[:, 9:10], in0=st[:, 8:9],
                                                       scalar1=1.0 / 1024.0, scalar2=1e-5,
                                                       op0=ALU.mult, op1=ALU.add), inc="V")
        sc.wait("vector", "V", fv)
        vv = sc.op("vector", lambda e: e.tensor_sub(out=st[:, 10:11], in0=st[:, 9:10],
                                                    in1=st[:, 7:8]), inc="V")
        sc.wait("scalar", "V", vv)
        sc.op("scalar", lambda e: e.activation(out=st[:, 11:12], in_=st[:, 10:11],
                                               func=AF.Ln, scale=1.0))
        rs = sc.op("scalar", lambda e: e.activation(out=st[:, 12:13], in_=st[:, 11:12],
                                                    func=AF.Exp, scale=-0.5), inc="SC")
        sc.wait("vector", "SC", rs)
        nm = sc.op("vector", lambda e: e.scalar_tensor_tensor(out=st[:, 13:14], in0=st[:, 6:7],
                                                              scalar=-1.0, in1=st[:, 12:13],
                                                              op0=ALU.mult, op1=ALU.mult),
                   inc="V")
        sc.wait("scalar", "V", nm)
        if t.get("dbg_p") is not None and s == 0 and qt == 0:
            dv2 = sc.op("vector", lambda e: e.tensor_copy(out=c["dbgbuf2"][:, 1024:1040],
                                                          in_=st[:, 0:16]), inc="V")
            sc.wait("gpsimd", "V", dv2)
            sc.op("gpsimd", lambda e: e.dma_start(out=t["dbg_s"][0:128, 0:1040].cast(F32)
                                                  if False else t["dbg_s2"][0:128, :],
                                                  in_=c["dbgbuf2"][:, 0:1040]), inc="LG")
        if qt >= 2:
            sc.wait("scalar", "Lh%d" % par, hstore[qt - 2])
        hv = sc.op("scalar", (lambda par=par:
                              lambda e: e.activation(out=hst[par][:], in_=rbuf, func=AF.Identity,
                                                     scale=st[:, 12:13], bias=st[:, 13:14]))(),
                   inc="SC")
        sc.wait("gpsimd", "SC", hv)
        hstore[qt] = sc.op("gpsimd", (lambda r0=r0, par=par:
                                      lambda e: e.dma_start(out=t["h_s"][r0:r0 + 128, :],
                                                            in_=hst[par][:]))(), inc="Lh%d" % par)
        sc.wait("vector", "SC", hv)
        rv = sc.op("vector", (lambda par=par:
                              lambda e: e.tensor_reduce(out=st[:, 14:15], in_=hst[par][:],
                                                        axis=AX.X, op=ALU.max,
                                                        apply_absolute_value=True))(), inc="V")
        sc.wait("vector", "V", rv)
        fv = sc.op("vector", lambda e: e.tensor_scalar(out=st[:, 14:15], in0=st[:, 14:15],
                                                       scalar1=1e-5, scalar2=None,
                                                       op0=ALU.max), inc="V")
        sc.wait("vector", "V", fv)
        sc.op("vector", (lambda gc=gc:
                         lambda e: e.tensor_scalar(out=t["rs_h"][:, gc:gc + 1],
                                                   in0=st[:, 14:15], scalar1=1.0 / 127.0,
                                                   scalar2=None, op0=ALU.mult))())
        rv = sc.op("vector", lambda e: e.reciprocal(out=st[:, 15:16], in_=st[:, 14:15]),
                   inc="V")
        sc.wait("vector", "V", rv)
        fv = sc.op("vector", lambda e: e.tensor_scalar(out=st[:, 15:16], in0=st[:, 15:16],
                                                       scalar1=127.0, scalar2=None,
                                                       op0=ALU.mult), inc="V")
        sc.wait("vector", "V", fv)
        sc.op("vector", (lambda par=par:
                         lambda e: e.tensor_scalar(out=rbuf, in0=hst[par][:],
                                                   scalar1=st[:, 15:16], scalar2=None,
                                                   op0=ALU.mult))())
        sc.op("vector", lambda e: e.tensor_scalar(out=rbuf, in0=rbuf,
                                                  scalar1=t["mgk"][:, 0:1], scalar2=None,
                                                  op0=ALU.add))
        sc.op("vector", lambda e: e.tensor_scalar(out=rbuf, in0=rbuf,
                                                  scalar1=t["mgk"][:, 1:2],
                                                  scalar2=127.0, op0=ALU.add, op1=ALU.min))
        sc.wait("vector", "T", hqT_done.get(qt - 2, 0))
        hb = sc.op("vector", (lambda par=par:
                              lambda e: e.tensor_scalar(out=hqb[par], in0=rbuf,
                                                        scalar1=-128.0,
                                                        scalar2=None, op0=ALU.max))(),
                   inc="V")
        hb_of[qt] = hb
        if t.get("dbg_st") is not None:
            sc.op("gpsimd", (lambda r0=r0:
                             lambda e: e.dma_start(out=t["dbg_st"][r0:r0 + 128, :],
                                                   in_=st[:, 0:16]))(), inc="LG")
            if s == 0 and qt == 20:
                sc.op("gpsimd", lambda e: e.dma_start(out=t["dbg_p"], in_=Psb), inc="LG")
                sc.op("gpsimd", lambda e: e.dma_start(out=t["dbg_s"], in_=Ssb), inc="LG")
    hqT_tensor(TC - 1)
    hqT_evac(TC - 1)



def phase_b1(nc, sc, t, c):
    wgq, wuq = c["wgq"], c["wuq"]
    hqT, sg = c["hqT"], c["sg"]
    ybf, yqb = c["ybf"], c["yqstage"]
    yqbf = c["yqbf"]
    pg, pu, psT = c["pg"], c["pu"], c["psTb"]
    stb = c["statsb"]

    for dcs in range(DC):
        sc.op("sync", (lambda dcs=dcs:
                       lambda e: e.dma_start(out=wgq[:, dcs * 4096:(dcs + 1) * 4096],
                                             in_=t["wgq_s"][dcs * 128:(dcs + 1) * 128, :]))(),
              inc="LWB1")
        sc.op("sync", (lambda dcs=dcs:
                       lambda e: e.dma_start(out=wuq[:, dcs * 4096:(dcs + 1) * 4096],
                                             in_=t["wuq_s"][dcs * 128:(dcs + 1) * 128, :]))(),
              inc="LWB1")
    sc.wait("tensor", "LWB1", sc.count("LWB1"))

    b1mm = {}
    ymul = {}
    yb_done = {}
    yst = {}
    last_psT = {}

    def emit_windows(gc):
        r0 = gc * 128
        par = gc % 2
        if gc >= 2:
            sc.wait("sync", "T", b1mm[gc - 2])
        hld = sc.op("sync", (lambda r0=r0, par=par:
                             lambda e: e.dma_start(out=hqT[par][:],
                                                   in_=t["hq_s"][r0:r0 + 128, :]))(),
                    inc="Lhqt%d" % par)
        sc.wait("tensor", "Lhqt%d" % par, hld)
        sc.op("vector", (lambda gc=gc:
                         lambda e: e.tensor_mul(out=stb[:, 0:1], in0=t["rs_h"][:, gc:gc + 1],
                                                in1=t["wmean"][:, 0:1]))())
        cgv = sc.op("vector", (lambda gc=gc:
                               lambda e: e.tensor_mul(out=stb[:, 1:2],
                                                      in0=t["rs_h"][:, gc:gc + 1],
                                                      in1=t["wmean"][:, 1:2]))(), inc="V")
        sc.wait("vector", "V", cgv)
        for win in range(8):
            wi = gc * 8 + win
            rot = win % 3
            sc.wait("tensor", "V", ymul.get(wi - 3, 0))
            tt = None
            for wq_, ps_ in ((wgq, pg), (wuq, pu)):
                for dcs in range(DC):
                    last = dcs == DC - 1 and ps_ is pu
                    tt = sc.op("tensor",
                               (lambda wq_=wq_, ps_=ps_, rot=rot, win=win, dcs=dcs, par=par:
                                lambda e: e.matmul(ps_[rot].ap(),
                                                   hqT[par][:, dcs * 128:(dcs + 1) * 128],
                                                   wq_[:, dcs * 4096 + win * 512:
                                                       dcs * 4096 + win * 512 + 512],
                                                   start=(dcs == 0), stop=(dcs == DC - 1)))(),
                               inc="T" if last else None)
            sc.wait("scalar", "T", tt)
            if win == 0:
                sc.wait("scalar", "V", cgv)
            sl = sc.op("scalar", (lambda rot=rot:
                                  lambda e: e.activation(out=sg[rot], in_=pg[rot].ap(),
                                                         func=AF.Silu, scale=stb[:, 0:1]))(),
                       inc="SC")
            sc.wait("vector", "SC", sl)
            ymul[wi] = sc.op("vector",
                             (lambda rot=rot, win=win:
                              lambda e: e.scalar_tensor_tensor(
                                  out=ybf[:, win * 512:(win + 1) * 512],
                                  in0=pu[rot].ap(), scalar=stb[:, 1:2], in1=sg[rot],
                                  op0=ALU.mult, op1=ALU.mult))(), inc="V")
            wred[gc] = sc.op("vector", (lambda win=win:
                                        lambda e: e.tensor_reduce(
                                            out=stb[:, 8 + win:9 + win],
                                            in_=ybf[:, win * 512:(win + 1) * 512],
                                            axis=AX.X, op=ALU.max,
                                            apply_absolute_value=True))(), inc="V")
        b1mm[gc] = sc.count("T")

    def emit_quant(gc):
        sc.wait("vector", "V", wred[gc])
        rv = sc.op("vector", lambda e: e.tensor_reduce(out=stb[:, 2:3], in_=stb[:, 8:16],
                                                       axis=AX.X, op=ALU.max), inc="V")
        sc.wait("vector", "V", rv)
        fv = sc.op("vector", lambda e: e.tensor_scalar(out=stb[:, 2:3], in0=stb[:, 2:3],
                                                       scalar1=1e-5, scalar2=None,
                                                       op0=ALU.max), inc="V")
        sc.wait("vector", "V", fv)
        sc.op("vector", (lambda gc=gc:
                         lambda e: e.tensor_scalar(out=t["rs_y"][:, gc:gc + 1],
                                                   in0=stb[:, 2:3], scalar1=1.0 / 127.0,
                                                   scalar2=None, op0=ALU.mult))())
        rv = sc.op("vector", lambda e: e.reciprocal(out=stb[:, 3:4], in_=stb[:, 2:3]),
                   inc="V")
        sc.wait("vector", "V", rv)
        fv = sc.op("vector", lambda e: e.tensor_scalar(out=stb[:, 3:4], in0=stb[:, 3:4],
                                                       scalar1=127.0, scalar2=None,
                                                       op0=ALU.mult), inc="V")
        sc.wait("scalar", "V", fv)
        sc.wait("scalar", "V", ymul[gc * 8 + 7])
        sc.wait("scalar", "V", p2_done.get(gc - 1, 0))
        p1 = sc.op("scalar", lambda e: e.activation(out=c["yf32"], in_=ybf,
                                                    func=AF.Identity,
                                                    scale=stb[:, 3:4]), inc="SC")
        sc.wait("vector", "SC", p1)
        sc.op("vector", lambda e: e.tensor_scalar(out=c["yf32"], in0=c["yf32"],
                                                  scalar1=t["mgk"][:, 0:1], scalar2=None,
                                                  op0=ALU.add))
        if gc >= 2:
            sc.wait("vector", "T", tq_done.get(gc - 2, 0))
        yb_done[gc] = sc.op("vector", (lambda gc=gc:
                                       lambda e: e.tensor_scalar(out=yqbf[gc % 2],
                                                                 in0=c["yf32"],
                                                                 scalar1=t["mgk"][:, 1:2],
                                                                 scalar2=None,
                                                                 op0=ALU.add))(), inc="V")
        p2_done[gc] = yb_done[gc]

    tq_done = {}
    wred = {}
    p2_done = {}

    def emit_yq_T(gc):
        r0 = gc * 128
        sc.wait("tensor", "V", yb_done[gc])
        for b in range(8):
            rot = b % 2
            sc.wait("tensor", "V", last_psT.get(rot, 0))
            for u in range(4):
                icx = 4 * b + u
                tt = sc.op("tensor", (lambda rot=rot, icx=icx, u=u, gc=gc:
                                      lambda e: e.transpose(psT[rot].ap()[:, u * 128:(u + 1) * 128],
                                                            yqbf[gc % 2][:, icx * 128:(icx + 1) * 128],
                                                            t["identb"]))(),
                           inc="T" if u == 3 else None)
            if b == 7:
                tq_done[gc] = tt
            sc.wait("vector", "T", tt)
            if b == 0 and gc >= 1:
                sc.wait("vector", "Lyq", yst[gc - 1])
            pe = sc.op("vector", (lambda rot=rot, b=b:
                                  lambda e: e.tensor_copy(out=yqb[:, b * 512:(b + 1) * 512],
                                                          in_=psT[rot].ap()))(), inc="V")
            last_psT[rot] = pe
        sc.wait("gpsimd", "V", pe)
        yst[gc] = sc.op("gpsimd", (lambda r0=r0:
                                   lambda e: e.dma_start(out=t["yq_s"][r0:r0 + 128, :],
                                                         in_=yqb))(), inc="Lyq")

    import os as _os
    _gcl = int(_os.environ.get("B1GC", GC))
    for gc in range(_gcl):
        emit_windows(gc)
        if gc >= 1:
            emit_yq_T(gc - 1)
        emit_quant(gc)
    emit_yq_T(_gcl - 1)



def phase_b2(nc, sc, t, c):
    wdq = c["wdq"]
    yqT, hres, ost = c["yqT"], c["hres"], c["ostage"]
    r2, sqd2 = c["r2"], c["sqd2"]
    pd = c["pd"]
    stb = c["statsd"]

    for icx in range(IC):
        sc.op("sync", (lambda icx=icx:
                       lambda e: e.dma_start(out=wdq[:, icx * 1024:(icx + 1) * 1024],
                                             in_=t["wdq_s"][icx * 128:(icx + 1) * 128, :]))(),
              inc="LWB2")
    sc.wait("tensor", "LWB2", sc.count("LWB2"))

    dmm, ep2, odone, ostore = {}, {}, {}, {}
    for gc in range(GC):
        r0 = gc * 128
        par = gc % 3
        rot = gc % 3
        sb = (gc % 2) * 8
        if gc >= 3:
            sc.wait("sync", "T", dmm[gc - 3])
            sc.wait("sync", "V", ep2[gc - 3])
        yl = sc.op("sync", (lambda r0=r0, par=par:
                            lambda e: e.dma_start(out=yqT[par][:],
                                                  in_=t["yq_s"][r0:r0 + 128, :]))(), inc="Lyqt%d" % par)
        hl = sc.op("sync", (lambda r0=r0, par=par:
                            lambda e: e.dma_start(out=hres[par][:],
                                                  in_=t["h_s"][r0:r0 + 128, :]))(), inc="Lhr%d" % par)
        sc.wait("tensor", "Lyqt%d" % par, yl)
        sc.wait("tensor", "V", ep2.get(gc - 3, 0))
        tt = None
        for j in range(2):
            for icx in range(IC):
                last = icx == IC - 1 and j == 1
                tt = sc.op("tensor", (lambda rot=rot, j=j, icx=icx, par=par:
                                      lambda e: e.matmul(pd[rot].ap()[:, j * 512:(j + 1) * 512],
                                                         yqT[par][:, icx * 128:(icx + 1) * 128],
                                                         wdq[:, icx * 1024 + j * 512:
                                                             icx * 1024 + j * 512 + 512],
                                                         start=(icx == 0),
                                                         stop=(icx == IC - 1)))(),
                           inc="T" if last else None)
        dmm[gc] = tt
        cdv = sc.op("vector", (lambda gc=gc:
                               lambda e: e.tensor_mul(out=stb[:, sb + 0:sb + 1],
                                                      in0=t["rs_y"][:, gc:gc + 1],
                                                      in1=t["wmean"][:, 2:3]))(), inc="V")
        sc.wait("vector", "T", dmm[gc])
        sc.wait("vector", "Lhr%d" % par, hl)
        sc.wait("vector", "V", cdv)
        sc.wait("vector", "SC", odone.get(gc - 2, 0))
        r2c = r2[gc % 2]
        ep = sc.op("vector", (lambda rot=rot, par=par, r2c=r2c:
                              lambda e: e.scalar_tensor_tensor(out=r2c, in0=pd[rot].ap(),
                                                               scalar=stb[:, sb + 0:sb + 1],
                                                               in1=hres[par][:],
                                                               op0=ALU.mult, op1=ALU.add,
                                                               accum_out=stb[:, sb + 1:sb + 2]))(),
                   inc="V")
        ep2[gc] = ep
        sc.wait("vector", "V", ep)
        fv = sc.op("vector", lambda e: e.tensor_scalar(out=stb[:, sb + 2:sb + 3], in0=stb[:, sb + 1:sb + 2],
                                                       scalar1=1.0 / 1024.0, scalar2=None,
                                                       op0=ALU.mult), inc="V")
        sc.wait("vector", "V", fv)
        mv = sc.op("vector", lambda e: e.tensor_mul(out=stb[:, sb + 3:sb + 4], in0=stb[:, sb + 2:sb + 3],
                                                    in1=stb[:, sb + 2:sb + 3]), inc="V")
        sc.wait("scalar", "V", mv)
        sq = sc.op("scalar", (lambda r2c=r2c:
                              lambda e: e.activation(out=sqd2, in_=r2c, func=AF.Square,
                                                     accum_out=stb[:, sb + 4:sb + 5]))(), inc="SC")
        sc.wait("vector", "SC", sq)
        fv = sc.op("vector", lambda e: e.tensor_scalar(out=stb[:, sb + 5:sb + 6], in0=stb[:, sb + 4:sb + 5],
                                                       scalar1=1.0 / 1024.0, scalar2=1e-5,
                                                       op0=ALU.mult, op1=ALU.add), inc="V")
        sc.wait("vector", "V", fv)
        vv = sc.op("vector", lambda e: e.tensor_sub(out=stb[:, sb + 5:sb + 6], in0=stb[:, sb + 5:sb + 6],
                                                    in1=stb[:, sb + 3:sb + 4]), inc="V")
        sc.wait("scalar", "V", vv)
        sc.op("scalar", lambda e: e.activation(out=stb[:, sb + 6:sb + 7], in_=stb[:, sb + 5:sb + 6],
                                               func=AF.Ln, scale=1.0))
        rs = sc.op("scalar", lambda e: e.activation(out=stb[:, sb + 7:sb + 8], in_=stb[:, sb + 6:sb + 7],
                                                    func=AF.Exp, scale=-0.5), inc="SC")
        sc.wait("vector", "SC", rs)
        nm = sc.op("vector", lambda e: e.scalar_tensor_tensor(out=stb[:, sb + 2:sb + 3], in0=stb[:, sb + 2:sb + 3],
                                                              scalar=-1.0, in1=stb[:, sb + 7:sb + 8],
                                                              op0=ALU.mult, op1=ALU.mult),
                   inc="V")
        sc.wait("scalar", "V", nm)
        if gc >= 3:
            sc.wait("scalar", "Lout%d" % par, ostore[gc - 3])
        ov = sc.op("scalar", (lambda par=par, r2c=r2c:
                              lambda e: e.activation(out=ost[par][:], in_=r2c,
                                                     func=AF.Identity,
                                                     scale=stb[:, sb + 7:sb + 8], bias=stb[:, sb + 2:sb + 3]))(),
                   inc="SC")
        odone[gc] = ov
        sc.wait("gpsimd", "SC", ov)
        ostore[gc] = sc.op("gpsimd", (lambda r0=r0, par=par:
                                      lambda e: e.dma_start(out=t["out"][r0:r0 + 128, :],
                                                            in_=ost[par][:]))(), inc="Lout%d" % par)
    sc.wait("sync", "Lout0", sc.count("Lout0"))
    sc.wait("sync", "Lout1", sc.count("Lout1"))
    sc.wait("sync", "Lout2", sc.count("Lout2"))



def build_nc(dbg=False, phases=("Q", "A", "B1", "B2"), nseq=SEQ):
    nc = bass.Bass("TRN2", target_bir_lowering=False, detect_race_conditions=False)
    skind = "ExternalOutput" if dbg else "Internal"
    t = {}
    t["x_in"] = nc.dram_tensor("x_in", [SEQ * S, D], F32, kind="ExternalInput").ap()
    for nm, shp in (("wqT", [D, D]), ("wkT", [D, D]), ("wvT", [D, D]),
                    ("wgT", [D, INNER]), ("wuT", [D, INNER]), ("wdT", [INNER, D])):
        t[nm] = nc.dram_tensor(nm, shp, F32, kind="ExternalInput").ap()
    t["brows"] = nc.dram_tensor("brows", [1, 3 * D], F32, kind="ExternalInput").ap()
    t["ident_in"] = nc.dram_tensor("ident_in", [128, 128], BF16, kind="ExternalInput").ap()
    t["onesf_in"] = nc.dram_tensor("onesf_in", [128, 128], F32, kind="ExternalInput").ap()
    t["onesb_in"] = nc.dram_tensor("onesb_in", [1, 512], BF16, kind="ExternalInput").ap()
    t["mask_in"] = nc.dram_tensor("mask_in", [128, 128], F32, kind="ExternalInput").ap()
    t["out"] = nc.dram_tensor("out", [SEQ * S, D], F32, kind="ExternalOutput").ap()
    t["qt_s"] = nc.dram_tensor("qt_s", [SEQ * S, D], BF16, kind=skind).ap()
    if dbg:
        t["dbg_st"] = nc.dram_tensor("dbg_st", [SEQ * S, 16], F32, kind=skind).ap()
        t["dbg_p"] = nc.dram_tensor("dbg_p", [128, 4096], BF16, kind=skind).ap()
        t["dbg_s"] = nc.dram_tensor("dbg_s", [128, 4096], BF16, kind=skind).ap()
        t["dbg_s2"] = nc.dram_tensor("dbg_s2", [128, 1040], F32, kind=skind).ap()
    t["h_s"] = nc.dram_tensor("h_s", [SEQ * S, D], F32, kind=skind).ap()
    t["hq_s"] = nc.dram_tensor("hq_s", [SEQ * S, D], BF16, kind=skind).ap()
    t["yq_s"] = nc.dram_tensor("yq_s", [SEQ * S, INNER], BF16, kind=skind).ap()
    t["wgq_s"] = nc.dram_tensor("wgq_s", [D, INNER], BF16, kind=skind).ap()
    t["wuq_s"] = nc.dram_tensor("wuq_s", [D, INNER], BF16, kind=skind).ap()
    t["wdq_s"] = nc.dram_tensor("wdq_s", [INNER, D], BF16, kind=skind).ap()

    counts = {}
    with ExitStack() as top:
        E = top.enter_context
        sem_names = ["LX", "LG", "SC", "V", "T", "Lconst", "LW", "Lqst", "Lyq",
                     "LWB1", "LWB2"]
        for pfx in ("Lstg", "Lqbf", "Lxb", "Lqt", "Lxq", "Lh", "Lhq", "Lhqt",
                    "Lyqt", "Lhr", "Lout"):
            sem_names += [pfx + "0", pfx + "1"]
        sem_names += ["Lyqt2", "Lhr2", "Lout2"]
        sems = {s_: E(nc.semaphore(s_)) for s_ in sem_names}
        ident = E(nc.sbuf_tensor("ident", [128, 128], BF16))
        maskf = E(nc.sbuf_tensor("maskf", [128, 128], F32))
        onesf = E(nc.sbuf_tensor("onesf", [128, 128], F32))
        onesb = E(nc.sbuf_tensor("onesb", [1, 512], BF16))
        t["wsc"] = E(nc.sbuf_tensor("wsc", [128, 4], F32)).ap()
        t["wmean"] = E(nc.sbuf_tensor("wmean", [128, 4], F32)).ap()
        t["rs_h"] = E(nc.sbuf_tensor("rs_h", [128, GC], F32)).ap()
        t["rs_y"] = E(nc.sbuf_tensor("rs_y", [128, GC], F32)).ap()
        t["mgk"] = E(nc.sbuf_tensor("mgk", [128, 4], F32)).ap()
        t["identb"] = ident.ap()

        if "Q" not in phases:
            with ExitStack() as ph:
                P = ph.enter_context
                sc = Sched(counts)
                sc.op("sync", lambda e: e.dma_start(out=ident.ap(), in_=t["ident_in"]),
                      inc="Lconst")
                sc.op("sync", lambda e: e.dma_start(out=maskf.ap(), in_=t["mask_in"]),
                      inc="Lconst")
                sc.op("sync", lambda e: e.dma_start(out=onesf.ap(), in_=t["onesf_in"]),
                      inc="Lconst")
                sc.op("sync", lambda e: e.dma_start(out=onesb.ap(), in_=t["onesb_in"]),
                      inc="Lconst")
                run_block(nc, sc, sems)
                prev = snap(counts)
        if True and "Q" in phases:
         with ExitStack() as ph:
            P = ph.enter_context
            c = {}
            c["q_stg"] = [P(nc.sbuf_tensor(f"q_stg{i}", [128, 4096], F32)).ap() for i in range(2)]
            c["q_bf"] = [P(nc.sbuf_tensor(f"q_bf{i}", [128, 4096], BF16)).ap() for i in range(2)]
            c["q_macc"] = P(nc.sbuf_tensor("q_macc", [128, 4], F32)).ap()
            c["q_tmp"] = P(nc.sbuf_tensor("q_tmp", [128, 4], F32)).ap()
            c["q_s11"] = P(nc.sbuf_tensor("q_s11", [1, 4], F32)).ap()
            c["ones_col"] = onesf.ap()[:, 0:1]
            c["ones_rowf"] = onesf.ap()[0:1, 0:128]
            c["q_p11"] = P(nc.psum_tensor("q_p11", [1, 1], F32))
            c["q_pbc"] = P(nc.psum_tensor("q_pbc", [128, 1], F32))
            c["q_pbc2"] = P(nc.psum_tensor("q_pbc2", [128, 1], F32))
            sc = Sched(counts)
            sc.op("sync", lambda e: e.dma_start(out=ident.ap(), in_=t["ident_in"]), inc="Lconst")
            sc.op("sync", lambda e: e.dma_start(out=maskf.ap(), in_=t["mask_in"]), inc="Lconst")
            sc.op("sync", lambda e: e.dma_start(out=onesf.ap(), in_=t["onesf_in"]), inc="Lconst")
            lcv = sc.op("sync", lambda e: e.dma_start(out=onesb.ap(), in_=t["onesb_in"]),
                        inc="Lconst")
            sc.wait("vector", "Lconst", lcv)
            sc.wait("tensor", "Lconst", lcv)
            mg = t["mgk"]
            fv = sc.op("vector", lambda e: e.tensor_scalar(out=mg[:, 2:3],
                                                           in0=onesf.ap()[:, 0:1],
                                                           scalar1=1e-30, scalar2=None,
                                                           op0=ALU.mult), inc="V")
            sc.wait("vector", "V", fv)
            fv = sc.op("vector", lambda e: e.tensor_scalar(out=mg[:, 2:3], in0=mg[:, 2:3],
                                                           scalar1=1e-30, scalar2=None,
                                                           op0=ALU.mult), inc="V")
            sc.wait("vector", "V", fv)
            fv = sc.op("vector", lambda e: e.tensor_scalar(out=mg[:, 3:4], in0=mg[:, 2:3],
                                                           scalar1=0.0, scalar2=None,
                                                           op0=ALU.is_gt), inc="V")
            sc.wait("vector", "V", fv)
            fv = sc.op("vector", lambda e: e.tensor_scalar(out=mg[:, 0:1], in0=mg[:, 3:4],
                                                           scalar1=M64 - M32, scalar2=M32,
                                                           op0=ALU.mult, op1=ALU.add),
                       inc="V")
            sc.wait("vector", "V", fv)
            fv = sc.op("vector", lambda e: e.tensor_scalar(out=mg[:, 1:2], in0=mg[:, 0:1],
                                                           scalar1=-1.0, scalar2=None,
                                                           op0=ALU.mult), inc="V")
            sc.wait("vector", "V", fv)
            phase_quant(nc, sc, t, c)
            run_block(nc, sc, sems)
            prev = snap(counts)

        with ExitStack() as pha:
            P = pha.enter_context
            a = {}
            a["KT"] = P(nc.sbuf_tensor("KT", [128, DC * 4096], BF16)).ap()
            a["V"] = P(nc.sbuf_tensor("Vres", [128, TC * 1024], BF16)).ap()
            a["ident"] = ident.ap()
            a["maskf"] = maskf.ap()
            a["ones512"] = onesb.ap()
            for s_ in range(nseq):
                with ExitStack() as ph1:
                    P1 = ph1.enter_context
                    c = {}
                    a["W"] = P1(nc.sbuf_tensor(f"Wqkv{s_}", [128, 3 * 8192], BF16)).ap()
                    a["brows"] = P1(nc.sbuf_tensor(f"browsb{s_}", [1, 3 * D], BF16)).ap()
                    c["xb"] = [P1(nc.sbuf_tensor(f"xb{i}_{s_}", [128, D], BF16))
                               for i in range(2)]
                    c["xT"] = [P1(nc.sbuf_tensor(f"xT{i}_{s_}", [128, 2048], BF16)).ap()
                               for i in range(2)]
                    c["qstage"] = P1(nc.sbuf_tensor(f"qstage{s_}", [128, 2048], BF16)).ap()
                    c["ptx"] = [P1(nc.psum_tensor(f"ptx{i}_{s_}", [128, 1024], BF16))
                                for i in range(2)]
                    c["psA"] = [P1(nc.psum_tensor(f"psA{i}_{s_}", [128, 512], F32))
                                for i in range(2)]
                    c["psV"] = [P1(nc.psum_tensor(f"psV{i}_{s_}", [128, 1024], F32))
                                for i in range(2)]
                    sc = Sched(counts)
                    for e_ in ENGINES:
                        sc.wait_all(e_, prev)
                    for proj, wnm in ((0, "wqT"), (1, "wkT"), (2, "wvT")):
                        for dcs in range(DC):
                            sc.op("gpsimd",
                                  (lambda proj=proj, wnm=wnm, dcs=dcs:
                                   lambda e: e.dma_start(
                                       out=a["W"][:, proj * 8192 + dcs * 1024:
                                                  proj * 8192 + (dcs + 1) * 1024],
                                       in_=t[wnm][dcs * 128:(dcs + 1) * 128, :]))(),
                                  inc="LW")
                    sc.op("gpsimd", lambda e: e.dma_start(out=a["brows"], in_=t["brows"]),
                          inc="LW")
                    sc.wait("tensor", "LW", sc.count("LW"))
                    phase_pass1(nc, sc, t, c, a, s_)
                    run_block(nc, sc, sems)
                    prev = snap(counts)
                with ExitStack() as ph2:
                    P2 = ph2.enter_context
                    c = {}
                    c["qtile"] = [P2(nc.sbuf_tensor(f"qtile{i}_{s_}", [128, D], BF16))
                                  for i in range(2)]
                    c["xq"] = [P2(nc.sbuf_tensor(f"xq{i}_{s_}", [128, D], F32))
                               for i in range(2)]
                    c["Ssb"] = P2(nc.sbuf_tensor(f"Ssb{s_}", [128, 4096], BF16)).ap()
                    c["Psb"] = P2(nc.sbuf_tensor(f"Psb{s_}", [128, 4096], BF16)).ap()
                    c["PTs"] = P2(nc.sbuf_tensor(f"PTs{s_}", [128, 1024], BF16)).ap()
                    c["stats"] = P2(nc.sbuf_tensor(f"stats{s_}", [128, 16], F32)).ap()
                    c["hstage"] = [P2(nc.sbuf_tensor(f"hst{i}_{s_}", [128, D], F32))
                                   for i in range(2)]
                    c["hqstage"] = [P2(nc.sbuf_tensor(f"hqst{i}_{s_}", [128, D], BF16))
                                    for i in range(2)]
                    c["hqb"] = [P2(nc.sbuf_tensor(f"hqb{i}_{s_}", [128, D], BF16)).ap()
                                for i in range(2)]
                    c["rbuf"] = P2(nc.sbuf_tensor(f"rbuf{s_}", [128, D], F32)).ap()
                    c["dbgbuf"] = P2(nc.sbuf_tensor(f"dbgbuf{s_}", [128, 272], BF16)).ap()
                    c["dbgbuf2"] = P2(nc.sbuf_tensor(f"dbgbuf2{s_}", [128, 1040], F32)).ap()
                    c["sqdump"] = c["Ssb"][:, 0:1024]
                    c["psS"] = [P2(nc.psum_tensor(f"psS{i}_{s_}", [128, 512], F32))
                                for i in range(2)]
                    c["psT"] = [P2(nc.psum_tensor(f"psTq{i}_{s_}", [128, 512], BF16))
                                for i in range(2)]
                    c["psO"] = P2(nc.psum_tensor(f"psO{s_}", [128, 1024], F32))
                    sc = Sched(counts)
                    for e_ in ENGINES:
                        sc.wait_all(e_, prev)
                    phase_qloop(nc, sc, t, c, a, s_)
                    run_block(nc, sc, sems)
                    prev = snap(counts)

        if "B1" in phases:
         with ExitStack() as ph:
            P = ph.enter_context
            c = {}
            c["wgq"] = P(nc.sbuf_tensor("wgq", [128, DC * 4096], BF16)).ap()
            c["wuq"] = P(nc.sbuf_tensor("wuq", [128, DC * 4096], BF16)).ap()
            c["hqT"] = [P(nc.sbuf_tensor(f"hqT{i}", [128, D], BF16)) for i in range(2)]
            c["sg"] = [P(nc.sbuf_tensor(f"sg{i}", [128, 512], BF16)).ap()
                       for i in range(3)]
            c["ybf"] = P(nc.sbuf_tensor("ybf", [128, INNER], BF16)).ap()
            c["yqbf"] = [P(nc.sbuf_tensor(f"yqbf{i}", [128, INNER], BF16)).ap()
                         for i in range(2)]
            c["yqstage"] = P(nc.sbuf_tensor("yqstage", [128, INNER], BF16)).ap()
            c["yf32"] = P(nc.sbuf_tensor("yf32", [128, INNER], F32)).ap()
            c["statsb"] = P(nc.sbuf_tensor("statsb", [128, 16], F32)).ap()
            c["pg"] = [P(nc.psum_tensor(f"pg{i}", [128, 512], F32)) for i in range(3)]
            c["pu"] = [P(nc.psum_tensor(f"pu{i}", [128, 512], F32)) for i in range(3)]
            c["psTb"] = [P(nc.psum_tensor(f"psTb{i}", [128, 512], BF16)) for i in range(2)]
            sc = Sched(counts)
            for e_ in ENGINES:
                sc.wait_all(e_, prev)
            phase_b1(nc, sc, t, c)
            run_block(nc, sc, sems)
            prev = snap(counts)

        if "B2" in phases:
         with ExitStack() as ph:
            P = ph.enter_context
            c = {}
            c["wdq"] = P(nc.sbuf_tensor("wdq", [128, IC * 1024], BF16)).ap()
            c["yqT"] = [P(nc.sbuf_tensor(f"yqT{i}", [128, INNER], BF16)) for i in range(3)]
            c["hres"] = [P(nc.sbuf_tensor(f"hres{i}", [128, D], F32)) for i in range(3)]
            c["ostage"] = [P(nc.sbuf_tensor(f"ost{i}", [128, D], F32)) for i in range(3)]
            c["r2"] = [P(nc.sbuf_tensor(f"r2_{i}", [128, D], F32)).ap() for i in range(2)]
            c["sqd2"] = P(nc.sbuf_tensor("sqd2", [128, D], F32)).ap()
            c["statsd"] = P(nc.sbuf_tensor("statsd", [128, 16], F32)).ap()
            c["pd"] = [P(nc.psum_tensor(f"pd{i}", [128, 1024], F32)) for i in range(3)]
            sc = Sched(counts)
            for e_ in ENGINES:
                sc.wait_all(e_, prev)
            phase_b2(nc, sc, t, c)
            run_block(nc, sc, sems)
    return nc


def make_consts():
    ident = np.eye(128, dtype=ml_dtypes.bfloat16)
    r = np.arange(128)
    mask = np.where(r[None, :] > r[:, None], np.float32(MASKV),
                    np.float32(0.0)).astype(np.float32)
    return ident, mask


def kernel(L0_attn_x, L0_attn_Wq, L0_attn_bq, L0_attn_Wk, L0_attn_bk, L0_attn_Wv,
           L0_attn_bv, L0_ffn_w_gate, L0_ffn_w_up, L0_ffn_w_down):
    x = np.ascontiguousarray(np.asarray(L0_attn_x, dtype=np.float32))
    wqT = np.ascontiguousarray(np.asarray(L0_attn_Wq, dtype=np.float32).T)
    wkT = np.ascontiguousarray(np.asarray(L0_attn_Wk, dtype=np.float32).T)
    wvT = np.ascontiguousarray(np.asarray(L0_attn_Wv, dtype=np.float32).T)
    wgT = np.ascontiguousarray(np.asarray(L0_ffn_w_gate, dtype=np.float32).T)
    wuT = np.ascontiguousarray(np.asarray(L0_ffn_w_up, dtype=np.float32).T)
    wdT = np.ascontiguousarray(np.asarray(L0_ffn_w_down, dtype=np.float32).T)
    brows = np.concatenate([np.asarray(L0_attn_bq, dtype=np.float32),
                            np.asarray(L0_attn_bk, dtype=np.float32),
                            np.asarray(L0_attn_bv, dtype=np.float32)]).reshape(1, 3 * D)
    ident, mask = make_consts()
    onesf = np.ones((128, 128), dtype=np.float32)
    onesb = np.ones((1, 512), dtype=ml_dtypes.bfloat16)

    nc = build_nc()
    in_maps = []
    for cid in range(NCORES):
        xs = np.ascontiguousarray(x[SEQ * cid:SEQ * (cid + 1)].reshape(SEQ * S, D))
        in_maps.append({"x_in": xs, "wqT": wqT, "wkT": wkT, "wvT": wvT,
                        "wgT": wgT, "wuT": wuT, "wdT": wdT, "brows": brows,
                        "ident_in": ident, "mask_in": mask,
                        "onesf_in": onesf, "onesb_in": onesb})
    res = run_bass_kernel_spmd(nc, in_maps, core_ids=list(range(NCORES)))
    out = np.concatenate([r["out"].reshape(SEQ, S, D) for r in res.results], axis=0)
    return out


# version 09 (final)
# speedup vs baseline: 1.5708x; 1.0218x over previous
import numpy as np
import ml_dtypes
from contextlib import ExitStack

import concourse.bass as bass
import concourse.mybir as mybir
from concourse.bass_utils import run_bass_kernel_spmd

F32 = mybir.dt.float32
BF16 = mybir.dt.bfloat16
I8 = mybir.dt.int8
AF = mybir.ActivationFunctionType
ALU = mybir.AluOpType
AX = mybir.AxisListType

NCORES = 8
SEQ = 2
S = 4096
D = 1024
INNER = 4096
TC = S // 128
GC = SEQ * TC
DC = D // 128
IC = INNER // 128
MASKV = -2500.0
M32 = 12582912.0
M64 = 6755399441055744.0

ENGINES = ("sync", "gpsimd", "scalar", "vector", "tensor")
ESEM = {"sync": "LX", "gpsimd": "LG", "scalar": "SC", "vector": "V", "tensor": "T"}
def _sem_step(name):
    return 16 if name.startswith("L") else 1


class Sched:

    def __init__(self, counts):
        self.ops = {e: [] for e in ENGINES}
        self.n = counts
        self._last_selfwait = {}

    def op(self, eng, builder, inc=None):
        step = _sem_step(inc) if inc else 0
        if inc:
            self.n[inc] = self.n.get(inc, 0) + step
        self.ops[eng].append([builder, inc, step])
        return self.n.get(inc, 0)

    def wait(self, eng, sem, val):
        if val and val > 0:
            self.ops[eng].append([("W", sem, val), None, 0])

    def wait_all(self, eng, snapshot):
        for s_, v in snapshot.items():
            self.wait(eng, s_, v)

    def count(self, sem):
        return self.n.get(sem, 0)

    def close(self):
        for e in ENGINES:
            for item in reversed(self.ops[e]):
                if not isinstance(item[0], tuple):
                    if item[1] is None:
                        sem = ESEM[e]
                        item[1] = sem
                        item[2] = _sem_step(sem)
                        self.n[sem] = self.n.get(sem, 0) + _sem_step(sem)
                    break


def run_block(nc, sched, sems):
    sched.close()
    with nc.Block() as blk:
        for ename in ENGINES:
            def body(e, _ename=ename):
                for b, inc, step in sched.ops[_ename]:
                    if isinstance(b, tuple):
                        e.wait_ge(sems[b[1]], b[2])
                    else:
                        ins = b(e)
                        if inc is not None:
                            ins.then_inc(sems[inc], step)
            getattr(blk, ename)(body)


def snap(counts):
    return dict(counts)



def phase_quant(nc, sc, t, c):
    mats = [(t["wgT"], t["wgq_s"], 0, 128), (t["wuT"], t["wuq_s"], 1, 128),
            (t["wdT"], t["wdq_s"], 2, 512)]
    stg = c["q_stg"]
    qbf = c["q_bf"]
    macc = c["q_macc"]
    tmp = c["q_tmp"]
    s11 = c["q_s11"]
    onec = c["ones_col"]
    oner = c["ones_rowf"]
    p11 = c["q_p11"]
    pbc = c["q_pbc"]
    pbc2 = c["q_pbc2"]
    wsc, wmean = t["wsc"], t["wmean"]

    vload = {}
    gstore = {}
    for dram, scratch, m, nr in mats:
        isp = nr // 128
        for k in range(8):
            par = k % 2
            if par in vload:
                sc.wait("sync", "V", vload[par])
            lv = sc.op("sync", (lambda dram=dram, k=k, par=par, nr=nr, isp=isp:
                                lambda e: e.dma_start(
                                    out=stg[par][:].rearrange("p (i o) -> p i o", i=isp),
                                    in_=dram[k * nr:(k + 1) * nr, :]
                                    .rearrange("(i p) o -> p i o", p=128)))(),
                       inc="Lstg%d" % par)
            sc.wait("vector", "Lstg%d" % par, lv)
            if k == 0:
                vload[par] = sc.op("vector",
                                   (lambda par=par, m=m:
                                    lambda e: e.tensor_reduce(out=macc[:, m:m + 1],
                                                              in_=stg[par][:], axis=AX.X,
                                                              op=ALU.add,
                                                              apply_absolute_value=True))(),
                                   inc="V")
            else:
                rv = sc.op("vector", (lambda par=par, m=m:
                                      lambda e: e.tensor_reduce(out=tmp[:, m:m + 1],
                                                                in_=stg[par][:],
                                                                axis=AX.X, op=ALU.add,
                                                                apply_absolute_value=True))(),
                           inc="V")
                sc.wait("vector", "V", rv)
                vload[par] = sc.op("vector", (lambda m=m:
                                              lambda e: e.tensor_add(out=macc[:, m:m + 1],
                                                                     in0=macc[:, m:m + 1],
                                                                     in1=tmp[:, m:m + 1]))(),
                                   inc="V")
        sc.wait("tensor", "V", sc.count("V"))
        tt = sc.op("tensor", (lambda m=m: lambda e: e.matmul(p11[:], onec[:],
                                                             macc[:, m:m + 1],
                                                             start=True, stop=True))(), inc="T")
        sc.wait("vector", "T", tt)
        nel = float(8 * 128 * 4096)
        fv = sc.op("vector", (lambda m=m:
                              lambda e: e.tensor_scalar(out=s11[:, m:m + 1], in0=p11[:],
                                                        scalar1=1.0 / nel, scalar2=1e-5,
                                                        op0=ALU.mult, op1=ALU.max))(),
                   inc="V")
        sc.wait("vector", "V", fv)
        v = sc.op("vector", (lambda m=m:
                             lambda e: e.reciprocal(out=s11[:, 3:4], in_=s11[:, m:m + 1]))(),
                  inc="V")
        sc.wait("tensor", "V", v)
        sc.op("tensor", (lambda m=m: lambda e: e.matmul(pbc[:], oner[:], s11[:, m:m + 1],
                                                        start=True, stop=True))())
        tt = sc.op("tensor", lambda e: e.matmul(pbc2[:], oner[:], s11[:, 3:4],
                                                start=True, stop=True), inc="T")
        sc.wait("vector", "T", tt)
        sc.op("vector", (lambda m=m: lambda e: e.tensor_copy(out=wmean[:, m:m + 1],
                                                             in_=pbc[:]))())
        sc.op("vector", (lambda m=m: lambda e: e.tensor_copy(out=wsc[:, m:m + 1],
                                                             in_=pbc2[:]))(), inc="V")
        for k in range(8):
            par = k % 2
            sc.wait("sync", "V", vload[par])
            lv = sc.op("sync", (lambda dram=dram, k=k, par=par, nr=nr, isp=isp:
                                lambda e: e.dma_start(
                                    out=stg[par][:].rearrange("p (i o) -> p i o", i=isp),
                                    in_=dram[k * nr:(k + 1) * nr, :]
                                    .rearrange("(i p) o -> p i o", p=128)))(),
                       inc="Lstg%d" % par)
            sc.wait("vector", "Lstg%d" % par, lv)
            if par in gstore:
                sc.wait("vector", "Lqbf%d" % par, gstore[par])
            sc.op("vector", (lambda par=par, m=m:
                             lambda e: e.tensor_scalar(out=stg[par][:], in0=stg[par][:],
                                                       scalar1=wsc[:, m:m + 1],
                                                       scalar2=None, op0=ALU.mult))())
            sc.op("vector", (lambda par=par:
                             lambda e: e.tensor_scalar(out=stg[par][:], in0=stg[par][:],
                                                       scalar1=t["mgk"][:, 0:1],
                                                       scalar2=None, op0=ALU.add))())
            sc.op("vector", (lambda par=par:
                             lambda e: e.tensor_scalar(out=stg[par][:], in0=stg[par][:],
                                                       scalar1=t["mgk"][:, 1:2], scalar2=1.0,
                                                       op0=ALU.add, op1=ALU.min))())
            vload[par] = sc.op("vector", (lambda par=par:
                                          lambda e: e.tensor_scalar(out=qbf[par][:],
                                                                    in0=stg[par][:],
                                                                    scalar1=-1.0,
                                                                    scalar2=None,
                                                                    op0=ALU.max))(),
                               inc="V")
            sc.wait("gpsimd", "V", vload[par])
            gstore[par] = sc.op("gpsimd",
                                (lambda scratch=scratch, k=k, par=par, nr=nr, isp=isp:
                                 lambda e: e.dma_start(
                                     out=scratch[k * nr:(k + 1) * nr, :]
                                     .rearrange("(i p) o -> p i o", p=128),
                                     in_=qbf[par][:].rearrange("p (i o) -> p i o",
                                                               i=isp)))(), inc="Lqbf%d" % par)



def phase_pass1(nc, sc, t, c, a, s):
    KT, Vr, W = a["KT"], a["V"], a["W"]
    xb, xT = c["xb"], c["xT"]
    qstage = c["qstage"]
    ptx, psA, psV = c["ptx"], c["psA"], c["psV"]
    brows, ones5 = a["brows"], a["ones512"]
    NG = 16

    xb_free, evac_done, mm_done = {}, {}, {}
    qev_done, qst_done, vev_done = {}, {}, {}

    def emit_transposes(g):
        for ts in range(2):
            tc_ = 2 * g + ts
            r0 = (s * TC + tc_) * 128
            par = tc_ % 2
            if tc_ >= 2:
                sc.wait("gpsimd", "T", xb_free[tc_ - 2])
            lv = sc.op("gpsimd", (lambda r0=r0, par=par:
                                  lambda e: e.dma_start(out=xb[par][:],
                                                        in_=t["x_in"][r0:r0 + 128, :]))(),
                       inc="Lxb%d" % par)
            sc.wait("tensor", "Lxb%d" % par, lv)
            if tc_ >= 2:
                sc.wait("tensor", "V", evac_done[tc_ - 2])
            for dcs in range(DC):
                sc.op("tensor", (lambda par=par, dcs=dcs:
                                 lambda e: e.transpose(ptx[par][:, dcs * 128:(dcs + 1) * 128],
                                                       xb[par][:, dcs * 128:(dcs + 1) * 128],
                                                       a["ident"]))(),
                      inc="T" if dcs == DC - 1 else None)
            xb_free[tc_] = sc.count("T")
            sc.wait("vector", "T", sc.count("T"))
            if g >= 2:
                sc.wait("vector", "T", mm_done[g - 2])
            dst = xT[g % 2].rearrange("p (dc two u) -> p dc two u", dc=DC, two=2)
            src = ptx[par].ap().rearrange("p (dc u) -> p dc u", dc=DC)
            evac_done[tc_] = sc.op("vector", (lambda dst=dst, src=src, ts=ts:
                                              lambda e: e.tensor_copy(out=dst[:, :, ts, :],
                                                                      in_=src))(), inc="V")

    def emit_group_mm(g):
        sc.wait("tensor", "V", evac_done[2 * g + 1])
        for proj in (0, 1):
            for oc in range(DC):
                idx = g * 2 * DC + proj * DC + oc
                rot = idx % 2
                sc.wait("tensor", "V", qev_done.get(idx - 2, 0))
                for dcs in range(DC):
                    sc.op("tensor", (lambda rot=rot, proj=proj, oc=oc, dcs=dcs, g=g:
                                     lambda e: e.matmul(psA[rot].ap()[:, 0:256],
                                                        W[:, proj * 8192 + dcs * 1024 + oc * 128:
                                                          proj * 8192 + dcs * 1024 + oc * 128 + 128],
                                                        xT[g % 2][:, dcs * 256:(dcs + 1) * 256],
                                                        start=(dcs == 0), stop=False))())
                tt = sc.op("tensor", (lambda rot=rot, proj=proj, oc=oc:
                                      lambda e: e.matmul(psA[rot].ap()[:, 0:256],
                                                         brows[0:1, proj * 1024 + oc * 128:
                                                               proj * 1024 + oc * 128 + 128],
                                                         ones5[0:1, 0:256],
                                                         start=False, stop=True))(), inc="T")
                sc.wait("vector", "T", tt)
                if proj == 1:
                    qev_done[idx] = sc.op("vector",
                                          (lambda rot=rot, oc=oc, g=g:
                                           lambda e: e.tensor_copy(
                                               out=KT[:, oc * 4096 + g * 256:
                                                      oc * 4096 + g * 256 + 256],
                                               in_=psA[rot].ap()[:, 0:256]))(), inc="V")
                else:
                    if oc == 0 and g >= 1:
                        sc.wait("vector", "Lqst", qst_done[g - 1])
                    dstq = qstage.rearrange("p (two oc u) -> p two oc u", two=2, oc=DC)
                    qev_done[idx] = sc.op("vector",
                                          (lambda rot=rot, oc=oc, dstq=dstq:
                                           lambda e: e.tensor_copy(
                                               out=dstq[:, :, oc, :],
                                               in_=psA[rot].ap()[:, 0:256].rearrange(
                                                   "p (two u) -> p two u", two=2)))(),
                                          inc="V")
        sc.wait("gpsimd", "V", qev_done[g * 2 * DC + DC - 1])
        r0 = (s * TC + 2 * g) * 128
        qst_done[g] = sc.op("gpsimd", (lambda r0=r0:
                                       lambda e: e.dma_start(
                                           out=t["qt_s"][r0:r0 + 256, :]
                                           .rearrange("(i p) o -> p i o", p=128),
                                           in_=qstage.rearrange("p (i o) -> p i o",
                                                               i=2)))(), inc="Lqst")
        for ts in range(2):
            vidx = g * 2 + ts
            rot = vidx % 2
            tc_ = 2 * g + ts
            sc.wait("tensor", "V", vev_done.get(vidx - 2, 0))
            for j in range(2):
                for dcs in range(DC):
                    sc.op("tensor", (lambda rot=rot, ts=ts, j=j, dcs=dcs, g=g:
                                     lambda e: e.matmul(psV[rot].ap()[:, j * 512:(j + 1) * 512],
                                                        xT[g % 2][:, dcs * 256 + ts * 128:
                                                                   dcs * 256 + ts * 128 + 128],
                                                        W[:, 2 * 8192 + dcs * 1024 + j * 512:
                                                          2 * 8192 + dcs * 1024 + j * 512 + 512],
                                                        start=(dcs == 0), stop=False))())
                tt = sc.op("tensor", (lambda rot=rot, j=j:
                                      lambda e: e.matmul(psV[rot].ap()[:, j * 512:(j + 1) * 512],
                                                         ones5[0:1, 0:128],
                                                         brows[0:1, 2048 + j * 512:
                                                               2048 + (j + 1) * 512],
                                                         start=False, stop=True))(),
                           inc="T" if j == 1 else None)
            sc.wait("vector", "T", tt)
            vev_done[vidx] = sc.op("vector",
                                   (lambda rot=rot, tc_=tc_:
                                    lambda e: e.tensor_copy(
                                        out=Vr[:, tc_ * 1024:(tc_ + 1) * 1024],
                                        in_=psV[rot].ap()))(), inc="V")
        mm_done[g] = sc.count("T")

    for g in range(NG):
        emit_transposes(g)
        if g >= 1:
            emit_group_mm(g - 1)
    emit_group_mm(NG - 1)



def phase_qloop(nc, sc, t, c, a, s):
    KT, Vr = a["KT"], a["V"]
    qtile, xq = c["qtile"], c["xq"]
    Ssb, PTs = c["Ssb"], c["PTs"]
    Psb = c["Psb"]
    psS, psT, psO = c["psS"], c["psT"], c["psO"]
    st = c["stats"]
    hst, hqst = c["hstage"], c["hqstage"]
    hqb = c["hqb"]
    sqd, rbuf = c["sqdump"], c["rbuf"]
    mask = a["maskf"]

    score_done = {}
    epi_done = {}
    hstore, hqstore = {}, {}
    last_psS = {}
    last_psT = {}
    last_PTs = {}
    ex_prev = {}
    ssb_rd_T = {0: 0}
    hb_of = {}
    hqT_done = {}
    hqtr_t = {}

    def hqT_tensor(tqt):
        tpar = tqt % 2
        sc.wait("tensor", "V", hb_of[tqt])
        for b in range(2):
            rot = b % 2
            sc.wait("tensor", "V", last_psT.get(rot, 0))
            for u in range(4):
                dcs = 4 * b + u
                tt = sc.op("tensor", (lambda rot=rot, dcs=dcs, u=u, tpar=tpar:
                                      lambda e: e.transpose(
                                          psT[rot].ap()[:, u * 128:(u + 1) * 128],
                                          hqb[tpar][:, dcs * 128:(dcs + 1) * 128],
                                          a["ident"]))(),
                           inc="T" if u == 3 else None)
            hqtr_t[(tqt, b)] = tt
        hqT_done[tqt] = sc.count("T")

    def hqT_evac(tqt):
        tpar = tqt % 2
        r0t = (s * TC + tqt) * 128
        for b in range(2):
            rot = b % 2
            sc.wait("vector", "T", hqtr_t[(tqt, b)])
            if b == 0 and tqt >= 2:
                sc.wait("vector", "Lhq%d" % tpar, hqstore[tqt - 2])
            hev = sc.op("vector", (lambda rot=rot, b=b, tpar=tpar:
                                   lambda e: e.tensor_copy(
                                       out=hqst[tpar][:, b * 512:(b + 1) * 512],
                                       in_=psT[rot].ap()))(), inc="V")
            last_psT[rot] = hev
        sc.wait("gpsimd", "V", hev)
        hqstore[tqt] = sc.op("gpsimd", (lambda r0t=r0t, tpar=tpar:
                                        lambda e: e.dma_start(
                                            out=t["hq_s"][r0t:r0t + 128, :],
                                            in_=hqst[tpar][:]))(), inc="Lhq%d" % tpar)

    for qt in range(TC):
        gc = s * TC + qt
        r0 = gc * 128
        nkc = qt + 1
        ngrp = (nkc + 3) // 4
        W_ = nkc * 128
        par = qt % 2
        if qt >= 2:
            sc.wait("sync", "T", score_done[qt - 2])
            sc.wait("sync", "V", epi_done.get(qt - 2, 0))
        qld = sc.op("sync", (lambda r0=r0, par=par:
                             lambda e: e.dma_start(out=qtile[par][:],
                                                   in_=t["qt_s"][r0:r0 + 128, :]))(), inc="Lqt%d" % par)
        xld = sc.op("sync", (lambda r0=r0, par=par:
                             lambda e: e.dma_start(out=xq[par][:],
                                                   in_=t["x_in"][r0:r0 + 128, :]))(), inc="Lxq%d" % par)
        sc.wait("tensor", "Lqt%d" % par, qld)
        ev_grp = {}
        for j in range(ngrp):
            w = min(512, (nkc - 4 * j) * 128)
            rot = j % 4
            sc.wait("tensor", "SC", last_psS.get(rot, 0))
            for dcs in range(DC):
                tt = sc.op("tensor", (lambda rot=rot, j=j, w=w, dcs=dcs, par=par:
                                      lambda e: e.matmul(psS[rot].ap()[:, 0:w],
                                                         qtile[par][:, dcs * 128:(dcs + 1) * 128],
                                                         KT[:, dcs * 4096 + j * 512:
                                                            dcs * 4096 + j * 512 + w],
                                                         start=(dcs == 0),
                                                         stop=(dcs == DC - 1)))(),
                           inc="T" if dcs == DC - 1 else None)
            sc.wait("vector", "T", tt)
            if j == ngrp - 1:
                sc.op("vector", (lambda rot=rot, w=w:
                                 lambda e: e.tensor_add(out=psS[rot].ap()[:, w - 128:w],
                                                        in0=psS[rot].ap()[:, w - 128:w],
                                                        in1=mask))())
            rv = sc.op("vector", (lambda rot=rot, w=w:
                                  lambda e: e.tensor_reduce(out=st[:, 0:1],
                                                            in_=psS[rot].ap()[:, 0:w],
                                                            axis=AX.X, op=ALU.max))(),
                       inc="V")
            sc.wait("vector", "V", rv)
            if j == 0:
                mv = sc.op("vector", lambda e: e.tensor_copy(out=st[:, 1:2], in_=st[:, 0:1]),
                           inc="V")
            else:
                mv = sc.op("vector", lambda e: e.tensor_max(out=st[:, 1:2], in0=st[:, 1:2],
                                                            in1=st[:, 0:1]), inc="V")
            sc.wait("scalar", "V", mv)
            if j <= 1:
                sc.wait("scalar", "T", ssb_rd_T.get(qt - 1, 0))
            ev_grp[j] = sc.op("scalar", (lambda rot=rot, j=j, w=w:
                                         lambda e: e.activation(out=Ssb[:, j * 512:j * 512 + w],
                                                                in_=psS[rot].ap()[:, 0:w],
                                                                func=AF.Copy,
                                                                scale=1.0 / 32.0))(), inc="SC")
            last_psS[rot] = ev_grp[j]
        score_done[qt] = sc.count("T")
        if qt >= 1:
            hqT_tensor(qt - 1)
        sc.wait("vector", "SC", ex_prev.get(qt - 1, 0))
        sc.wait("vector", "V", mv)
        bv = sc.op("vector", lambda e: e.tensor_scalar(out=st[:, 2:3], in0=st[:, 1:2],
                                                       scalar1=-1.0 / 32.0, scalar2=None,
                                                       op0=ALU.mult), inc="V")
        sc.wait("scalar", "V", bv)
        ex = sc.op("scalar", (lambda W_=W_:
                              lambda e: e.activation(out=Psb[:, 0:W_], in_=Ssb[:, 0:W_],
                                                     func=AF.Exp, bias=st[:, 2:3], scale=1.0,
                                                     accum_out=st[:, 3:4]))(), inc="SC")
        ex_prev[qt] = ex
        if qt >= 1:
            hqT_evac(qt - 1)
        sc.wait("vector", "SC", ex)
        lv = sc.op("vector", lambda e: e.reciprocal(out=st[:, 4:5], in_=st[:, 3:4]), inc="V")
        sc.wait("vector", "V", lv)
        if t.get("dbg_p") is not None and s == 0 and qt == 0:
            sc.op("vector", lambda e: e.tensor_copy(out=c["dbgbuf"][:, 0:128],
                                                    in_=Ssb[:, 0:128]))
            sc.op("vector", lambda e: e.tensor_copy(out=c["dbgbuf"][:, 128:256],
                                                    in_=Psb[:, 0:128]))
            dv = sc.op("vector", lambda e: e.tensor_copy(out=c["dbgbuf"][:, 256:272],
                                                         in_=st[:, 0:16]), inc="V")
            sc.wait("gpsimd", "V", dv)
            sc.op("gpsimd", lambda e: e.dma_start(out=t["dbg_p"][0:128, 0:272],
                                                  in_=c["dbgbuf"][:, 0:272]), inc="LG")
            dbg_qt0 = True
        else:
            dbg_qt0 = False
        nb = (nkc + 3) // 4
        pt_ev = {}
        sc.wait("tensor", "SC", ex)
        sc.wait("tensor", "V", epi_done.get(qt - 1, 0))

        def emit_pt(b, qt=qt):
            rot = b % 2
            sc.wait("tensor", "V", last_psT.get(rot, 0))
            nch = min(4, nkc - 4 * b)
            for u in range(nch):
                cidx = 4 * b + u
                tt = sc.op("tensor", (lambda rot=rot, u=u, cidx=cidx:
                                      lambda e: e.transpose(psT[rot].ap()[:, u * 128:(u + 1) * 128],
                                                            Psb[:, cidx * 128:(cidx + 1) * 128],
                                                            a["ident"]))(),
                           inc="T" if u == nch - 1 else None)
            ssb_rd_T[qt] = tt
            sc.wait("vector", "T", tt)
            sc.wait("vector", "T", last_PTs.get(rot, 0))
            pt_ev[b] = sc.op("vector", (lambda rot=rot, nch=nch:
                                        lambda e: e.tensor_copy(
                                            out=PTs[:, rot * 512:rot * 512 + nch * 128],
                                            in_=psT[rot].ap()[:, 0:nch * 128]))(), inc="V")
            last_psT[rot] = pt_ev[b]

        def emit_pv(b, final):
            rot = b % 2
            sc.wait("tensor", "V", pt_ev[b])
            nch = min(4, nkc - 4 * b)
            tt = None
            for u in range(nch):
                cidx = 4 * b + u
                for j in range(2):
                    batch_last = u == nch - 1 and j == 1
                    tt = sc.op("tensor",
                               (lambda rot=rot, u=u, cidx=cidx, j=j, final=final:
                                lambda e: e.matmul(psO.ap()[:, j * 512:(j + 1) * 512],
                                                   PTs[:, rot * 512 + u * 128:
                                                       rot * 512 + (u + 1) * 128],
                                                   Vr[:, cidx * 1024 + j * 512:
                                                      cidx * 1024 + j * 512 + 512],
                                                   start=(cidx == 0),
                                                   stop=(final and u == nch - 1)))(),
                               inc="T" if batch_last else None)
            last_PTs[rot] = tt
            return tt

        for b in range(nb):
            emit_pt(b)
            if b >= 1:
                emit_pv(b - 1, final=False)
        pv_done = emit_pv(nb - 1, final=True)
        sc.wait("vector", "T", pv_done)
        sc.wait("vector", "Lxq%d" % par, xld)
        ep = sc.op("vector", (lambda par=par:
                              lambda e: e.scalar_tensor_tensor(out=rbuf, in0=psO.ap(),
                                                               scalar=st[:, 4:5], in1=xq[par][:],
                                                               op0=ALU.mult, op1=ALU.add,
                                                               accum_out=st[:, 5:6]))(), inc="V")
        epi_done[qt] = ep
        sc.wait("vector", "V", ep)
        if t.get("dbg_p") is not None and s == 0 and qt == 0:
            sc.op("vector", lambda e: e.tensor_copy(out=c["dbgbuf2"][:, 0:1024],
                                                    in_=rbuf))
        fv = sc.op("vector", lambda e: e.tensor_scalar(out=st[:, 6:7], in0=st[:, 5:6],
                                                       scalar1=1.0 / 1024.0, scalar2=None,
                                                       op0=ALU.mult), inc="V")
        sc.wait("vector", "V", fv)
        mv = sc.op("vector", lambda e: e.tensor_mul(out=st[:, 7:8], in0=st[:, 6:7],
                                                    in1=st[:, 6:7]), inc="V")
        sc.wait("scalar", "V", mv)
        sq = sc.op("scalar", lambda e: e.activation(out=sqd, in_=rbuf, func=AF.Square,
                                                    accum_out=st[:, 8:9]), inc="SC")
        sc.wait("vector", "SC", sq)
        fv = sc.op("vector", lambda e: e.tensor_scalar(out=st[:, 9:10], in0=st[:, 8:9],
                                                       scalar1=1.0 / 1024.0, scalar2=1e-5,
                                                       op0=ALU.mult, op1=ALU.add), inc="V")
        sc.wait("vector", "V", fv)
        vv = sc.op("vector", lambda e: e.tensor_sub(out=st[:, 10:11], in0=st[:, 9:10],
                                                    in1=st[:, 7:8]), inc="V")
        sc.wait("scalar", "V", vv)
        sc.op("scalar", lambda e: e.activation(out=st[:, 11:12], in_=st[:, 10:11],
                                               func=AF.Ln, scale=1.0))
        rs = sc.op("scalar", lambda e: e.activation(out=st[:, 12:13], in_=st[:, 11:12],
                                                    func=AF.Exp, scale=-0.5), inc="SC")
        sc.wait("vector", "SC", rs)
        nm = sc.op("vector", lambda e: e.scalar_tensor_tensor(out=st[:, 13:14], in0=st[:, 6:7],
                                                              scalar=-1.0, in1=st[:, 12:13],
                                                              op0=ALU.mult, op1=ALU.mult),
                   inc="V")
        sc.wait("scalar", "V", nm)
        if t.get("dbg_p") is not None and s == 0 and qt == 0:
            dv2 = sc.op("vector", lambda e: e.tensor_copy(out=c["dbgbuf2"][:, 1024:1040],
                                                          in_=st[:, 0:16]), inc="V")
            sc.wait("gpsimd", "V", dv2)
            sc.op("gpsimd", lambda e: e.dma_start(out=t["dbg_s"][0:128, 0:1040].cast(F32)
                                                  if False else t["dbg_s2"][0:128, :],
                                                  in_=c["dbgbuf2"][:, 0:1040]), inc="LG")
        if qt >= 2:
            sc.wait("scalar", "Lh%d" % par, hstore[qt - 2])
        hv = sc.op("scalar", (lambda par=par:
                              lambda e: e.activation(out=hst[par][:], in_=rbuf, func=AF.Identity,
                                                     scale=st[:, 12:13], bias=st[:, 13:14]))(),
                   inc="SC")
        sc.wait("gpsimd", "SC", hv)
        hstore[qt] = sc.op("gpsimd", (lambda r0=r0, par=par:
                                      lambda e: e.dma_start(out=t["h_s"][r0:r0 + 128, :],
                                                            in_=hst[par][:]))(), inc="Lh%d" % par)
        sc.wait("vector", "SC", hv)
        rv = sc.op("vector", (lambda par=par:
                              lambda e: e.tensor_reduce(out=st[:, 14:15], in_=hst[par][:],
                                                        axis=AX.X, op=ALU.max,
                                                        apply_absolute_value=True))(), inc="V")
        sc.wait("vector", "V", rv)
        fv = sc.op("vector", lambda e: e.tensor_scalar(out=st[:, 14:15], in0=st[:, 14:15],
                                                       scalar1=1e-5, scalar2=None,
                                                       op0=ALU.max), inc="V")
        sc.wait("vector", "V", fv)
        sc.op("vector", (lambda gc=gc:
                         lambda e: e.tensor_scalar(out=t["rs_h"][:, gc:gc + 1],
                                                   in0=st[:, 14:15], scalar1=1.0 / 127.0,
                                                   scalar2=None, op0=ALU.mult))())
        rv = sc.op("vector", lambda e: e.reciprocal(out=st[:, 15:16], in_=st[:, 14:15]),
                   inc="V")
        sc.wait("vector", "V", rv)
        fv = sc.op("vector", lambda e: e.tensor_scalar(out=st[:, 15:16], in0=st[:, 15:16],
                                                       scalar1=127.0, scalar2=None,
                                                       op0=ALU.mult), inc="V")
        sc.wait("vector", "V", fv)
        sc.op("vector", (lambda par=par:
                         lambda e: e.tensor_scalar(out=rbuf, in0=hst[par][:],
                                                   scalar1=st[:, 15:16], scalar2=None,
                                                   op0=ALU.mult))())
        sc.op("vector", lambda e: e.tensor_scalar(out=rbuf, in0=rbuf,
                                                  scalar1=t["mgk"][:, 0:1], scalar2=None,
                                                  op0=ALU.add))
        sc.op("vector", lambda e: e.tensor_scalar(out=rbuf, in0=rbuf,
                                                  scalar1=t["mgk"][:, 1:2],
                                                  scalar2=127.0, op0=ALU.add, op1=ALU.min))
        sc.wait("vector", "T", hqT_done.get(qt - 2, 0))
        hb = sc.op("vector", (lambda par=par:
                              lambda e: e.tensor_scalar(out=hqb[par], in0=rbuf,
                                                        scalar1=-128.0,
                                                        scalar2=None, op0=ALU.max))(),
                   inc="V")
        hb_of[qt] = hb
        if t.get("dbg_st") is not None:
            sc.op("gpsimd", (lambda r0=r0:
                             lambda e: e.dma_start(out=t["dbg_st"][r0:r0 + 128, :],
                                                   in_=st[:, 0:16]))(), inc="LG")
            if s == 0 and qt == 20:
                sc.op("gpsimd", lambda e: e.dma_start(out=t["dbg_p"], in_=Psb), inc="LG")
                sc.op("gpsimd", lambda e: e.dma_start(out=t["dbg_s"], in_=Ssb), inc="LG")
    hqT_tensor(TC - 1)
    hqT_evac(TC - 1)



def phase_b1(nc, sc, t, c):
    wgq, wuq = c["wgq"], c["wuq"]
    hqT, sg = c["hqT"], c["sg"]
    ybf, yqb = c["ybf"], c["yqstage"]
    yqbf = c["yqbf"]
    pg, pu, psT = c["pg"], c["pu"], c["psTb"]
    stb = c["statsb"]

    for dcs in range(DC):
        sc.op("sync", (lambda dcs=dcs:
                       lambda e: e.dma_start(out=wgq[:, dcs * 4096:(dcs + 1) * 4096],
                                             in_=t["wgq_s"][dcs * 128:(dcs + 1) * 128, :]))(),
              inc="LWB1")
        sc.op("sync", (lambda dcs=dcs:
                       lambda e: e.dma_start(out=wuq[:, dcs * 4096:(dcs + 1) * 4096],
                                             in_=t["wuq_s"][dcs * 128:(dcs + 1) * 128, :]))(),
              inc="LWB1")
    sc.wait("tensor", "LWB1", sc.count("LWB1"))

    b1mm = {}
    ymul = {}
    yb_done = {}
    yst = {}
    last_psT = {}

    def emit_windows(gc):
        r0 = gc * 128
        par = gc % 2
        if gc >= 2:
            sc.wait("sync", "T", b1mm[gc - 2])
        hld = sc.op("sync", (lambda r0=r0, par=par:
                             lambda e: e.dma_start(out=hqT[par][:],
                                                   in_=t["hq_s"][r0:r0 + 128, :]))(),
                    inc="Lhqt%d" % par)
        sc.wait("tensor", "Lhqt%d" % par, hld)
        sc.op("vector", (lambda gc=gc:
                         lambda e: e.tensor_mul(out=stb[:, 0:1], in0=t["rs_h"][:, gc:gc + 1],
                                                in1=t["wmean"][:, 0:1]))())
        cgv = sc.op("vector", (lambda gc=gc:
                               lambda e: e.tensor_mul(out=stb[:, 1:2],
                                                      in0=t["rs_h"][:, gc:gc + 1],
                                                      in1=t["wmean"][:, 1:2]))(), inc="V")
        sc.wait("vector", "V", cgv)
        for win in range(8):
            wi = gc * 8 + win
            rot = win % 3
            sc.wait("tensor", "V", ymul.get(wi - 3, 0))
            tt = None
            for wq_, ps_ in ((wgq, pg), (wuq, pu)):
                for dcs in range(DC):
                    last = dcs == DC - 1 and ps_ is pu
                    tt = sc.op("tensor",
                               (lambda wq_=wq_, ps_=ps_, rot=rot, win=win, dcs=dcs, par=par:
                                lambda e: e.matmul(ps_[rot].ap(),
                                                   hqT[par][:, dcs * 128:(dcs + 1) * 128],
                                                   wq_[:, dcs * 4096 + win * 512:
                                                       dcs * 4096 + win * 512 + 512],
                                                   start=(dcs == 0), stop=(dcs == DC - 1)))(),
                               inc="T" if last else None)
            sc.wait("scalar", "T", tt)
            if win == 0:
                sc.wait("scalar", "V", cgv)
            sl = sc.op("scalar", (lambda rot=rot:
                                  lambda e: e.activation(out=sg[rot], in_=pg[rot].ap(),
                                                         func=AF.Silu, scale=stb[:, 0:1]))(),
                       inc="SC")
            sc.wait("vector", "SC", sl)
            ymul[wi] = sc.op("vector",
                             (lambda rot=rot, win=win:
                              lambda e: e.scalar_tensor_tensor(
                                  out=ybf[:, win * 512:(win + 1) * 512],
                                  in0=pu[rot].ap(), scalar=stb[:, 1:2], in1=sg[rot],
                                  op0=ALU.mult, op1=ALU.mult))(), inc="V")
            wred[gc] = sc.op("vector", (lambda win=win:
                                        lambda e: e.tensor_reduce(
                                            out=stb[:, 8 + win:9 + win],
                                            in_=ybf[:, win * 512:(win + 1) * 512],
                                            axis=AX.X, op=ALU.max,
                                            apply_absolute_value=True))(), inc="V")
        b1mm[gc] = sc.count("T")

    def emit_quant(gc):
        sc.wait("vector", "V", wred[gc])
        rv = sc.op("vector", lambda e: e.tensor_reduce(out=stb[:, 2:3], in_=stb[:, 8:16],
                                                       axis=AX.X, op=ALU.max), inc="V")
        sc.wait("vector", "V", rv)
        fv = sc.op("vector", lambda e: e.tensor_scalar(out=stb[:, 2:3], in0=stb[:, 2:3],
                                                       scalar1=1e-5, scalar2=None,
                                                       op0=ALU.max), inc="V")
        sc.wait("vector", "V", fv)
        sc.op("vector", (lambda gc=gc:
                         lambda e: e.tensor_scalar(out=t["rs_y"][:, gc:gc + 1],
                                                   in0=stb[:, 2:3], scalar1=1.0 / 127.0,
                                                   scalar2=None, op0=ALU.mult))())
        rv = sc.op("vector", lambda e: e.reciprocal(out=stb[:, 3:4], in_=stb[:, 2:3]),
                   inc="V")
        sc.wait("vector", "V", rv)
        fv = sc.op("vector", lambda e: e.tensor_scalar(out=stb[:, 3:4], in0=stb[:, 3:4],
                                                       scalar1=127.0, scalar2=None,
                                                       op0=ALU.mult), inc="V")
        sc.wait("scalar", "V", fv)
        sc.wait("scalar", "V", ymul[gc * 8 + 7])
        sc.wait("scalar", "V", p2_done.get(gc - 1, 0))
        p1 = sc.op("scalar", lambda e: e.activation(out=c["yf32"], in_=ybf,
                                                    func=AF.Identity,
                                                    scale=stb[:, 3:4]), inc="SC")
        sc.wait("vector", "SC", p1)
        sc.op("vector", lambda e: e.tensor_scalar(out=c["yf32"], in0=c["yf32"],
                                                  scalar1=t["mgk"][:, 0:1], scalar2=None,
                                                  op0=ALU.add))
        if gc >= 2:
            sc.wait("vector", "T", tq_done.get(gc - 2, 0))
        yb_done[gc] = sc.op("vector", (lambda gc=gc:
                                       lambda e: e.tensor_scalar(out=yqbf[gc % 2],
                                                                 in0=c["yf32"],
                                                                 scalar1=t["mgk"][:, 1:2],
                                                                 scalar2=None,
                                                                 op0=ALU.add))(), inc="V")
        p2_done[gc] = yb_done[gc]

    tq_done = {}
    wred = {}
    p2_done = {}

    def emit_yq_T(gc):
        r0 = gc * 128
        sc.wait("tensor", "V", yb_done[gc])
        for b in range(8):
            rot = b % 2
            sc.wait("tensor", "V", last_psT.get(rot, 0))
            for u in range(4):
                icx = 4 * b + u
                tt = sc.op("tensor", (lambda rot=rot, icx=icx, u=u, gc=gc:
                                      lambda e: e.transpose(psT[rot].ap()[:, u * 128:(u + 1) * 128],
                                                            yqbf[gc % 2][:, icx * 128:(icx + 1) * 128],
                                                            t["identb"]))(),
                           inc="T" if u == 3 else None)
            if b == 7:
                tq_done[gc] = tt
            sc.wait("vector", "T", tt)
            if b == 0 and gc >= 1:
                sc.wait("vector", "Lyq", yst[gc - 1])
            pe = sc.op("vector", (lambda rot=rot, b=b:
                                  lambda e: e.tensor_copy(out=yqb[:, b * 512:(b + 1) * 512],
                                                          in_=psT[rot].ap()))(), inc="V")
            last_psT[rot] = pe
        sc.wait("gpsimd", "V", pe)
        yst[gc] = sc.op("gpsimd", (lambda r0=r0:
                                   lambda e: e.dma_start(out=t["yq_s"][r0:r0 + 128, :],
                                                         in_=yqb))(), inc="Lyq")

    import os as _os
    _gcl = int(_os.environ.get("B1GC", GC))
    for gc in range(_gcl):
        emit_windows(gc)
        if gc >= 1:
            emit_yq_T(gc - 1)
        emit_quant(gc)
    emit_yq_T(_gcl - 1)



def phase_b2(nc, sc, t, c):
    wdq = c["wdq"]
    yqT, hres, ost = c["yqT"], c["hres"], c["ostage"]
    r2, sqd2 = c["r2"], c["sqd2"]
    pd = c["pd"]
    stb = c["statsd"]

    for icx in range(IC):
        sc.op("sync", (lambda icx=icx:
                       lambda e: e.dma_start(out=wdq[:, icx * 1024:(icx + 1) * 1024],
                                             in_=t["wdq_s"][icx * 128:(icx + 1) * 128, :]))(),
              inc="LWB2")
    sc.wait("tensor", "LWB2", sc.count("LWB2"))

    dmm, ep2, odone, ostore = {}, {}, {}, {}
    for gc in range(GC):
        r0 = gc * 128
        par = gc % 3
        rot = gc % 3
        sb = (gc % 2) * 8
        if gc >= 3:
            sc.wait("sync", "T", dmm[gc - 3])
            sc.wait("sync", "V", ep2[gc - 3])
        yl = sc.op("sync", (lambda r0=r0, par=par:
                            lambda e: e.dma_start(out=yqT[par][:],
                                                  in_=t["yq_s"][r0:r0 + 128, :]))(), inc="Lyqt%d" % par)
        hl = sc.op("sync", (lambda r0=r0, par=par:
                            lambda e: e.dma_start(out=hres[par][:],
                                                  in_=t["h_s"][r0:r0 + 128, :]))(), inc="Lhr%d" % par)
        sc.wait("tensor", "Lyqt%d" % par, yl)
        sc.wait("tensor", "V", ep2.get(gc - 3, 0))
        tt = None
        for j in range(2):
            for icx in range(IC):
                last = icx == IC - 1 and j == 1
                tt = sc.op("tensor", (lambda rot=rot, j=j, icx=icx, par=par:
                                      lambda e: e.matmul(pd[rot].ap()[:, j * 512:(j + 1) * 512],
                                                         yqT[par][:, icx * 128:(icx + 1) * 128],
                                                         wdq[:, icx * 1024 + j * 512:
                                                             icx * 1024 + j * 512 + 512],
                                                         start=(icx == 0),
                                                         stop=(icx == IC - 1)))(),
                           inc="T" if last else None)
        dmm[gc] = tt
        cdv = sc.op("vector", (lambda gc=gc:
                               lambda e: e.tensor_mul(out=stb[:, sb + 0:sb + 1],
                                                      in0=t["rs_y"][:, gc:gc + 1],
                                                      in1=t["wmean"][:, 2:3]))(), inc="V")
        sc.wait("vector", "T", dmm[gc])
        sc.wait("vector", "Lhr%d" % par, hl)
        sc.wait("vector", "V", cdv)
        sc.wait("vector", "SC", odone.get(gc - 2, 0))
        r2c = r2[gc % 2]
        ep = sc.op("vector", (lambda rot=rot, par=par, r2c=r2c:
                              lambda e: e.scalar_tensor_tensor(out=r2c, in0=pd[rot].ap(),
                                                               scalar=stb[:, sb + 0:sb + 1],
                                                               in1=hres[par][:],
                                                               op0=ALU.mult, op1=ALU.add,
                                                               accum_out=stb[:, sb + 1:sb + 2]))(),
                   inc="V")
        ep2[gc] = ep
        sc.wait("vector", "V", ep)
        fv = sc.op("vector", lambda e: e.tensor_scalar(out=stb[:, sb + 2:sb + 3], in0=stb[:, sb + 1:sb + 2],
                                                       scalar1=1.0 / 1024.0, scalar2=None,
                                                       op0=ALU.mult), inc="V")
        sc.wait("vector", "V", fv)
        mv = sc.op("vector", lambda e: e.tensor_mul(out=stb[:, sb + 3:sb + 4], in0=stb[:, sb + 2:sb + 3],
                                                    in1=stb[:, sb + 2:sb + 3]), inc="V")
        sc.wait("scalar", "V", mv)
        sq = sc.op("scalar", (lambda r2c=r2c:
                              lambda e: e.activation(out=sqd2, in_=r2c, func=AF.Square,
                                                     accum_out=stb[:, sb + 4:sb + 5]))(), inc="SC")
        sc.wait("vector", "SC", sq)
        fv = sc.op("vector", lambda e: e.tensor_scalar(out=stb[:, sb + 5:sb + 6], in0=stb[:, sb + 4:sb + 5],
                                                       scalar1=1.0 / 1024.0, scalar2=1e-5,
                                                       op0=ALU.mult, op1=ALU.add), inc="V")
        sc.wait("vector", "V", fv)
        vv = sc.op("vector", lambda e: e.tensor_sub(out=stb[:, sb + 5:sb + 6], in0=stb[:, sb + 5:sb + 6],
                                                    in1=stb[:, sb + 3:sb + 4]), inc="V")
        sc.wait("scalar", "V", vv)
        sc.op("scalar", lambda e: e.activation(out=stb[:, sb + 6:sb + 7], in_=stb[:, sb + 5:sb + 6],
                                               func=AF.Ln, scale=1.0))
        rs = sc.op("scalar", lambda e: e.activation(out=stb[:, sb + 7:sb + 8], in_=stb[:, sb + 6:sb + 7],
                                                    func=AF.Exp, scale=-0.5), inc="SC")
        sc.wait("vector", "SC", rs)
        nm = sc.op("vector", lambda e: e.scalar_tensor_tensor(out=stb[:, sb + 2:sb + 3], in0=stb[:, sb + 2:sb + 3],
                                                              scalar=-1.0, in1=stb[:, sb + 7:sb + 8],
                                                              op0=ALU.mult, op1=ALU.mult),
                   inc="V")
        sc.wait("scalar", "V", nm)
        if gc >= 3:
            sc.wait("scalar", "Lout%d" % par, ostore[gc - 3])
        ov = sc.op("scalar", (lambda par=par, r2c=r2c:
                              lambda e: e.activation(out=ost[par][:], in_=r2c,
                                                     func=AF.Identity,
                                                     scale=stb[:, sb + 7:sb + 8], bias=stb[:, sb + 2:sb + 3]))(),
                   inc="SC")
        odone[gc] = ov
        sc.wait("gpsimd", "SC", ov)
        ostore[gc] = sc.op("gpsimd", (lambda r0=r0, par=par:
                                      lambda e: e.dma_start(out=t["out"][r0:r0 + 128, :],
                                                            in_=ost[par][:]))(), inc="Lout%d" % par)
    sc.wait("sync", "Lout0", sc.count("Lout0"))
    sc.wait("sync", "Lout1", sc.count("Lout1"))
    sc.wait("sync", "Lout2", sc.count("Lout2"))



def build_nc(dbg=False, phases=("Q", "A", "B1", "B2"), nseq=SEQ):
    nc = bass.Bass("TRN2", target_bir_lowering=False, detect_race_conditions=False)
    skind = "ExternalOutput" if dbg else "Internal"
    t = {}
    t["x_in"] = nc.dram_tensor("x_in", [SEQ * S, D], F32, kind="ExternalInput").ap()
    for nm, shp in (("wqT", [D, D]), ("wkT", [D, D]), ("wvT", [D, D]),
                    ("wgT", [D, INNER]), ("wuT", [D, INNER]), ("wdT", [INNER, D])):
        t[nm] = nc.dram_tensor(nm, shp, F32, kind="ExternalInput").ap()
    t["brows"] = nc.dram_tensor("brows", [1, 3 * D], F32, kind="ExternalInput").ap()
    t["ident_in"] = nc.dram_tensor("ident_in", [128, 128], BF16, kind="ExternalInput").ap()
    t["onesf_in"] = nc.dram_tensor("onesf_in", [128, 128], F32, kind="ExternalInput").ap()
    t["onesb_in"] = nc.dram_tensor("onesb_in", [1, 512], BF16, kind="ExternalInput").ap()
    t["mask_in"] = nc.dram_tensor("mask_in", [128, 128], F32, kind="ExternalInput").ap()
    t["out"] = nc.dram_tensor("out", [SEQ * S, D], F32, kind="ExternalOutput").ap()
    t["qt_s"] = nc.dram_tensor("qt_s", [SEQ * S, D], BF16, kind=skind).ap()
    if dbg:
        t["dbg_st"] = nc.dram_tensor("dbg_st", [SEQ * S, 16], F32, kind=skind).ap()
        t["dbg_p"] = nc.dram_tensor("dbg_p", [128, 4096], BF16, kind=skind).ap()
        t["dbg_s"] = nc.dram_tensor("dbg_s", [128, 4096], BF16, kind=skind).ap()
        t["dbg_s2"] = nc.dram_tensor("dbg_s2", [128, 1040], F32, kind=skind).ap()
    t["h_s"] = nc.dram_tensor("h_s", [SEQ * S, D], F32, kind=skind).ap()
    t["hq_s"] = nc.dram_tensor("hq_s", [SEQ * S, D], BF16, kind=skind).ap()
    t["yq_s"] = nc.dram_tensor("yq_s", [SEQ * S, INNER], BF16, kind=skind).ap()
    t["wgq_s"] = nc.dram_tensor("wgq_s", [D, INNER], BF16, kind=skind).ap()
    t["wuq_s"] = nc.dram_tensor("wuq_s", [D, INNER], BF16, kind=skind).ap()
    t["wdq_s"] = nc.dram_tensor("wdq_s", [INNER, D], BF16, kind=skind).ap()

    counts = {}
    with ExitStack() as top:
        E = top.enter_context
        sem_names = ["LX", "LG", "SC", "V", "T", "Lconst", "LW", "Lqst", "Lyq",
                     "LWB1", "LWB2"]
        for pfx in ("Lstg", "Lqbf", "Lxb", "Lqt", "Lxq", "Lh", "Lhq", "Lhqt",
                    "Lyqt", "Lhr", "Lout"):
            sem_names += [pfx + "0", pfx + "1"]
        sem_names += ["Lyqt2", "Lhr2", "Lout2"]
        sems = {s_: E(nc.semaphore(s_)) for s_ in sem_names}
        ident = E(nc.sbuf_tensor("ident", [128, 128], BF16))
        maskf = E(nc.sbuf_tensor("maskf", [128, 128], F32))
        onesf = E(nc.sbuf_tensor("onesf", [128, 128], F32))
        onesb = E(nc.sbuf_tensor("onesb", [1, 512], BF16))
        t["wsc"] = E(nc.sbuf_tensor("wsc", [128, 4], F32)).ap()
        t["wmean"] = E(nc.sbuf_tensor("wmean", [128, 4], F32)).ap()
        t["rs_h"] = E(nc.sbuf_tensor("rs_h", [128, GC], F32)).ap()
        t["rs_y"] = E(nc.sbuf_tensor("rs_y", [128, GC], F32)).ap()
        t["mgk"] = E(nc.sbuf_tensor("mgk", [128, 4], F32)).ap()
        t["identb"] = ident.ap()

        if "Q" not in phases:
            with ExitStack() as ph:
                P = ph.enter_context
                sc = Sched(counts)
                sc.op("sync", lambda e: e.dma_start(out=ident.ap(), in_=t["ident_in"]),
                      inc="Lconst")
                sc.op("sync", lambda e: e.dma_start(out=maskf.ap(), in_=t["mask_in"]),
                      inc="Lconst")
                sc.op("sync", lambda e: e.dma_start(out=onesf.ap(), in_=t["onesf_in"]),
                      inc="Lconst")
                sc.op("sync", lambda e: e.dma_start(out=onesb.ap(), in_=t["onesb_in"]),
                      inc="Lconst")
                run_block(nc, sc, sems)
                prev = snap(counts)
        if True and "Q" in phases:
         with ExitStack() as ph:
            P = ph.enter_context
            c = {}
            c["q_stg"] = [P(nc.sbuf_tensor(f"q_stg{i}", [128, 4096], F32)).ap() for i in range(2)]
            c["q_bf"] = [P(nc.sbuf_tensor(f"q_bf{i}", [128, 4096], BF16)).ap() for i in range(2)]
            c["q_macc"] = P(nc.sbuf_tensor("q_macc", [128, 4], F32)).ap()
            c["q_tmp"] = P(nc.sbuf_tensor("q_tmp", [128, 4], F32)).ap()
            c["q_s11"] = P(nc.sbuf_tensor("q_s11", [1, 4], F32)).ap()
            c["ones_col"] = onesf.ap()[:, 0:1]
            c["ones_rowf"] = onesf.ap()[0:1, 0:128]
            c["q_p11"] = P(nc.psum_tensor("q_p11", [1, 1], F32))
            c["q_pbc"] = P(nc.psum_tensor("q_pbc", [128, 1], F32))
            c["q_pbc2"] = P(nc.psum_tensor("q_pbc2", [128, 1], F32))
            sc = Sched(counts)
            sc.op("sync", lambda e: e.dma_start(out=ident.ap(), in_=t["ident_in"]), inc="Lconst")
            sc.op("sync", lambda e: e.dma_start(out=maskf.ap(), in_=t["mask_in"]), inc="Lconst")
            sc.op("sync", lambda e: e.dma_start(out=onesf.ap(), in_=t["onesf_in"]), inc="Lconst")
            lcv = sc.op("sync", lambda e: e.dma_start(out=onesb.ap(), in_=t["onesb_in"]),
                        inc="Lconst")
            sc.wait("vector", "Lconst", lcv)
            sc.wait("tensor", "Lconst", lcv)
            mg = t["mgk"]
            fv = sc.op("vector", lambda e: e.tensor_scalar(out=mg[:, 2:3],
                                                           in0=onesf.ap()[:, 0:1],
                                                           scalar1=1e-30, scalar2=None,
                                                           op0=ALU.mult), inc="V")
            sc.wait("vector", "V", fv)
            fv = sc.op("vector", lambda e: e.tensor_scalar(out=mg[:, 2:3], in0=mg[:, 2:3],
                                                           scalar1=1e-30, scalar2=None,
                                                           op0=ALU.mult), inc="V")
            sc.wait("vector", "V", fv)
            fv = sc.op("vector", lambda e: e.tensor_scalar(out=mg[:, 3:4], in0=mg[:, 2:3],
                                                           scalar1=0.0, scalar2=None,
                                                           op0=ALU.is_gt), inc="V")
            sc.wait("vector", "V", fv)
            fv = sc.op("vector", lambda e: e.tensor_scalar(out=mg[:, 0:1], in0=mg[:, 3:4],
                                                           scalar1=M64 - M32, scalar2=M32,
                                                           op0=ALU.mult, op1=ALU.add),
                       inc="V")
            sc.wait("vector", "V", fv)
            fv = sc.op("vector", lambda e: e.tensor_scalar(out=mg[:, 1:2], in0=mg[:, 0:1],
                                                           scalar1=-1.0, scalar2=None,
                                                           op0=ALU.mult), inc="V")
            sc.wait("vector", "V", fv)
            phase_quant(nc, sc, t, c)
            run_block(nc, sc, sems)
            prev = snap(counts)

        with ExitStack() as pha:
            P = pha.enter_context
            a = {}
            a["KT"] = P(nc.sbuf_tensor("KT", [128, DC * 4096], BF16)).ap()
            a["V"] = P(nc.sbuf_tensor("Vres", [128, TC * 1024], BF16)).ap()
            a["ident"] = ident.ap()
            a["maskf"] = maskf.ap()
            a["ones512"] = onesb.ap()
            for s_ in range(nseq):
                with ExitStack() as ph1:
                    P1 = ph1.enter_context
                    c = {}
                    a["W"] = P1(nc.sbuf_tensor(f"Wqkv{s_}", [128, 3 * 8192], BF16)).ap()
                    a["brows"] = P1(nc.sbuf_tensor(f"browsb{s_}", [1, 3 * D], BF16)).ap()
                    c["xb"] = [P1(nc.sbuf_tensor(f"xb{i}_{s_}", [128, D], BF16))
                               for i in range(2)]
                    c["xT"] = [P1(nc.sbuf_tensor(f"xT{i}_{s_}", [128, 2048], BF16)).ap()
                               for i in range(2)]
                    c["qstage"] = P1(nc.sbuf_tensor(f"qstage{s_}", [128, 2048], BF16)).ap()
                    c["ptx"] = [P1(nc.psum_tensor(f"ptx{i}_{s_}", [128, 1024], BF16))
                                for i in range(2)]
                    c["psA"] = [P1(nc.psum_tensor(f"psA{i}_{s_}", [128, 512], F32))
                                for i in range(2)]
                    c["psV"] = [P1(nc.psum_tensor(f"psV{i}_{s_}", [128, 1024], F32))
                                for i in range(2)]
                    sc = Sched(counts)
                    for e_ in ENGINES:
                        sc.wait_all(e_, prev)
                    for proj, wnm in ((0, "wqT"), (1, "wkT"), (2, "wvT")):
                        for dcs in range(DC):
                            sc.op("gpsimd",
                                  (lambda proj=proj, wnm=wnm, dcs=dcs:
                                   lambda e: e.dma_start(
                                       out=a["W"][:, proj * 8192 + dcs * 1024:
                                                  proj * 8192 + (dcs + 1) * 1024],
                                       in_=t[wnm][dcs * 128:(dcs + 1) * 128, :]))(),
                                  inc="LW")
                    sc.op("gpsimd", lambda e: e.dma_start(out=a["brows"], in_=t["brows"]),
                          inc="LW")
                    sc.wait("tensor", "LW", sc.count("LW"))
                    phase_pass1(nc, sc, t, c, a, s_)
                    run_block(nc, sc, sems)
                    prev = snap(counts)
                with ExitStack() as ph2:
                    P2 = ph2.enter_context
                    c = {}
                    c["qtile"] = [P2(nc.sbuf_tensor(f"qtile{i}_{s_}", [128, D], BF16))
                                  for i in range(2)]
                    c["xq"] = [P2(nc.sbuf_tensor(f"xq{i}_{s_}", [128, D], F32))
                               for i in range(2)]
                    c["Ssb"] = P2(nc.sbuf_tensor(f"Ssb{s_}", [128, 4096], BF16)).ap()
                    c["Psb"] = P2(nc.sbuf_tensor(f"Psb{s_}", [128, 4096], BF16)).ap()
                    c["PTs"] = P2(nc.sbuf_tensor(f"PTs{s_}", [128, 1024], BF16)).ap()
                    c["stats"] = P2(nc.sbuf_tensor(f"stats{s_}", [128, 16], F32)).ap()
                    c["hstage"] = [P2(nc.sbuf_tensor(f"hst{i}_{s_}", [128, D], F32))
                                   for i in range(2)]
                    c["hqstage"] = [P2(nc.sbuf_tensor(f"hqst{i}_{s_}", [128, D], BF16))
                                    for i in range(2)]
                    c["hqb"] = [P2(nc.sbuf_tensor(f"hqb{i}_{s_}", [128, D], BF16)).ap()
                                for i in range(2)]
                    c["rbuf"] = P2(nc.sbuf_tensor(f"rbuf{s_}", [128, D], F32)).ap()
                    c["dbgbuf"] = P2(nc.sbuf_tensor(f"dbgbuf{s_}", [128, 272], BF16)).ap()
                    c["dbgbuf2"] = P2(nc.sbuf_tensor(f"dbgbuf2{s_}", [128, 1040], F32)).ap()
                    c["sqdump"] = c["Ssb"][:, 0:1024]
                    c["psS"] = [P2(nc.psum_tensor(f"psS{i}_{s_}", [128, 512], F32))
                                for i in range(4)]
                    c["psT"] = [P2(nc.psum_tensor(f"psTq{i}_{s_}", [128, 512], BF16))
                                for i in range(2)]
                    c["psO"] = P2(nc.psum_tensor(f"psO{s_}", [128, 1024], F32))
                    sc = Sched(counts)
                    for e_ in ENGINES:
                        sc.wait_all(e_, prev)
                    phase_qloop(nc, sc, t, c, a, s_)
                    run_block(nc, sc, sems)
                    prev = snap(counts)

        if "B1" in phases:
         with ExitStack() as ph:
            P = ph.enter_context
            c = {}
            c["wgq"] = P(nc.sbuf_tensor("wgq", [128, DC * 4096], BF16)).ap()
            c["wuq"] = P(nc.sbuf_tensor("wuq", [128, DC * 4096], BF16)).ap()
            c["hqT"] = [P(nc.sbuf_tensor(f"hqT{i}", [128, D], BF16)) for i in range(2)]
            c["sg"] = [P(nc.sbuf_tensor(f"sg{i}", [128, 512], BF16)).ap()
                       for i in range(3)]
            c["ybf"] = P(nc.sbuf_tensor("ybf", [128, INNER], BF16)).ap()
            c["yqbf"] = [P(nc.sbuf_tensor(f"yqbf{i}", [128, INNER], BF16)).ap()
                         for i in range(2)]
            c["yqstage"] = P(nc.sbuf_tensor("yqstage", [128, INNER], BF16)).ap()
            c["yf32"] = P(nc.sbuf_tensor("yf32", [128, INNER], F32)).ap()
            c["statsb"] = P(nc.sbuf_tensor("statsb", [128, 16], F32)).ap()
            c["pg"] = [P(nc.psum_tensor(f"pg{i}", [128, 512], F32)) for i in range(3)]
            c["pu"] = [P(nc.psum_tensor(f"pu{i}", [128, 512], F32)) for i in range(3)]
            c["psTb"] = [P(nc.psum_tensor(f"psTb{i}", [128, 512], BF16)) for i in range(2)]
            sc = Sched(counts)
            for e_ in ENGINES:
                sc.wait_all(e_, prev)
            phase_b1(nc, sc, t, c)
            run_block(nc, sc, sems)
            prev = snap(counts)

        if "B2" in phases:
         with ExitStack() as ph:
            P = ph.enter_context
            c = {}
            c["wdq"] = P(nc.sbuf_tensor("wdq", [128, IC * 1024], BF16)).ap()
            c["yqT"] = [P(nc.sbuf_tensor(f"yqT{i}", [128, INNER], BF16)) for i in range(3)]
            c["hres"] = [P(nc.sbuf_tensor(f"hres{i}", [128, D], F32)) for i in range(3)]
            c["ostage"] = [P(nc.sbuf_tensor(f"ost{i}", [128, D], F32)) for i in range(3)]
            c["r2"] = [P(nc.sbuf_tensor(f"r2_{i}", [128, D], F32)).ap() for i in range(2)]
            c["sqd2"] = P(nc.sbuf_tensor("sqd2", [128, D], F32)).ap()
            c["statsd"] = P(nc.sbuf_tensor("statsd", [128, 16], F32)).ap()
            c["pd"] = [P(nc.psum_tensor(f"pd{i}", [128, 1024], F32)) for i in range(3)]
            sc = Sched(counts)
            for e_ in ENGINES:
                sc.wait_all(e_, prev)
            phase_b2(nc, sc, t, c)
            run_block(nc, sc, sems)
    return nc


def make_consts():
    ident = np.eye(128, dtype=ml_dtypes.bfloat16)
    r = np.arange(128)
    mask = np.where(r[None, :] > r[:, None], np.float32(MASKV),
                    np.float32(0.0)).astype(np.float32)
    return ident, mask


def kernel(L0_attn_x, L0_attn_Wq, L0_attn_bq, L0_attn_Wk, L0_attn_bk, L0_attn_Wv,
           L0_attn_bv, L0_ffn_w_gate, L0_ffn_w_up, L0_ffn_w_down):
    x = np.ascontiguousarray(np.asarray(L0_attn_x, dtype=np.float32))
    wqT = np.ascontiguousarray(np.asarray(L0_attn_Wq, dtype=np.float32).T)
    wkT = np.ascontiguousarray(np.asarray(L0_attn_Wk, dtype=np.float32).T)
    wvT = np.ascontiguousarray(np.asarray(L0_attn_Wv, dtype=np.float32).T)
    wgT = np.ascontiguousarray(np.asarray(L0_ffn_w_gate, dtype=np.float32).T)
    wuT = np.ascontiguousarray(np.asarray(L0_ffn_w_up, dtype=np.float32).T)
    wdT = np.ascontiguousarray(np.asarray(L0_ffn_w_down, dtype=np.float32).T)
    brows = np.concatenate([np.asarray(L0_attn_bq, dtype=np.float32),
                            np.asarray(L0_attn_bk, dtype=np.float32),
                            np.asarray(L0_attn_bv, dtype=np.float32)]).reshape(1, 3 * D)
    ident, mask = make_consts()
    onesf = np.ones((128, 128), dtype=np.float32)
    onesb = np.ones((1, 512), dtype=ml_dtypes.bfloat16)

    nc = build_nc()
    in_maps = []
    for cid in range(NCORES):
        xs = np.ascontiguousarray(x[SEQ * cid:SEQ * (cid + 1)].reshape(SEQ * S, D))
        in_maps.append({"x_in": xs, "wqT": wqT, "wkT": wkT, "wvT": wvT,
                        "wgT": wgT, "wuT": wuT, "wdT": wdT, "brows": brows,
                        "ident_in": ident, "mask_in": mask,
                        "onesf_in": onesf, "onesb_in": onesb})
    res = run_bass_kernel_spmd(nc, in_maps, core_ids=list(range(NCORES)))
    out = np.concatenate([r["out"].reshape(SEQ, S, D) for r in res.results], axis=0)
    return out
